# version 06
# baseline (speedup 1.0000x reference)
import math
from contextlib import ExitStack
import numpy as np
import concourse.bass as bass
import concourse.mybir as mybir
from concourse.bass_utils import run_bass_kernel_spmd

F32 = mybir.dt.float32
BF16 = mybir.dt.bfloat16
I32 = mybir.dt.int32
AF = mybir.ActivationFunctionType
ALU = mybir.AluOpType
AX = mybir.AxisListType

D = 4096
S = 2048
TOWN = 1024
HEADS = 32
HD = 128
HID = 11008
NDT = D // 128
NHT = HID // 128
NTT = S // 128
NQT = TOWN // 128
SCALE = 1.0 / math.sqrt(HD)
EPS = 1e-5

ENGS = ("sync", "tensor", "vector", "scalar", "gpsimd")


class Sched:
    def __init__(self, nc):
        self.nc = nc
        self.st = {e: [] for e in ENGS}
        self.sems = {}
        self.cnt = {}
        self.seen = {e: {} for e in ENGS}

    def sem(self, name):
        self.sems[name] = self.nc.alloc_semaphore(name)
        self.cnt[name] = 0

    def wait(self, eng, name, v):
        if v <= 0 or v <= self.seen[eng].get(name, 0):
            return
        self.seen[eng][name] = v
        sem = self.sems[name]
        self.st[eng].append(lambda e, sem=sem, v=v: e.wait_ge(sem, v))

    def op(self, eng, meth, *args, inc=None, inck=1, **kw):
        sem = self.sems[inc] if inc else None

        def fn(e, meth=meth, args=args, kw=kw, sem=sem, inck=inck):
            ins = getattr(e, meth)(*args, **kw)
            if sem is not None:
                ins.then_inc(sem, inck)
        self.st[eng].append(fn)
        if inc:
            self.cnt[inc] += inck
            return self.cnt[inc]
        return None

    def raw(self, eng, fn):
        self.st[eng].append(fn)

    def barrier(self, engines=ENGS):
        marks = {n: self.cnt[n] for n in self.cnt}
        own = {"sync": "dsy", "tensor": "pe", "vector": "ve",
               "scalar": "ac", "gpsimd": "po"}
        for e in engines:
            for n, v in marks.items():
                if v > 0 and n != own[e]:
                    self.wait(e, n, v)


def ceil_div(a, b):
    return (a + b - 1) // b


def build(debug=0):
    nc = bass.Bass("TRN2", target_bir_lowering=False, debug=False)
    s = Sched(nc)
    for name in ("pe", "ve", "ac", "po", "dsy", "dsc", "dgp",
                 "dxa0", "dxa1", "dwq0", "dwq1", "dsq0", "dsq1", "dsv0", "dsv1",
                 "dh0", "dh1", "dow0", "dow1", "dxc0", "dxc1", "dso0", "dso1",
                 "dd10", "dd11", "dd30", "dd31", "dsh0", "dsh1", "dha",
                 "de0", "de1", "dyo0", "dyo1", "dya0", "dya1", "dout0", "dout1"):
        s.sem(name)

    x_b = nc.dram_tensor("x_b", [S, D], F32, kind="ExternalInput").ap()
    qkvw = nc.dram_tensor("qkvw", [D, 3 * D], F32, kind="ExternalInput").ap()
    qkvb = nc.dram_tensor("qkvb", [3 * D], F32, kind="ExternalInput").ap()
    ow = nc.dram_tensor("ow", [D, D], F32, kind="ExternalInput").ap()
    ob = nc.dram_tensor("ob", [D], F32, kind="ExternalInput").ap()
    w1 = nc.dram_tensor("w1", [352256, 64], I32, kind="ExternalInput").ap()
    s1 = nc.dram_tensor("s1", [704512], F32, kind="ExternalInput").ap()
    w2 = nc.dram_tensor("w2", [352256, 64], I32, kind="ExternalInput").ap()
    s2 = nc.dram_tensor("s2", [704512], F32, kind="ExternalInput").ap()
    w3 = nc.dram_tensor("w3", [352256, 64], I32, kind="ExternalInput").ap()
    s3 = nc.dram_tensor("s3", [704512], F32, kind="ExternalInput").ap()
    y_own = nc.dram_tensor("y_own", [TOWN, D], F32, kind="ExternalOutput").ap()

    def scratch(name, shape, dt, dbg_lvl):
        kind = "ExternalOutput" if (debug == dbg_lvl or debug == 7) else Ellipsis
        if kind is Ellipsis:
            return nc.dram_tensor(name, shape, dt).ap()
        return nc.dram_tensor(name, shape, dt, kind="ExternalOutput").ap()

    q_sp = scratch("d_q", [D, TOWN], BF16, 2)
    k_sp = scratch("d_k", [D, S], BF16, 2)
    v_sp = scratch("d_v", [S, D], BF16, 2)
    o1_sp = scratch("d_out1", [TOWN, D], F32, 4)
    ha_sp = scratch("d_hact", [HID, TOWN], BF16, 5)

    dbg = {}
    if debug in (1, 2):
        dbg["ln1T"] = nc.dram_tensor("d_ln1T", [128, NDT, S], BF16, kind="ExternalOutput").ap()

    ident_b = nc.alloc_sbuf_tensor("ident_b", [128, 128], BF16)
    ident_f = nc.alloc_sbuf_tensor("ident_f", [128, 128], F32)
    iota_f = nc.alloc_sbuf_tensor("iota_f", [128, 128], F32)
    eps_t = nc.alloc_sbuf_tensor("eps_t", [128, 1], F32)
    id1_f = nc.alloc_sbuf_tensor("id1_f", [1, 1], F32)

    psA = nc.alloc_psum_tensor("psA", [128, 4096], F32)

    def bankf(i, n=1):
        return psA[:, i * 512:(i + n) * 512]

    def bankb(i, n=1):
        return psA[:, i * 512:(i + n) * 512].bitcast(BF16)

    s.op("gpsimd", "iota", iota_f[:, :], [[1, 128]], channel_multiplier=-1,
         allow_small_or_imprecise_dtypes=True, inc="po")
    s.op("gpsimd", "memset", eps_t[:, :], EPS, inc="po")
    s.op("gpsimd", "memset", id1_f[:, :], 1.0, inc="po")
    v_iota = s.cnt["po"]
    s.wait("vector", "po", v_iota)
    s.op("vector", "tensor_scalar", ident_b[:, :], iota_f[:, :], 0.0, None, ALU.is_equal, inc="ve")
    s.op("vector", "tensor_scalar", ident_f[:, :], iota_f[:, :], 0.0, None, ALU.is_equal, inc="ve")
    v_ident = s.cnt["ve"]

    st8 = nc.alloc_sbuf_tensor("st8", [128, 8], F32)
    mr_t = nc.alloc_sbuf_tensor("mr_t", [128, 1], F32)
    esLn = ExitStack()
    ln1T = esLn.enter_context(nc.sbuf_tensor("ln1T", [128, NDT, S], BF16))
    esA = ExitStack()
    xf = [esA.enter_context(nc.sbuf_tensor(f"xf{i}", [128, D], F32)) for i in range(2)]
    xsq = esA.enter_context(nc.sbuf_tensor("xsq", [128, D], F32))
    lnb = [esA.enter_context(nc.sbuf_tensor(f"lnb{i}", [128, D], BF16)) for i in range(2)]

    def emit_ln_tile(bufs, src_hbm, row0, ti, lnT_dst, tcols, last_use):
        xf, xsq, lnb = bufs
        p = ti % 2
        s.wait("sync", "ve", last_use.get(("xf", p), 0))
        vdma = s.op("sync", "dma_start", inc=f"dxa{p}", inck=16,
                    out=xf[p][:, :], in_=src_hbm[row0:row0 + 128, :])
        s.wait("vector", f"dxa{p}", vdma)
        v1 = s.op("vector", "tensor_reduce", st8[:, 0:1], xf[p][:, :], AX.X, ALU.add,
                  negate=True, inc="ve")
        s.wait("scalar", f"dxa{p}", vdma)
        s.wait("scalar", "ve", last_use.get("xsq", 0))
        a1 = s.op("scalar", "activation", xsq[:, :], xf[p][:, :], AF.Square, inc="ac")
        s.wait("vector", "ac", a1)
        v2 = s.op("vector", "tensor_reduce", st8[:, 2:3], xsq[:, :], AX.X, ALU.add, inc="ve")
        last_use["xsq"] = v2
        s.wait("vector", "ve", v1)
        s.op("vector", "tensor_scalar", st8[:, 1:2], st8[:, 0:1], 1.0 / D, None, ALU.mult, inc="ve")
        s.wait("vector", "ve", s.cnt["ve"])
        s.op("vector", "tensor_tensor", st8[:, 4:5], st8[:, 1:2], st8[:, 1:2], ALU.mult, inc="ve")
        s.wait("vector", "ve", s.cnt["ve"])
        s.op("vector", "tensor_scalar", st8[:, 3:4], st8[:, 2:3], 1.0 / D, None, ALU.mult, inc="ve")
        s.wait("vector", "ve", s.cnt["ve"])
        v3 = s.op("vector", "tensor_tensor", st8[:, 5:6], st8[:, 3:4], st8[:, 4:5], ALU.subtract, inc="ve")
        s.wait("scalar", "ve", v3)
        a2 = s.op("scalar", "activation", st8[:, 6:7], st8[:, 5:6], AF.Ln, bias=eps_t[:, :], scale=1.0, inc="ac")
        s.wait("vector", "ac", a2)
        v4 = s.op("vector", "tensor_scalar", st8[:, 6:7], st8[:, 6:7], -0.5, None, ALU.mult, inc="ve")
        s.wait("scalar", "ve", v4)
        a3 = s.op("scalar", "activation", st8[:, 7:8], st8[:, 6:7], AF.Exp, inc="ac")
        s.wait("vector", "ac", a3)
        v5 = s.op("vector", "tensor_tensor", mr_t[:, :], st8[:, 1:2], st8[:, 7:8], ALU.mult, inc="ve")
        s.wait("vector", "ve", v5)
        s.wait("vector", "pe", last_use.get(("lnb", p), 0))
        v6 = s.op("vector", "tensor_scalar", lnb[p][:, :], xf[p][:, :],
                  st8[:, 7:8], mr_t[:, :], ALU.mult, ALU.add, inc="ve")
        last_use[("xf", p)] = v6
        s.wait("tensor", "ve", max(v6, v_ident))
        for g in range(4):
            s.wait("tensor", "ac", last_use.get(("Abank", g), 0))
            for j in range(8):
                dt = 8 * g + j
                s.op("tensor", "matmul", bankb(g)[:, j * 128:(j + 1) * 128],
                     lnb[p][:, dt * 128:(dt + 1) * 128], ident_b[:, :],
                     is_transpose=True, start=(j == 0), stop=(j == 7),
                     skip_group_check=True,
                     inc=("pe" if j == 7 else None))
            pev = s.cnt["pe"]
            if g == 3:
                last_use[("lnb", p)] = pev
            s.wait("scalar", "pe", pev)
            av = s.op("scalar", "copy", lnT_dst[:, 8 * g:8 * g + 8, tcols], bankb(g)[:, :], inc="ac")
            last_use[("Abank", g)] = av
        return last_use

    lu = {}
    for ti in range(NTT):
        emit_ln_tile((xf, xsq, lnb), x_b, ti * 128, ti, ln1T,
                     slice(ti * 128, (ti + 1) * 128), lu)

    if debug == 1:
        s.barrier()
        dv = s.op("sync", "dma_start", inc="dsy", inck=16, out=dbg["ln1T"], in_=ln1T[:, :, :])
        s.wait("sync", "dsy", dv)
        return nc, s, dbg

    s.barrier()
    esA.close()

    esQB = ExitStack()
    qb_all = esQB.enter_context(nc.sbuf_tensor("qb_all", [128, 96], F32))
    qb_s = esQB.enter_context(nc.sbuf_tensor("qb_s", [128, 32], F32))
    with nc.sbuf_tensor("brow", [1, 3 * D], F32) as brow:
        bv = s.op("sync", "dma_start", inc="dsy", inck=16, out=brow[:, :], in_=qkvb[None, :])
        s.wait("tensor", "dsy", bv)
        s.wait("tensor", "ve", v_ident)
        for ctb in range(96):
            s.op("tensor", "matmul", bankf(0)[:, ctb:ctb + 1], brow[0:1, ctb * 128:(ctb + 1) * 128],
                 id1_f[:, :], is_transpose=True, start=(ctb == 0), stop=(ctb == 95),
                 skip_group_check=True, inc=("pe" if ctb == 95 else None))
        pev = s.cnt["pe"]
        s.wait("scalar", "pe", pev)
        a_qb = s.op("scalar", "copy", qb_all[:, :], bankf(0)[:, 0:96], inc="ac")
        s.wait("vector", "ac", a_qb)
        v_qb = s.op("vector", "tensor_scalar", qb_s[:, :], qb_all[:, 0:32], SCALE, None, ALU.mult, inc="ve")
        s.barrier()

    esB = ExitStack()
    wstage = [esB.enter_context(nc.sbuf_tensor(f"wstage{i}", [128, NDT, 128], F32)) for i in range(2)]
    wbf = [esB.enter_context(nc.sbuf_tensor(f"wbf{i}", [128, NDT, 128], BF16)) for i in range(2)]
    qkst = [esB.enter_context(nc.sbuf_tensor(f"qkst{i}", [128, S], BF16)) for i in range(2)]
    vtst = [esB.enter_context(nc.sbuf_tensor(f"vtst{i}", [128, NTT, 128], BF16)) for i in range(2)]

    qkvw_r = qkvw.rearrange("(dt p) c -> p dt c", p=128)
    v_sp_r = v_sp.rearrange("(j p) c -> p j c", p=128)
    pair_free = {}
    lastvc = {0: 0, 1: 0}
    lastmm = {0: 0, 1: 0}
    laststage = {0: 0, 1: 0}
    last_vt = {0: 0, 1: 0}
    import os as _os
    _secs = _os.environ.get("B1_SECS", "012")
    for ct in range(96):
        p = ct % 2
        g = ct % 2
        b0 = 4 * g
        sec = ct // 32
        if str(sec) not in _secs:
            continue
        s.wait("sync", "ve", lastvc[p])
        dv = s.op("sync", "dma_start", inc=f"dwq{p}", inck=16,
                  out=wstage[p][:, :, :], in_=qkvw_r[:, :, ct * 128:(ct + 1) * 128])
        s.wait("vector", f"dwq{p}", dv)
        s.wait("vector", "pe", lastmm[p])
        vc = s.op("vector", "tensor_copy", wbf[p][:, :, :], wstage[p][:, :, :], inc="ve")
        lastvc[p] = vc
        s.wait("tensor", "ve", vc)
        ncols = TOWN if sec == 0 else S
        nch = ncols // 512
        nrounds = nch // 2
        ev = 0
        for rnd in range(nrounds):
            pr = b0 + 2 * (rnd % 2)
            key = (g, rnd % 2)
            s.wait("tensor", *pair_free.get(key, ("ac", 0)))
            for ch2 in range(2):
                ch = rnd * 2 + ch2
                for dt in range(NDT):
                    s.op("tensor", "matmul", bankf(pr + ch2),
                         wbf[p][:, dt, :], ln1T[:, dt, ch * 512:(ch + 1) * 512],
                         start=(dt == 0), stop=(dt == NDT - 1),
                         inc=("pe" if (dt == NDT - 1 and ch2 == 1) else None))
            mm_v = s.cnt["pe"]
            s.wait("scalar", "pe", mm_v)
            if rnd == 0:
                s.wait("scalar", f"dsq{p}", laststage[p])
            if sec == 0:
                ev = s.op("scalar", "activation", qkst[p][:, 0:TOWN], bankf(pr, 2), AF.Identity,
                          bias=qb_s[:, ct:ct + 1], scale=SCALE, inc="ac")
            else:
                ev = s.op("scalar", "activation", qkst[p][:, rnd * TOWN:(rnd + 1) * TOWN],
                          bankf(pr, 2), AF.Identity,
                          bias=qb_all[:, ct:ct + 1], scale=1.0, inc="ac")
            pair_free[key] = ("ac", ev)
        lastmm[p] = s.cnt["pe"]
        if sec == 0:
            s.wait("scalar", "ac", ev)
            sv = s.op("scalar", "dma_start", inc=f"dsq{p}", inck=16,
                      out=q_sp[ct * 128:(ct + 1) * 128, :], in_=qkst[p][:, 0:TOWN])
            laststage[p] = sv
        elif sec == 1:
            s.wait("scalar", "ac", ev)
            sv = s.op("scalar", "dma_start", inc=f"dsq{p}", inck=16,
                      out=k_sp[(ct - 32) * 128:(ct - 31) * 128, :], in_=qkst[p][:, :])
            laststage[p] = sv
        else:
            cc = ct - 64
            s.wait("tensor", "ac", ev)
            s.wait("tensor", *pair_free.get((g, 0), ("ac", 0)))
            for j in range(NTT):
                bb = b0 + j // 8
                s.op("tensor", "matmul", bankb(bb)[:, (j % 8) * 128:(j % 8 + 1) * 128],
                     qkst[p][:, j * 128:(j + 1) * 128], ident_b[:, :],
                     is_transpose=True, start=(j % 8 == 0), stop=(j % 8 == 7),
                     skip_group_check=True,
                     inc=("pe" if j % 8 == 7 else None))
        if sec == 2:
            tr_v = s.cnt["pe"]
            s.wait("scalar", "pe", tr_v)
            s.wait("scalar", f"dsv{p}", last_vt[p])
            ev2 = s.op("scalar", "copy", vtst[p][:, :, :], bankb(b0, 2)[:, :], inc="ac")
            pair_free[(g, 0)] = ("ac", ev2)
            s.wait("scalar", "ac", ev2)
            sv = s.op("scalar", "dma_start", inc=f"dsv{p}", inck=16,
                      out=v_sp_r[:, :, cc * 128:(cc + 1) * 128], in_=vtst[p][:, :, :])
            last_vt[p] = sv
        if _os.environ.get("B1_SER"):
            s.barrier()

    if debug == 2:
        s.barrier()
        dv = s.op("sync", "dma_start", inc="dsy", inck=16, out=dbg["ln1T"], in_=ln1T[:, :, :])
        s.wait("sync", "dsy", dv)
        return nc, s, dbg

    s.barrier()
    esB.close()
    esQB.close()
    esLn.close()

    esLn2 = ExitStack()
    ln2T = esLn2.enter_context(nc.sbuf_tensor("ln2T", [128, NDT, TOWN], BF16))
    esCtx = ExitStack()
    ctxT = esCtx.enter_context(nc.sbuf_tensor("ctxT", [128, HEADS, TOWN], BF16))
    esB2 = ExitStack()
    qh = [esB2.enter_context(nc.sbuf_tensor(f"qh{i}", [128, TOWN], BF16)) for i in range(2)]
    kh = [esB2.enter_context(nc.sbuf_tensor(f"kh{i}", [128, S], BF16)) for i in range(2)]
    vh = [esB2.enter_context(nc.sbuf_tensor(f"vh{i}", [128, NTT, 128], BF16)) for i in range(2)]
    probs = [esB2.enter_context(nc.sbuf_tensor(f"probs{i}", [128, S], BF16)) for i in range(2)]
    probs_s = [esB2.enter_context(nc.sbuf_tensor(f"probs_s{i}", [128, S], BF16)) for i in range(2)]
    probsT = esB2.enter_context(nc.sbuf_tensor("probsT", [128, NTT, TOWN], BF16))
    negmax = esB2.enter_context(nc.sbuf_tensor("negmax", [128, 8], F32))
    nm4 = esB2.enter_context(nc.sbuf_tensor("nm4", [128, 8], F32))
    den = esB2.enter_context(nc.sbuf_tensor("den", [128, 8], F32))
    rec = esB2.enter_context(nc.sbuf_tensor("rec", [128, 8], F32))

    bank_free = {b: ("ac", 0) for b in range(8)}
    hcons = {}
    last_exp = {}
    last_rec = {}
    last_pool = {}
    pool_p = {0: 0, 1: 0}
    tr_p = {0: 0, 1: 0}
    lqk_h = {}
    lv_h = {}
    smm = {}
    a1_h = {}
    pool_h = {}
    ev_h = {}

    def emit_qk_loads(h):
        hp = h % 2
        s.wait("sync", "pe", hcons.get(("q", hp), 0))
        s.op("sync", "dma_start", inc=f"dh{hp}", inck=16,
             out=qh[hp][:, :], in_=q_sp[h * 128:(h + 1) * 128, :])
        s.wait("sync", "pe", hcons.get(("k", hp), 0))
        lqk_h[h] = s.op("sync", "dma_start", inc=f"dh{hp}", inck=16,
                        out=kh[hp][:, :], in_=k_sp[h * 128:(h + 1) * 128, :])

    def emit_v_load(h):
        hp = h % 2
        s.wait("sync", "pe", hcons.get(("v", hp), 0))
        lv_h[h] = s.op("sync", "dma_start", inc=f"dh{hp}", inck=16,
                       out=vh[hp][:, :, :], in_=v_sp_r[:, :, h * 128:(h + 1) * 128])

    def emit_scores(h, qi):
        hp = h % 2
        b0 = 4 * (qi % 2)
        s.wait("tensor", f"dh{hp}", lqk_h[h])
        ms = {}
        for c in (2, 3, 0, 1):
            s.wait("tensor", *bank_free[b0 + c])
            ms[c] = s.op("tensor", "matmul", bankf(b0 + c),
                         qh[hp][:, qi * 128:(qi + 1) * 128], kh[hp][:, c * 512:(c + 1) * 512],
                         start=True, stop=True, inc="pe")
        smm[(h, qi)] = ms
        if qi == 7:
            hcons[("q", hp)] = ms[1]
            hcons[("k", hp)] = ms[1]

    def emit_chain(h, qi):
        pp = qi % 2
        b0 = 4 * (qi % 2)
        ms = smm[(h, qi)]
        s.wait("vector", "ac", last_exp.get(qi, 0))
        vm = 0
        for c in (2, 3, 0, 1):
            s.wait("vector", "pe", ms[c])
            vm = s.op("vector", "tensor_reduce", nm4[:, 4 * pp + c:4 * pp + c + 1],
                      bankf(b0 + c), AX.X, ALU.max, negate=True, inc="ve")
        s.wait("vector", "ve", vm)
        vcomb = s.op("vector", "tensor_reduce", negmax[:, qi:qi + 1],
                     nm4[:, 4 * pp:4 * pp + 4], AX.X, ALU.min, inc="ve")
        s.wait("scalar", "ve", vcomb)
        s.wait("scalar", "ve", last_rec.get(qi, 0))
        s.wait("scalar", "po", pool_p[pp])
        a1 = s.op("scalar", "activation", probs[pp][:, :], bankf(b0, 4), AF.Exp,
                  bias=negmax[:, qi:qi + 1], scale=1.0,
                  accum_out=den[:, qi:qi + 1], inc="ac")
        last_exp[qi] = a1
        a1_h[(h, qi)] = a1
        bank_free[b0 + 2] = ("ac", a1)
        bank_free[b0 + 3] = ("ac", a1)
        s.wait("vector", "ac", a1)
        s.wait("vector", "po", last_pool.get(qi, 0))
        vrec = s.op("vector", "reciprocal", rec[:, qi:qi + 1], den[:, qi:qi + 1], inc="ve")
        last_rec[qi] = vrec
        s.wait("gpsimd", "ve", vrec)
        s.wait("gpsimd", "ac", a1)
        s.wait("gpsimd", "pe", tr_p[pp])
        pv = s.op("gpsimd", "tensor_scalar", probs_s[pp][:, :], probs[pp][:, :],
                  rec[:, qi:qi + 1], None, ALU.mult, inc="po")
        last_pool[qi] = pv
        pool_p[pp] = pv
        pool_h[(h, qi)] = pv

    def emit_transpose(h, qi):
        pp = qi % 2
        b0 = 4 * (qi % 2)
        s.wait("tensor", "po", pool_h[(h, qi)])
        s.wait("tensor", "ac", a1_h[(h, qi)])
        for j in range(NTT):
            bb = b0 + j // 8
            s.op("tensor", "matmul", bankb(bb)[:, (j % 8) * 128:(j % 8 + 1) * 128],
                 probs_s[pp][:, j * 128:(j + 1) * 128], ident_b[:, :],
                 is_transpose=True, start=(j % 8 == 0), stop=(j % 8 == 7),
                 skip_group_check=True, inc=("pe" if j % 8 == 7 else None))
        tr_v = s.cnt["pe"]
        tr_p[pp] = tr_v
        qcols = slice(qi * 128, (qi + 1) * 128)
        s.wait("scalar", "pe", tr_v)
        e0 = s.op("scalar", "copy", probsT[:, 0:8, qcols], bankb(b0)[:, :], inc="ac")
        bank_free[b0] = ("ac", e0)
        s.wait("vector", "pe", tr_v)
        e1 = s.op("vector", "tensor_copy", probsT[:, 8:16, qcols], bankb(b0 + 1)[:, :], inc="ve")
        bank_free[b0 + 1] = ("ve", e1)
        ev_h[(h, qi)] = (e0, e1)

    def emit_ctx(h):
        hp = h % 2
        e0, e1 = ev_h[(h, 7)]
        s.wait("tensor", f"dh{hp}", lv_h[h])
        s.wait("tensor", "ac", e0)
        s.wait("tensor", "ve", e1)
        for r in range(2):
            cb = 6 + r
            s.wait("tensor", *bank_free[cb])
            for j in range(NTT):
                s.op("tensor", "matmul", bankf(cb),
                     vh[hp][:, j, :], probsT[:, j, r * 512:(r + 1) * 512],
                     start=(j == 0), stop=(j == NTT - 1),
                     inc=("pe" if j == NTT - 1 else None))
            mmc = s.cnt["pe"]
            if r == 1:
                hcons[("v", hp)] = mmc
            s.wait("scalar", "pe", mmc)
            ec = s.op("scalar", "copy", ctxT[:, h, r * 512:(r + 1) * 512], bankf(cb), inc="ac")
            bank_free[cb] = ("ac", ec)

    emit_qk_loads(0)
    emit_v_load(0)
    emit_qk_loads(1)
    emit_v_load(1)
    emit_scores(0, 0)
    for h in range(HEADS):
        for qi in range(NQT):
            if qi < 7:
                emit_scores(h, qi + 1)
            elif h + 1 < HEADS:
                emit_scores(h + 1, 0)
                if h + 2 < HEADS:
                    emit_qk_loads(h + 2)
            emit_chain(h, qi)
            emit_transpose(h, qi)
        emit_ctx(h)
        if h + 2 < HEADS:
            emit_v_load(h + 2)

    if debug == 3:
        dbg["ctxT"] = nc.dram_tensor("d_ctxT", [128, HEADS, TOWN], BF16, kind="ExternalOutput").ap()
        s.barrier()
        dv = s.op("sync", "dma_start", inc="dsy", inck=16, out=dbg["ctxT"], in_=ctxT[:, :, :])
        s.wait("sync", "dsy", dv)
        return nc, s, dbg

    s.barrier()
    esB2.close()

    esC = ExitStack()
    ob_cols = esC.enter_context(nc.sbuf_tensor("ob_cols", [128, 32], F32))
    with nc.sbuf_tensor("brow2", [1, D], F32) as brow2:
        bv2 = s.op("sync", "dma_start", inc="dsy", inck=16, out=brow2[:, :], in_=ob[None, :])
        s.wait("tensor", "dsy", bv2)
        for ctb in range(32):
            s.op("tensor", "matmul", bankf(0)[:, ctb:ctb + 1], brow2[0:1, ctb * 128:(ctb + 1) * 128],
                 id1_f[:, :], is_transpose=True, start=(ctb == 0), stop=(ctb == 31),
                 skip_group_check=True, inc=("pe" if ctb == 31 else None))
        pev = s.cnt["pe"]
        s.wait("scalar", "pe", pev)
        a_ob = s.op("scalar", "copy", ob_cols[:, :], bankf(0)[:, 0:32], inc="ac")
        s.barrier()

    owst = [esC.enter_context(nc.sbuf_tensor(f"owst{i}", [128, NDT, 128], F32)) for i in range(2)]
    owbf = [esC.enter_context(nc.sbuf_tensor(f"owbf{i}", [128, NDT, 128], BF16)) for i in range(2)]
    atst = [esC.enter_context(nc.sbuf_tensor(f"atst{i}", [128, TOWN], F32)) for i in range(2)]
    xcols = [esC.enter_context(nc.sbuf_tensor(f"xcols{i}", [128, NQT, 128], F32)) for i in range(2)]
    o1st = [esC.enter_context(nc.sbuf_tensor(f"o1st{i}", [128, NQT, 128], F32)) for i in range(2)]

    ow_r = ow.rearrange("(ct p) c -> p ct c", p=128)
    x_own_r = x_b[0:TOWN, :].rearrange("(j p) c -> p j c", p=128)
    o1_r = o1_sp.rearrange("(j p) c -> p j c", p=128)

    bank_free = {b: ("ac", 0) for b in range(8)}
    lastconv = {0: 0, 1: 0}
    lastmm = {0: 0, 1: 0}
    last_at = {0: 0, 1: 0}
    last_o1 = {0: 0, 1: 0}
    last_xc = {0: 0, 1: 0}
    for ot in range(NDT):
        p = ot % 2
        s.wait("sync", "ve", lastconv[p])
        dv = s.op("sync", "dma_start", inc=f"dow{p}", inck=16,
                  out=owst[p][:, :, :], in_=ow_r[:, :, ot * 128:(ot + 1) * 128])
        s.wait("sync", "ve", last_xc[p])
        dx = s.op("sync", "dma_start", inc=f"dxc{p}", inck=16,
                  out=xcols[p][:, :, :], in_=x_own_r[:, :, ot * 128:(ot + 1) * 128])
        s.wait("vector", f"dow{p}", dv)
        s.wait("vector", "pe", lastmm[p])
        vc = s.op("vector", "tensor_copy", owbf[p][:, :, :], owst[p][:, :, :], inc="ve")
        lastconv[p] = vc
        s.wait("tensor", "ve", vc)
        evs = []
        for r in range(2):
            bk = (2 * ot + r) % 4
            s.wait("tensor", *bank_free[bk])
            for ct in range(NDT):
                s.op("tensor", "matmul", bankf(bk),
                     owbf[p][:, ct, :], ctxT[:, ct, r * 512:(r + 1) * 512],
                     start=(ct == 0), stop=(ct == NDT - 1),
                     inc=("pe" if ct == NDT - 1 else None))
            mm_v = s.cnt["pe"]
            s.wait("scalar", "pe", mm_v)
            if r == 0:
                s.wait("scalar", "pe", last_at[p])
            ev = s.op("scalar", "activation", atst[p][:, r * 512:(r + 1) * 512], bankf(bk),
                      AF.Identity, bias=ob_cols[:, ot:ot + 1], scale=1.0, inc="ac")
            bank_free[bk] = ("ac", ev)
            evs.append(ev)
        lastmm[p] = s.cnt["pe"]
        tb = 4 + 2 * (ot % 2)
        s.wait("tensor", "ac", evs[-1])
        for j in range(NQT):
            bb = tb + j // 4
            s.wait("tensor", *bank_free[bb]) if j % 4 == 0 else None
            s.op("tensor", "matmul", bankf(bb)[:, (j % 4) * 128:(j % 4 + 1) * 128],
                 atst[p][:, j * 128:(j + 1) * 128], ident_f[:, :],
                 start=(j % 4 == 0), stop=(j % 4 == 3),
                 skip_group_check=True, inc=("pe" if j % 4 == 3 else None))
        tr_v = s.cnt["pe"]
        last_at[p] = tr_v
        s.wait("vector", "pe", tr_v)
        s.wait("vector", f"dxc{p}", dx)
        s.wait("vector", f"dso{p}", last_o1[p])
        s.op("vector", "tensor_tensor", o1st[p][:, 0:4, :], bankf(tb), xcols[p][:, 0:4, :],
             ALU.add, inc="ve")
        v6 = s.op("vector", "tensor_tensor", o1st[p][:, 4:8, :], bankf(tb + 1), xcols[p][:, 4:8, :],
                  ALU.add, inc="ve")
        last_xc[p] = v6
        bank_free[tb] = ("ve", v6)
        bank_free[tb + 1] = ("ve", v6)
        s.wait("scalar", "ve", v6)
        sv = s.op("scalar", "dma_start", inc=f"dso{p}", inck=16,
                  out=o1_r[:, :, ot * 128:(ot + 1) * 128], in_=o1st[p][:, :, :])
        last_o1[p] = sv

    s.barrier()
    if debug == 4:
        return nc, s, dbg
    esC.close()
    esCtx.close()

    esC2 = ExitStack()
    xf2 = [esC2.enter_context(nc.sbuf_tensor(f"xf2{i}", [128, D], F32)) for i in range(2)]
    xsq2 = esC2.enter_context(nc.sbuf_tensor("xsq2", [128, D], F32))
    lnb2 = [esC2.enter_context(nc.sbuf_tensor(f"lnb2{i}", [128, D], BF16)) for i in range(2)]
    lu2 = {}
    for ti in range(NQT):
        emit_ln_tile((xf2, xsq2, lnb2), o1_sp, ti * 128, ti, ln2T,
                     slice(ti * 128, (ti + 1) * 128), lu2)
    s.barrier()
    esC2.close()

    if debug == 45:
        dbg["ln2T"] = nc.dram_tensor("d_ln2T", [128, NDT, TOWN], BF16, kind="ExternalOutput").ap()
        dv = s.op("sync", "dma_start", inc="dsy", inck=16, out=dbg["ln2T"], in_=ln2T[:, :, :])
        s.wait("sync", "dsy", dv)
        return nc, s, dbg

    esD = ExitStack()
    pk1 = [esD.enter_context(nc.sbuf_tensor(f"pk1_{i}", [128, 2048], I32)) for i in range(2)]
    pk3 = [esD.enter_context(nc.sbuf_tensor(f"pk3_{i}", [128, 2048], I32)) for i in range(2)]
    sc1 = [esD.enter_context(nc.sbuf_tensor(f"sc1_{i}", [128, 64], F32)) for i in range(2)]
    sc3 = [esD.enter_context(nc.sbuf_tensor(f"sc3_{i}", [128, 64], F32)) for i in range(2)]
    msb1 = esD.enter_context(nc.sbuf_tensor("msb1", [128, 2048], I32))
    lsb1 = esD.enter_context(nc.sbuf_tensor("lsb1", [128, 2048], I32))
    msb3 = esD.enter_context(nc.sbuf_tensor("msb3", [128, 2048], I32))
    lsb3 = esD.enter_context(nc.sbuf_tensor("lsb3", [128, 2048], I32))
    W1nat = esD.enter_context(nc.sbuf_tensor("W1nat", [128, NDT, 128], BF16))
    W3nat = esD.enter_context(nc.sbuf_tensor("W3nat", [128, NDT, 128], BF16))
    W1T = esD.enter_context(nc.sbuf_tensor("W1T", [128, NDT, 128], BF16))
    W3T = esD.enter_context(nc.sbuf_tensor("W3T", [128, NDT, 128], BF16))
    ast = esD.enter_context(nc.sbuf_tensor("ast", [128, TOWN], BF16))
    bst = esD.enter_context(nc.sbuf_tensor("bst", [128, TOWN], BF16))
    hstage = [esD.enter_context(nc.sbuf_tensor(f"hstage{i}", [128, TOWN], BF16)) for i in range(2)]

    w1v = w1.rearrange("(h s) j -> h (s j)", s=32)
    w3v = w3.rearrange("(h s) j -> h (s j)", s=32)
    s1v = s1.rearrange("(h g) -> h g", g=64)
    s3v = s3.rearrange("(h g) -> h g", g=64)
    ha_r = ha_sp.rearrange("(ht p) t -> p ht t", p=128)

    def emit_dequant(pk, scv, msb, lsb, Wnat, nseg, wait_pk, wait_sc, wait_wnat_free,
                     prev_po=0):
        s.wait("vector", wait_pk, wait_sc)
        s.wait("vector", "ve", prev_po)
        vm = s.op("vector", "tensor_scalar", msb[:, 0:nseg * 64], pk[:, 0:nseg * 64],
                  4, None, ALU.arith_shift_right, inc="ve")
        vl = s.op("vector", "tensor_scalar", lsb[:, 0:nseg * 64], pk[:, 0:nseg * 64],
                  28, 28, ALU.arith_shift_left, ALU.arith_shift_right, inc="ve")
        s.wait("vector", "ve", vl)
        m3 = msb[:, 0:nseg * 64].rearrange("p (s j) -> p s j", j=64)
        l3 = lsb[:, 0:nseg * 64].rearrange("p (s j) -> p s j", j=64)
        sce = scv[:, 0:2 * nseg:2].unsqueeze(2).broadcast_to([128, nseg, 64])
        sco = scv[:, 1:2 * nseg:2].unsqueeze(2).broadcast_to([128, nseg, 64])
        s.wait("vector", "pe", wait_wnat_free)
        po = s.op("vector", "tensor_tensor", Wnat[:, 0:nseg, 0:64], m3, sce, ALU.mult, inc="ve")
        ve = s.op("vector", "tensor_tensor", Wnat[:, 0:nseg, 64:128], l3, sco, ALU.mult, inc="ve")
        return ve, ve

    def emit_wtrans(Wnat, WT, nseg, banks2, ready_ve, ready_po, wt_free):
        s.wait("tensor", "ve", max(ready_ve, ready_po))
        evs = []
        for base in range(0, nseg, 16):
            n = min(16, nseg - base)
            for bi in range(2):
                s.wait("tensor", *bank_free[banks2 + bi])
            for j in range(n):
                bb = banks2 + j // 8
                s.op("tensor", "matmul", bankb(bb)[:, (j % 8) * 128:(j % 8 + 1) * 128],
                     Wnat[:, base + j, :], ident_b[:, :],
                     is_transpose=True, start=(j % 8 == 0),
                     stop=(j % 8 == 7 or j == n - 1),
                     skip_group_check=True,
                     inc=("pe" if (j % 8 == 7 or j == n - 1) else None))
            tr_v = s.cnt["pe"]
            s.wait("scalar", "pe", tr_v)
            s.wait("scalar", "pe", wt_free)
            nb1 = min(8, n)
            e1 = s.op("scalar", "copy", WT[:, base:base + nb1, :], bankb(banks2)[:, 0:nb1 * 128], inc="ac")
            bank_free[banks2] = ("ac", e1)
            if n > 8:
                e2 = s.op("scalar", "copy", WT[:, base + 8:base + n, :],
                          bankb(banks2 + 1)[:, 0:(n - 8) * 128], inc="ac")
                bank_free[banks2 + 1] = ("ac", e2)
                evs.append(e2)
            evs.append(e1)
        return max(evs), s.cnt["pe"]

    bank_free = {b: ("ac", 0) for b in range(8)}
    lastdeq = {"w1": 0, "w3": 0}
    lastwtr = {"w1": 0, "w3": 0}
    lastpo = {}
    lasth = {0: 0, 1: 0}
    lastpk = {}
    lastpo_h = 0
    for ht in range(NHT):
        p = ht % 2
        s.wait("sync", "ve", lastpk.get(("pk1", p), 0))
        s.wait("sync", "ve", lastpo.get("w1", 0))
        d1 = s.op("sync", "dma_start", inc=f"dd1{p}", inck=16,
                  out=pk1[p][:, :], in_=w1v[ht * 128:(ht + 1) * 128, :])
        ds1 = s.op("sync", "dma_start", inc=f"dd1{p}", inck=16,
                   out=sc1[p][:, :], in_=s1v[ht * 128:(ht + 1) * 128, :])
        s.wait("sync", "ve", lastpk.get(("pk3", p), 0))
        s.wait("sync", "ve", lastpo.get("w3", 0))
        d3 = s.op("sync", "dma_start", inc=f"dd3{p}", inck=16,
                  out=pk3[p][:, :], in_=w3v[ht * 128:(ht + 1) * 128, :])
        ds3 = s.op("sync", "dma_start", inc=f"dd3{p}", inck=16,
                   out=sc3[p][:, :], in_=s3v[ht * 128:(ht + 1) * 128, :])
        ve1, po1 = emit_dequant(pk1[p], sc1[p], msb1, lsb1, W1nat, NDT, f"dd1{p}", ds1,
                                lastwtr["w1"], lastpo.get("w1", 0))
        ve3, po3 = emit_dequant(pk3[p], sc3[p], msb3, lsb3, W3nat, NDT, f"dd3{p}", ds3,
                                lastwtr["w3"], lastpo.get("w3", 0))
        lastpo["w1"] = po1
        lastpo["w3"] = po3
        lastpk[("pk1", p)] = ve1
        lastpk[("pk3", p)] = ve3
        wt1, lastwtr["w1"] = emit_wtrans(W1nat, W1T, NDT, 0, ve1, po1, lastdeq["w1"])
        wt3, lastwtr["w3"] = emit_wtrans(W3nat, W3T, NDT, 2, ve3, po3, lastdeq["w3"])
        s.wait("tensor", "ac", max(wt1, wt3))
        for (WT, bb0) in ((W1T, 4), (W3T, 6)):
            for ch in range(2):
                s.wait("tensor", *bank_free[bb0 + ch])
                for dt in range(NDT):
                    s.op("tensor", "matmul", bankf(bb0 + ch),
                         WT[:, dt, :], ln2T[:, dt, ch * 512:(ch + 1) * 512],
                         start=(dt == 0), stop=(dt == NDT - 1),
                         inc=("pe" if dt == NDT - 1 else None))
        mm_v = s.cnt["pe"]
        lastdeq["w1"] = mm_v
        lastdeq["w3"] = mm_v
        s.wait("scalar", "pe", mm_v)
        s.wait("scalar", "po", lastpo_h)
        a1 = s.op("scalar", "activation", ast[:, :], bankf(4, 2), AF.Silu, inc="ac")
        a2 = s.op("scalar", "copy", bst[:, :], bankf(6, 2), inc="ac")
        for bb in (4, 5):
            bank_free[bb] = ("ac", a1)
        for bb in (6, 7):
            bank_free[bb] = ("ac", a2)
        s.wait("gpsimd", "ac", a2)
        s.wait("gpsimd", f"dsh{p}", lasth[p])
        vh_ = s.op("gpsimd", "tensor_tensor", hstage[p][:, :], ast[:, :], bst[:, :], ALU.mult, inc="po")
        lastpo_h = vh_
        s.wait("scalar", "po", vh_)
        sv = s.op("scalar", "dma_start", inc=f"dsh{p}", inck=16,
                  out=ha_sp[ht * 128:(ht + 1) * 128, :], in_=hstage[p][:, :])
        lasth[p] = sv

    s.barrier()
    if debug == 5:
        return nc, s, dbg
    esD.close()
    esLn2.close()

    if debug in (6, 7):
        yA_sp = nc.dram_tensor("d_yA", [TOWN, D], F32, kind="ExternalOutput").ap()
    else:
        yA_sp = nc.dram_tensor("yA_sp", [TOWN, D], F32).ap()
    esE = ExitStack()
    hactH = esE.enter_context(nc.sbuf_tensor("hactH", [128, 43, TOWN], BF16))
    pk2 = [esE.enter_context(nc.sbuf_tensor(f"pk2_{i}", [128, 2752], I32)) for i in range(2)]
    sc2 = [esE.enter_context(nc.sbuf_tensor(f"sc2_{i}", [128, 86], F32)) for i in range(2)]
    msb2 = esE.enter_context(nc.sbuf_tensor("msb2", [128, 2752], I32))
    lsb2 = esE.enter_context(nc.sbuf_tensor("lsb2", [128, 2752], I32))
    W2nat = [esE.enter_context(nc.sbuf_tensor("W2nat", [128, 43, 128], BF16))] * 2
    W2T = [esE.enter_context(nc.sbuf_tensor("W2T", [128, 43, 128], BF16))] * 2
    ystage = [esE.enter_context(nc.sbuf_tensor(f"ystage{i}", [128, TOWN], F32)) for i in range(2)]
    yAc = [esE.enter_context(nc.sbuf_tensor(f"yAc{i}", [128, NQT, 128], F32)) for i in range(2)]
    sum1 = [esE.enter_context(nc.sbuf_tensor(f"sum1_{i}", [128, NQT, 128], F32)) for i in range(2)]
    o1c = [esE.enter_context(nc.sbuf_tensor(f"o1c{i}", [128, NQT, 128], F32)) for i in range(2)]
    outst = [esE.enter_context(nc.sbuf_tensor(f"outst{i}", [128, NQT, 128], F32)) for i in range(2)]

    w2v = w2.rearrange("(o s) j -> o (s j)", s=86)
    yA_r = yA_sp.rearrange("(j p) c -> p j c", p=128)
    s2v = s2.rearrange("(o g) -> o g", g=172)
    y_own_r = y_own.rearrange("(j p) c -> p j c", p=128)

    for half in range(2):
        h0 = half * 43
        s.barrier()
        hl = 0
        for ck in range(4):
            r0 = 11 * ck
            r1 = min(43, 11 * (ck + 1))
            hl = s.op("sync", "dma_start", inc="dha", inck=16,
                      out=hactH[:, r0:r1, :], in_=ha_r[:, h0 + r0:h0 + r1, :])
        s.wait("tensor", "dha", hl)
        lastpk2 = {0: 0, 1: 0}
        lastpo2 = 0
        lastW2 = 0
        lastW2tr = 0
        lastst = {0: 0, 1: 0}
        lastyA = {0: 0, 1: 0}
        lasto1 = {0: 0, 1: 0}
        lastout = {0: 0, 1: 0}
        lastsum = {0: 0, 1: 0}
        eload = {}
        ystv = {}

        def e_loads(ot):
            p = ot % 2
            s.wait("sync", "ve", lastpk2[p])
            s.wait("sync", "ve", lastpo2)
            s.op("sync", "dma_start", inc=f"de{p}", inck=16,
                 out=pk2[p][:, :], in_=w2v[ot * 128:(ot + 1) * 128, half * 2752:(half + 1) * 2752])
            ds2 = s.op("sync", "dma_start", inc=f"de{p}", inck=16,
                       out=sc2[p][:, :], in_=s2v[ot * 128:(ot + 1) * 128, half * 86:(half + 1) * 86])
            do = 0
            if half == 1:
                s.wait("sync", "ve", lastyA[p])
                s.op("sync", "dma_start", inc=f"dyo{p}", inck=16,
                     out=yAc[p][:, :, :], in_=yA_r[:, :, ot * 128:(ot + 1) * 128])
                s.wait("sync", "ve", lasto1[p])
                do = s.op("sync", "dma_start", inc=f"dyo{p}", inck=16,
                          out=o1c[p][:, :, :], in_=o1_r[:, :, ot * 128:(ot + 1) * 128])
            eload[ot] = (ds2, do)

        vs1_h = {}

        def e_finish(ot):
            p = ot % 2
            vs1 = vs1_h.get(ot, 0)
            s.wait("tensor", "ac", ystv[ot])
            for bi in (6, 7):
                s.wait("tensor", *bank_free[bi])
            for j in range(NQT):
                bb = 6 + j // 4
                s.op("tensor", "matmul", bankf(bb)[:, (j % 4) * 128:(j % 4 + 1) * 128],
                     ystage[p][:, j * 128:(j + 1) * 128], ident_f[:, :],
                     start=(j % 4 == 0), stop=(j % 4 == 3),
                     skip_group_check=True, inc=("pe" if j % 4 == 3 else None))
            tr_v = s.cnt["pe"]
            lastst[p] = tr_v
            if half == 0:
                s.wait("scalar", "pe", tr_v)
                s.wait("scalar", f"dya{p}", lastout[p])
                e6 = s.op("scalar", "copy", outst[p][:, 0:4, :], bankf(6), inc="ac")
                e7 = s.op("scalar", "copy", outst[p][:, 4:8, :], bankf(7), inc="ac")
                bank_free[6] = ("ac", e6)
                bank_free[7] = ("ac", e7)
                s.wait("scalar", "ac", e7)
                sv = s.op("scalar", "dma_start", inc=f"dya{p}", inck=16,
                          out=yA_r[:, :, ot * 128:(ot + 1) * 128], in_=outst[p][:, :, :])
                lastout[p] = sv
            else:
                s.wait("vector", "pe", tr_v)
                s.wait("vector", "ve", vs1)
                s.wait("vector", f"dout{p}", lastout[p])
                s.op("vector", "tensor_tensor", outst[p][:, 0:4, :], bankf(6), sum1[p][:, 0:4, :],
                     ALU.add, inc="ve")
                vo = s.op("vector", "tensor_tensor", outst[p][:, 4:8, :], bankf(7), sum1[p][:, 4:8, :],
                          ALU.add, inc="ve")
                lastsum[p] = vo
                for bb in (6, 7):
                    bank_free[bb] = ("ve", vo)
                s.wait("scalar", "ve", vo)
                so = s.op("scalar", "dma_start", inc=f"dout{p}", inck=16,
                          out=y_own_r[:, :, ot * 128:(ot + 1) * 128], in_=outst[p][:, :, :])
                lastout[p] = so

        e_loads(0)
        for ot in range(NDT):
            p = ot % 2
            if ot + 1 < NDT:
                e_loads(ot + 1)
            ve2, po2 = emit_dequant(pk2[p], sc2[p], msb2, lsb2, W2nat[p], 43, f"de{p}", eload[ot][0],
                                    lastW2tr, lastpo2)
            lastpo2 = po2
            lastpk2[p] = ve2
            if half == 1:
                s.wait("vector", f"dyo{p}", eload[ot][1])
                s.wait("vector", "ve", lastsum[p])
                vs1 = s.op("vector", "tensor_tensor", sum1[p][:, :, :], o1c[p][:, :, :],
                           yAc[p][:, :, :], ALU.add, inc="ve")
                lastyA[p] = vs1
                lasto1[p] = vs1
                vs1_h[ot] = vs1
            wt2, lastW2tr = emit_wtrans(W2nat[p], W2T[p], 43, 0, ve2, po2, lastW2)
            if ot >= 1:
                e_finish(ot - 1)
            s.wait("tensor", "ac", wt2)
            for ch in range(2):
                s.wait("tensor", *bank_free[4 + ch])
                for j in range(43):
                    s.op("tensor", "matmul", bankf(4 + ch),
                         W2T[p][:, j, :], hactH[:, j, ch * 512:(ch + 1) * 512],
                         start=(j == 0), stop=(j == 42),
                         inc=("pe" if j == 42 else None))
            mm_v = s.cnt["pe"]
            lastW2 = mm_v
            s.wait("scalar", "pe", mm_v)
            s.wait("scalar", "pe", lastst[p])
            ev = s.op("scalar", "copy", ystage[p][:, :], bankf(4, 2), inc="ac")
            for bb in (4, 5):
                bank_free[bb] = ("ac", ev)
            ystv[ot] = ev
        e_finish(NDT - 1)

    s.barrier()
    return nc, s, dbg


def finalize(nc, s):
    with nc.Block() as block:
        @block.sync
        def _(e):
            for fn in s.st["sync"]:
                fn(e)

        @block.tensor
        def _(e):
            for fn in s.st["tensor"]:
                fn(e)

        @block.vector
        def _(e):
            for fn in s.st["vector"]:
                fn(e)

        @block.scalar
        def _(e):
            for fn in s.st["scalar"]:
                fn(e)

        @block.gpsimd
        def _(e):
            for fn in s.st["gpsimd"]:
                fn(e)
    return nc


def make_in_maps(inputs):
    x = np.ascontiguousarray(inputs["L0_attn_x"], dtype=np.float32)
    qkvw = np.ascontiguousarray(inputs["L0_attn_attn_qkvw"], dtype=np.float32)
    qkvb = np.ascontiguousarray(inputs["L0_attn_attn_qkvb"], dtype=np.float32)
    ow = np.ascontiguousarray(inputs["L0_attn_attn_ow"], dtype=np.float32)
    ob = np.ascontiguousarray(inputs["L0_attn_attn_ob"], dtype=np.float32)
    ws = {k: np.ascontiguousarray(inputs[f"L0_ffn_{k}"]) for k in ("w1", "s1", "w2", "s2", "w3", "s3")}
    in_maps = []
    for c in range(8):
        b, half = c // 2, c % 2
        own = x[b, half * TOWN:(half + 1) * TOWN]
        other = x[b, (1 - half) * TOWN:(2 - half) * TOWN]
        xb = np.concatenate([own, other], axis=0)
        m = {"x_b": xb, "qkvw": qkvw, "qkvb": qkvb, "ow": ow, "ob": ob}
        m.update(ws)
        in_maps.append(m)
    return in_maps


def kernel(**inputs):
    in_maps = make_in_maps(inputs)
    nc, s, dbg = build(debug=0)
    finalize(nc, s)
    res = run_bass_kernel_spmd(nc, in_maps, core_ids=list(range(8)))
    out = np.empty((4, S, D), dtype=np.float32)
    for c in range(8):
        b, half = c // 2, c % 2
        out[b, half * TOWN:(half + 1) * TOWN] = res.results[c]["y_own"]
    return out


# version 07
# speedup vs baseline: 1.5380x; 1.5380x over previous
import math
from contextlib import ExitStack
import numpy as np
import concourse.bass as bass
import concourse.mybir as mybir
from concourse.bass_utils import run_bass_kernel_spmd

F32 = mybir.dt.float32
BF16 = mybir.dt.bfloat16
I32 = mybir.dt.int32
AF = mybir.ActivationFunctionType
ALU = mybir.AluOpType
AX = mybir.AxisListType

D = 4096
S = 2048
TOWN = 1024
HEADS = 32
HD = 128
HID = 11008
NDT = D // 128
NHT = HID // 128
NTT = S // 128
NQT = TOWN // 128
SCALE = 1.0 / math.sqrt(HD)
EPS = 1e-5

ENGS = ("sync", "tensor", "vector", "scalar", "gpsimd")


class Sched:
    def __init__(self, nc):
        self.nc = nc
        self.st = {e: [] for e in ENGS}
        self.sems = {}
        self.cnt = {}
        self.seen = {e: {} for e in ENGS}

    def sem(self, name):
        self.sems[name] = self.nc.alloc_semaphore(name)
        self.cnt[name] = 0

    def wait(self, eng, name, v):
        if v <= 0 or v <= self.seen[eng].get(name, 0):
            return
        self.seen[eng][name] = v
        sem = self.sems[name]
        self.st[eng].append(lambda e, sem=sem, v=v: e.wait_ge(sem, v))

    def op(self, eng, meth, *args, inc=None, inck=1, **kw):
        sem = self.sems[inc] if inc else None

        def fn(e, meth=meth, args=args, kw=kw, sem=sem, inck=inck):
            ins = getattr(e, meth)(*args, **kw)
            if sem is not None:
                ins.then_inc(sem, inck)
        self.st[eng].append(fn)
        if inc:
            self.cnt[inc] += inck
            return self.cnt[inc]
        return None

    def raw(self, eng, fn):
        self.st[eng].append(fn)

    def barrier(self, engines=ENGS):
        marks = {n: self.cnt[n] for n in self.cnt}
        own = {"sync": "dsy", "tensor": "pe", "vector": "ve",
               "scalar": "ac", "gpsimd": "po"}
        for e in engines:
            for n, v in marks.items():
                if v > 0 and n != own[e]:
                    self.wait(e, n, v)


def ceil_div(a, b):
    return (a + b - 1) // b


def build(debug=0):
    nc = bass.Bass("TRN2", target_bir_lowering=False, debug=False)
    s = Sched(nc)
    for name in ("pe", "ve", "ac", "po", "dsy", "dsc", "dgp",
                 "dxa0", "dxa1", "dwq0", "dwq1", "dsq0", "dsq1", "dsv0", "dsv1",
                 "dh0", "dh1", "dow0", "dow1", "dxc0", "dxc1", "dso0", "dso1",
                 "dd10", "dd11", "dd30", "dd31", "dsh0", "dsh1", "dha",
                 "de0", "de1", "dyo0", "dyo1", "dya0", "dya1", "dout0", "dout1"):
        s.sem(name)

    x_b = nc.dram_tensor("x_b", [S, D], F32, kind="ExternalInput").ap()
    qkvw = nc.dram_tensor("qkvw", [D, 3 * D], F32, kind="ExternalInput").ap()
    qkvb = nc.dram_tensor("qkvb", [3 * D], F32, kind="ExternalInput").ap()
    ow = nc.dram_tensor("ow", [D, D], F32, kind="ExternalInput").ap()
    ob = nc.dram_tensor("ob", [D], F32, kind="ExternalInput").ap()
    w1 = nc.dram_tensor("w1", [352256, 64], I32, kind="ExternalInput").ap()
    s1 = nc.dram_tensor("s1", [704512], F32, kind="ExternalInput").ap()
    w2 = nc.dram_tensor("w2", [352256, 64], I32, kind="ExternalInput").ap()
    s2 = nc.dram_tensor("s2", [704512], F32, kind="ExternalInput").ap()
    w3 = nc.dram_tensor("w3", [352256, 64], I32, kind="ExternalInput").ap()
    s3 = nc.dram_tensor("s3", [704512], F32, kind="ExternalInput").ap()
    y_own = nc.dram_tensor("y_own", [TOWN, D], F32, kind="ExternalOutput").ap()

    def scratch(name, shape, dt, dbg_lvl):
        kind = "ExternalOutput" if (debug == dbg_lvl or debug == 7) else Ellipsis
        if kind is Ellipsis:
            return nc.dram_tensor(name, shape, dt).ap()
        return nc.dram_tensor(name, shape, dt, kind="ExternalOutput").ap()

    q_sp = scratch("d_q", [D, TOWN], BF16, 2)
    k_sp = scratch("d_k", [D, S], BF16, 2)
    v_sp = scratch("d_v", [S, D], BF16, 2)
    o1_sp = scratch("d_out1", [TOWN, D], F32, 4)
    ha_sp = scratch("d_hact", [HID, TOWN], BF16, 5)

    dbg = {}
    if debug in (1, 2):
        dbg["ln1T"] = nc.dram_tensor("d_ln1T", [128, NDT, S], BF16, kind="ExternalOutput").ap()

    ident_b = nc.alloc_sbuf_tensor("ident_b", [128, 128], BF16)
    ident_f = nc.alloc_sbuf_tensor("ident_f", [128, 128], F32)
    iota_f = nc.alloc_sbuf_tensor("iota_f", [128, 128], F32)
    eps_t = nc.alloc_sbuf_tensor("eps_t", [128, 1], F32)
    id1_f = nc.alloc_sbuf_tensor("id1_f", [1, 1], F32)

    psA = nc.alloc_psum_tensor("psA", [128, 4096], F32)

    def bankf(i, n=1):
        return psA[:, i * 512:(i + n) * 512]

    def bankb(i, n=1):
        return psA[:, i * 512:(i + n) * 512].bitcast(BF16)

    s.op("gpsimd", "iota", iota_f[:, :], [[1, 128]], channel_multiplier=-1,
         allow_small_or_imprecise_dtypes=True, inc="po")
    s.op("gpsimd", "memset", eps_t[:, :], EPS, inc="po")
    s.op("gpsimd", "memset", id1_f[:, :], 1.0, inc="po")
    v_iota = s.cnt["po"]
    s.wait("vector", "po", v_iota)
    s.op("vector", "tensor_scalar", ident_b[:, :], iota_f[:, :], 0.0, None, ALU.is_equal, inc="ve")
    s.op("vector", "tensor_scalar", ident_f[:, :], iota_f[:, :], 0.0, None, ALU.is_equal, inc="ve")
    v_ident = s.cnt["ve"]

    st8 = nc.alloc_sbuf_tensor("st8", [128, 8], F32)
    mr_t = nc.alloc_sbuf_tensor("mr_t", [128, 1], F32)
    esLn = ExitStack()
    ln1T = esLn.enter_context(nc.sbuf_tensor("ln1T", [128, NDT, S], BF16))
    esA = ExitStack()
    xf = [esA.enter_context(nc.sbuf_tensor(f"xf{i}", [128, D], F32)) for i in range(2)]
    xsq = esA.enter_context(nc.sbuf_tensor("xsq", [128, D], F32))
    lnb = [esA.enter_context(nc.sbuf_tensor(f"lnb{i}", [128, D], BF16)) for i in range(2)]

    def emit_ln_tile(bufs, src_hbm, row0, ti, lnT_dst, tcols, last_use):
        xf, xsq, lnb = bufs
        p = ti % 2
        s.wait("sync", "ve", last_use.get(("xf", p), 0))
        vdma = s.op("sync", "dma_start", inc=f"dxa{p}", inck=16,
                    out=xf[p][:, :], in_=src_hbm[row0:row0 + 128, :])
        s.wait("vector", f"dxa{p}", vdma)
        v1 = s.op("vector", "tensor_reduce", st8[:, 0:1], xf[p][:, :], AX.X, ALU.add,
                  negate=True, inc="ve")
        s.wait("scalar", f"dxa{p}", vdma)
        s.wait("scalar", "ve", last_use.get("xsq", 0))
        a1 = s.op("scalar", "activation", xsq[:, :], xf[p][:, :], AF.Square, inc="ac")
        s.wait("vector", "ac", a1)
        v2 = s.op("vector", "tensor_reduce", st8[:, 2:3], xsq[:, :], AX.X, ALU.add, inc="ve")
        last_use["xsq"] = v2
        s.wait("vector", "ve", v1)
        s.op("vector", "tensor_scalar", st8[:, 1:2], st8[:, 0:1], 1.0 / D, None, ALU.mult, inc="ve")
        s.wait("vector", "ve", s.cnt["ve"])
        s.op("vector", "tensor_tensor", st8[:, 4:5], st8[:, 1:2], st8[:, 1:2], ALU.mult, inc="ve")
        s.wait("vector", "ve", s.cnt["ve"])
        s.op("vector", "tensor_scalar", st8[:, 3:4], st8[:, 2:3], 1.0 / D, None, ALU.mult, inc="ve")
        s.wait("vector", "ve", s.cnt["ve"])
        v3 = s.op("vector", "tensor_tensor", st8[:, 5:6], st8[:, 3:4], st8[:, 4:5], ALU.subtract, inc="ve")
        s.wait("scalar", "ve", v3)
        a2 = s.op("scalar", "activation", st8[:, 6:7], st8[:, 5:6], AF.Ln, bias=eps_t[:, :], scale=1.0, inc="ac")
        s.wait("vector", "ac", a2)
        v4 = s.op("vector", "tensor_scalar", st8[:, 6:7], st8[:, 6:7], -0.5, None, ALU.mult, inc="ve")
        s.wait("scalar", "ve", v4)
        a3 = s.op("scalar", "activation", st8[:, 7:8], st8[:, 6:7], AF.Exp, inc="ac")
        s.wait("vector", "ac", a3)
        v5 = s.op("vector", "tensor_tensor", mr_t[:, :], st8[:, 1:2], st8[:, 7:8], ALU.mult, inc="ve")
        s.wait("vector", "ve", v5)
        s.wait("vector", "pe", last_use.get(("lnb", p), 0))
        v6 = s.op("vector", "tensor_scalar", lnb[p][:, :], xf[p][:, :],
                  st8[:, 7:8], mr_t[:, :], ALU.mult, ALU.add, inc="ve")
        last_use[("xf", p)] = v6
        s.wait("tensor", "ve", max(v6, v_ident))
        for g in range(4):
            s.wait("tensor", "ac", last_use.get(("Abank", g), 0))
            for j in range(8):
                dt = 8 * g + j
                s.op("tensor", "matmul", bankb(g)[:, j * 128:(j + 1) * 128],
                     lnb[p][:, dt * 128:(dt + 1) * 128], ident_b[:, :],
                     is_transpose=True, start=(j == 0), stop=(j == 7),
                     skip_group_check=True,
                     inc=("pe" if j == 7 else None))
            pev = s.cnt["pe"]
            if g == 3:
                last_use[("lnb", p)] = pev
            s.wait("scalar", "pe", pev)
            av = s.op("scalar", "copy", lnT_dst[:, 8 * g:8 * g + 8, tcols], bankb(g)[:, :], inc="ac")
            last_use[("Abank", g)] = av
        return last_use

    lu = {}
    for ti in range(NTT):
        emit_ln_tile((xf, xsq, lnb), x_b, ti * 128, ti, ln1T,
                     slice(ti * 128, (ti + 1) * 128), lu)

    if debug == 1:
        s.barrier()
        dv = s.op("sync", "dma_start", inc="dsy", inck=16, out=dbg["ln1T"], in_=ln1T[:, :, :])
        s.wait("sync", "dsy", dv)
        return nc, s, dbg

    s.barrier()
    esA.close()

    esQB = ExitStack()
    qb_all = esQB.enter_context(nc.sbuf_tensor("qb_all", [128, 96], F32))
    qb_s = esQB.enter_context(nc.sbuf_tensor("qb_s", [128, 32], F32))
    with nc.sbuf_tensor("brow", [1, 3 * D], F32) as brow:
        bv = s.op("sync", "dma_start", inc="dsy", inck=16, out=brow[:, :], in_=qkvb[None, :])
        s.wait("tensor", "dsy", bv)
        s.wait("tensor", "ve", v_ident)
        for ctb in range(96):
            s.op("tensor", "matmul", bankf(0)[:, ctb:ctb + 1], brow[0:1, ctb * 128:(ctb + 1) * 128],
                 id1_f[:, :], is_transpose=True, start=(ctb == 0), stop=(ctb == 95),
                 skip_group_check=True, inc=("pe" if ctb == 95 else None))
        pev = s.cnt["pe"]
        s.wait("scalar", "pe", pev)
        a_qb = s.op("scalar", "copy", qb_all[:, :], bankf(0)[:, 0:96], inc="ac")
        s.wait("vector", "ac", a_qb)
        v_qb = s.op("vector", "tensor_scalar", qb_s[:, :], qb_all[:, 0:32], SCALE, None, ALU.mult, inc="ve")
        s.barrier()

    esB = ExitStack()
    wstage = [esB.enter_context(nc.sbuf_tensor(f"wstage{i}", [128, NDT, 128], F32)) for i in range(2)]
    wbf = [esB.enter_context(nc.sbuf_tensor(f"wbf{i}", [128, NDT, 128], BF16)) for i in range(2)]
    qkst = [esB.enter_context(nc.sbuf_tensor(f"qkst{i}", [128, S], BF16)) for i in range(2)]
    vtst = [esB.enter_context(nc.sbuf_tensor(f"vtst{i}", [128, NTT, 128], BF16)) for i in range(2)]

    qkvw_r = qkvw.rearrange("(dt p) c -> p dt c", p=128)
    v_sp_r = v_sp.rearrange("(j p) c -> p j c", p=128)
    pair_free = {}
    lastvc = {0: 0, 1: 0}
    lastmm = {0: 0, 1: 0}
    laststage = {0: 0, 1: 0}
    last_vt = {0: 0, 1: 0}
    import os as _os
    _secs = _os.environ.get("B1_SECS", "012")
    for ct in range(96):
        p = ct % 2
        g = ct % 2
        b0 = 4 * g
        sec = ct // 32
        if str(sec) not in _secs:
            continue
        s.wait("sync", "ve", lastvc[p])
        dv = s.op("sync", "dma_start", inc=f"dwq{p}", inck=16,
                  out=wstage[p][:, :, :], in_=qkvw_r[:, :, ct * 128:(ct + 1) * 128])
        s.wait("vector", f"dwq{p}", dv)
        s.wait("vector", "pe", lastmm[p])
        vc = s.op("vector", "tensor_copy", wbf[p][:, :, :], wstage[p][:, :, :], inc="ve")
        lastvc[p] = vc
        s.wait("tensor", "ve", vc)
        ncols = TOWN if sec == 0 else S
        nch = ncols // 512
        nrounds = nch // 2
        ev = 0
        for rnd in range(nrounds):
            pr = b0 + 2 * (rnd % 2)
            key = (g, rnd % 2)
            s.wait("tensor", *pair_free.get(key, ("ac", 0)))
            for ch2 in range(2):
                ch = rnd * 2 + ch2
                for dt in range(NDT):
                    s.op("tensor", "matmul", bankf(pr + ch2),
                         wbf[p][:, dt, :], ln1T[:, dt, ch * 512:(ch + 1) * 512],
                         start=(dt == 0), stop=(dt == NDT - 1),
                         inc=("pe" if (dt == NDT - 1 and ch2 == 1) else None))
            mm_v = s.cnt["pe"]
            s.wait("scalar", "pe", mm_v)
            if rnd == 0:
                s.wait("scalar", f"dsq{p}", laststage[p])
            if sec == 0:
                ev = s.op("scalar", "activation", qkst[p][:, 0:TOWN], bankf(pr, 2), AF.Identity,
                          bias=qb_s[:, ct:ct + 1], scale=SCALE, inc="ac")
            else:
                ev = s.op("scalar", "activation", qkst[p][:, rnd * TOWN:(rnd + 1) * TOWN],
                          bankf(pr, 2), AF.Identity,
                          bias=qb_all[:, ct:ct + 1], scale=1.0, inc="ac")
            pair_free[key] = ("ac", ev)
        lastmm[p] = s.cnt["pe"]
        if sec == 0:
            s.wait("scalar", "ac", ev)
            sv = s.op("scalar", "dma_start", inc=f"dsq{p}", inck=16,
                      out=q_sp[ct * 128:(ct + 1) * 128, :], in_=qkst[p][:, 0:TOWN])
            laststage[p] = sv
        elif sec == 1:
            s.wait("scalar", "ac", ev)
            sv = s.op("scalar", "dma_start", inc=f"dsq{p}", inck=16,
                      out=k_sp[(ct - 32) * 128:(ct - 31) * 128, :], in_=qkst[p][:, :])
            laststage[p] = sv
        else:
            cc = ct - 64
            s.wait("tensor", "ac", ev)
            s.wait("tensor", *pair_free.get((g, 0), ("ac", 0)))
            for j in range(NTT):
                bb = b0 + j // 8
                s.op("tensor", "matmul", bankb(bb)[:, (j % 8) * 128:(j % 8 + 1) * 128],
                     qkst[p][:, j * 128:(j + 1) * 128], ident_b[:, :],
                     is_transpose=True, start=(j % 8 == 0), stop=(j % 8 == 7),
                     skip_group_check=True,
                     inc=("pe" if j % 8 == 7 else None))
        if sec == 2:
            tr_v = s.cnt["pe"]
            s.wait("scalar", "pe", tr_v)
            s.wait("scalar", f"dsv{p}", last_vt[p])
            ev2 = s.op("scalar", "copy", vtst[p][:, :, :], bankb(b0, 2)[:, :], inc="ac")
            pair_free[(g, 0)] = ("ac", ev2)
            s.wait("scalar", "ac", ev2)
            sv = s.op("scalar", "dma_start", inc=f"dsv{p}", inck=16,
                      out=v_sp_r[:, :, cc * 128:(cc + 1) * 128], in_=vtst[p][:, :, :])
            last_vt[p] = sv
        if _os.environ.get("B1_SER"):
            s.barrier()

    if debug == 2:
        s.barrier()
        dv = s.op("sync", "dma_start", inc="dsy", inck=16, out=dbg["ln1T"], in_=ln1T[:, :, :])
        s.wait("sync", "dsy", dv)
        return nc, s, dbg

    s.barrier()
    esB.close()
    esQB.close()
    esLn.close()

    esLn2 = ExitStack()
    ln2T = esLn2.enter_context(nc.sbuf_tensor("ln2T", [128, NDT, TOWN], BF16))
    esCtx = ExitStack()
    ctxT = esCtx.enter_context(nc.sbuf_tensor("ctxT", [128, HEADS, TOWN], BF16))
    esB2 = ExitStack()
    qh = [esB2.enter_context(nc.sbuf_tensor(f"qh{i}", [128, TOWN], BF16)) for i in range(2)]
    kh = [esB2.enter_context(nc.sbuf_tensor(f"kh{i}", [128, S], BF16)) for i in range(2)]
    vh = [esB2.enter_context(nc.sbuf_tensor(f"vh{i}", [128, NTT, 128], BF16)) for i in range(2)]
    probs = [esB2.enter_context(nc.sbuf_tensor(f"probs{i}", [128, S], BF16)) for i in range(2)]
    diag = [esB2.enter_context(nc.sbuf_tensor(f"diag{i}", [128, 128], BF16)) for i in range(2)]
    probsT = esB2.enter_context(nc.sbuf_tensor("probsT", [128, NTT, TOWN], BF16))
    negmax = esB2.enter_context(nc.sbuf_tensor("negmax", [128, 8], F32))
    nm4 = esB2.enter_context(nc.sbuf_tensor("nm4", [128, 8], F32))
    den = esB2.enter_context(nc.sbuf_tensor("den", [128, 8], F32))
    rec = esB2.enter_context(nc.sbuf_tensor("rec", [128, 8], F32))

    bank_free = {b: ("ac", 0) for b in range(8)}
    hcons = {}
    last_exp = {}
    last_rec = {}
    diagmm_p = {0: 0, 1: 0}
    lqk_h = {}
    lv_h = {}
    smm = {}
    a1_h = {}
    vdg_h = {}
    ev_h = {}

    def emit_qk_loads(h):
        hp = h % 2
        s.wait("sync", "pe", hcons.get(("q", hp), 0))
        s.op("sync", "dma_start", inc=f"dh{hp}", inck=16,
             out=qh[hp][:, :], in_=q_sp[h * 128:(h + 1) * 128, :])
        s.wait("sync", "pe", hcons.get(("k", hp), 0))
        lqk_h[h] = s.op("sync", "dma_start", inc=f"dh{hp}", inck=16,
                        out=kh[hp][:, :], in_=k_sp[h * 128:(h + 1) * 128, :])

    def emit_v_load(h):
        hp = h % 2
        s.wait("sync", "pe", hcons.get(("v", hp), 0))
        lv_h[h] = s.op("sync", "dma_start", inc=f"dh{hp}", inck=16,
                       out=vh[hp][:, :, :], in_=v_sp_r[:, :, h * 128:(h + 1) * 128])

    def emit_scores(h, qi):
        hp = h % 2
        b0 = 4 * (qi % 2)
        s.wait("tensor", f"dh{hp}", lqk_h[h])
        ms = {}
        for c in (2, 3, 0, 1):
            s.wait("tensor", *bank_free[b0 + c])
            ms[c] = s.op("tensor", "matmul", bankf(b0 + c),
                         qh[hp][:, qi * 128:(qi + 1) * 128], kh[hp][:, c * 512:(c + 1) * 512],
                         start=True, stop=True, inc="pe")
        smm[(h, qi)] = ms
        if qi == 7:
            hcons[("q", hp)] = ms[1]
            hcons[("k", hp)] = ms[1]

    def emit_chain(h, qi):
        pp = qi % 2
        b0 = 4 * (qi % 2)
        ms = smm[(h, qi)]
        s.wait("vector", "ac", last_exp.get(qi, 0))
        vm = 0
        for c in (2, 3, 0, 1):
            s.wait("vector", "pe", ms[c])
            vm = s.op("vector", "tensor_reduce", nm4[:, 4 * pp + c:4 * pp + c + 1],
                      bankf(b0 + c), AX.X, ALU.max, negate=True, inc="ve")
        s.wait("vector", "ve", vm)
        vcomb = s.op("vector", "tensor_reduce", negmax[:, qi:qi + 1],
                     nm4[:, 4 * pp:4 * pp + 4], AX.X, ALU.min, inc="ve")
        s.wait("scalar", "ve", vcomb)
        s.wait("scalar", "ve", last_rec.get(qi, 0))
        s.wait("scalar", "pe", diagmm_p[pp])
        a1 = s.op("scalar", "activation", probs[pp][:, :], bankf(b0, 4), AF.Exp,
                  bias=negmax[:, qi:qi + 1], scale=1.0,
                  accum_out=den[:, qi:qi + 1], inc="ac")
        last_exp[qi] = a1
        a1_h[(h, qi)] = a1
        s.wait("vector", "ac", a1)
        vrec = s.op("vector", "reciprocal", rec[:, qi:qi + 1], den[:, qi:qi + 1], inc="ve")
        last_rec[qi] = vrec
        s.wait("vector", "ve", vrec)
        s.wait("vector", "pe", diagmm_p[pp])
        vdg = s.op("vector", "tensor_scalar", diag[pp][:, :], ident_b[:, :],
                   rec[:, qi:qi + 1], None, ALU.mult, inc="ve")
        vdg_h[(h, qi)] = vdg

    def emit_transpose(h, qi):
        pp = qi % 2
        b0 = 4 * (qi % 2)
        s.wait("tensor", "ve", vdg_h[(h, qi)])
        s.wait("tensor", "ac", a1_h[(h, qi)])
        for j in range(NTT):
            bb = b0 + j // 4
            s.op("tensor", "matmul", bankf(bb)[:, (j % 4) * 128:(j % 4 + 1) * 128],
                 probs[pp][:, j * 128:(j + 1) * 128], diag[pp][:, :],
                 start=True, stop=True, skip_group_check=True,
                 inc=("pe" if j == NTT - 1 else None))
        tr_v = s.cnt["pe"]
        diagmm_p[pp] = tr_v
        qcols = slice(qi * 128, (qi + 1) * 128)
        s.wait("scalar", "pe", tr_v)
        e0 = s.op("scalar", "copy", probsT[:, 0:4, qcols], bankf(b0), inc="ac")
        bank_free[b0] = ("ac", e0)
        e0 = s.op("scalar", "copy", probsT[:, 4:8, qcols], bankf(b0 + 1), inc="ac")
        bank_free[b0 + 1] = ("ac", e0)
        s.wait("vector", "pe", tr_v)
        e1 = s.op("vector", "tensor_copy", probsT[:, 8:12, qcols], bankf(b0 + 2), inc="ve")
        bank_free[b0 + 2] = ("ve", e1)
        e1 = s.op("vector", "tensor_copy", probsT[:, 12:16, qcols], bankf(b0 + 3), inc="ve")
        bank_free[b0 + 3] = ("ve", e1)
        ev_h[(h, qi)] = (e0, e1)

    def emit_ctx(h):
        hp = h % 2
        e0, e1 = ev_h[(h, 7)]
        s.wait("tensor", f"dh{hp}", lv_h[h])
        s.wait("tensor", "ac", e0)
        s.wait("tensor", "ve", e1)
        for r in range(2):
            cb = 6 + r
            s.wait("tensor", *bank_free[cb])
            for j in range(NTT):
                s.op("tensor", "matmul", bankf(cb),
                     vh[hp][:, j, :], probsT[:, j, r * 512:(r + 1) * 512],
                     start=(j == 0), stop=(j == NTT - 1),
                     inc=("pe" if j == NTT - 1 else None))
            mmc = s.cnt["pe"]
            if r == 1:
                hcons[("v", hp)] = mmc
            s.wait("scalar", "pe", mmc)
            ec = s.op("scalar", "copy", ctxT[:, h, r * 512:(r + 1) * 512], bankf(cb), inc="ac")
            bank_free[cb] = ("ac", ec)

    emit_qk_loads(0)
    emit_v_load(0)
    emit_qk_loads(1)
    emit_v_load(1)
    emit_scores(0, 0)
    for h in range(HEADS):
        for qi in range(NQT):
            if qi < 7:
                emit_scores(h, qi + 1)
            elif h + 1 < HEADS:
                emit_scores(h + 1, 0)
                if h + 2 < HEADS:
                    emit_qk_loads(h + 2)
            emit_chain(h, qi)
            emit_transpose(h, qi)
        emit_ctx(h)
        if h + 2 < HEADS:
            emit_v_load(h + 2)

    if debug == 3:
        dbg["ctxT"] = nc.dram_tensor("d_ctxT", [128, HEADS, TOWN], BF16, kind="ExternalOutput").ap()
        s.barrier()
        dv = s.op("sync", "dma_start", inc="dsy", inck=16, out=dbg["ctxT"], in_=ctxT[:, :, :])
        s.wait("sync", "dsy", dv)
        return nc, s, dbg

    s.barrier()
    esB2.close()

    esC = ExitStack()
    ob_cols = esC.enter_context(nc.sbuf_tensor("ob_cols", [128, 32], F32))
    with nc.sbuf_tensor("brow2", [1, D], F32) as brow2:
        bv2 = s.op("sync", "dma_start", inc="dsy", inck=16, out=brow2[:, :], in_=ob[None, :])
        s.wait("tensor", "dsy", bv2)
        for ctb in range(32):
            s.op("tensor", "matmul", bankf(0)[:, ctb:ctb + 1], brow2[0:1, ctb * 128:(ctb + 1) * 128],
                 id1_f[:, :], is_transpose=True, start=(ctb == 0), stop=(ctb == 31),
                 skip_group_check=True, inc=("pe" if ctb == 31 else None))
        pev = s.cnt["pe"]
        s.wait("scalar", "pe", pev)
        a_ob = s.op("scalar", "copy", ob_cols[:, :], bankf(0)[:, 0:32], inc="ac")
        s.barrier()

    owst = [esC.enter_context(nc.sbuf_tensor(f"owst{i}", [128, NDT, 128], F32)) for i in range(2)]
    owbf = [esC.enter_context(nc.sbuf_tensor(f"owbf{i}", [128, NDT, 128], BF16)) for i in range(2)]
    atst = [esC.enter_context(nc.sbuf_tensor(f"atst{i}", [128, TOWN], F32)) for i in range(2)]
    xcols = [esC.enter_context(nc.sbuf_tensor(f"xcols{i}", [128, NQT, 128], F32)) for i in range(2)]
    o1st = [esC.enter_context(nc.sbuf_tensor(f"o1st{i}", [128, NQT, 128], F32)) for i in range(2)]

    ow_r = ow.rearrange("(ct p) c -> p ct c", p=128)
    x_own_r = x_b[0:TOWN, :].rearrange("(j p) c -> p j c", p=128)
    o1_r = o1_sp.rearrange("(j p) c -> p j c", p=128)

    bank_free = {b: ("ac", 0) for b in range(8)}
    lastconv = {0: 0, 1: 0}
    lastmm = {0: 0, 1: 0}
    last_at = {0: 0, 1: 0}
    last_o1 = {0: 0, 1: 0}
    last_xc = {0: 0, 1: 0}
    for ot in range(NDT):
        p = ot % 2
        s.wait("sync", "ve", lastconv[p])
        dv = s.op("sync", "dma_start", inc=f"dow{p}", inck=16,
                  out=owst[p][:, :, :], in_=ow_r[:, :, ot * 128:(ot + 1) * 128])
        s.wait("sync", "ve", last_xc[p])
        dx = s.op("sync", "dma_start", inc=f"dxc{p}", inck=16,
                  out=xcols[p][:, :, :], in_=x_own_r[:, :, ot * 128:(ot + 1) * 128])
        s.wait("vector", f"dow{p}", dv)
        s.wait("vector", "pe", lastmm[p])
        vc = s.op("vector", "tensor_copy", owbf[p][:, :, :], owst[p][:, :, :], inc="ve")
        lastconv[p] = vc
        s.wait("tensor", "ve", vc)
        evs = []
        for r in range(2):
            bk = (2 * ot + r) % 4
            s.wait("tensor", *bank_free[bk])
            for ct in range(NDT):
                s.op("tensor", "matmul", bankf(bk),
                     owbf[p][:, ct, :], ctxT[:, ct, r * 512:(r + 1) * 512],
                     start=(ct == 0), stop=(ct == NDT - 1),
                     inc=("pe" if ct == NDT - 1 else None))
            mm_v = s.cnt["pe"]
            s.wait("scalar", "pe", mm_v)
            if r == 0:
                s.wait("scalar", "pe", last_at[p])
            ev = s.op("scalar", "activation", atst[p][:, r * 512:(r + 1) * 512], bankf(bk),
                      AF.Identity, bias=ob_cols[:, ot:ot + 1], scale=1.0, inc="ac")
            bank_free[bk] = ("ac", ev)
            evs.append(ev)
        lastmm[p] = s.cnt["pe"]
        tb = 4 + 2 * (ot % 2)
        s.wait("tensor", "ac", evs[-1])
        for j in range(NQT):
            bb = tb + j // 4
            s.wait("tensor", *bank_free[bb]) if j % 4 == 0 else None
            s.op("tensor", "matmul", bankf(bb)[:, (j % 4) * 128:(j % 4 + 1) * 128],
                 atst[p][:, j * 128:(j + 1) * 128], ident_f[:, :],
                 start=(j % 4 == 0), stop=(j % 4 == 3),
                 skip_group_check=True, inc=("pe" if j % 4 == 3 else None))
        tr_v = s.cnt["pe"]
        last_at[p] = tr_v
        s.wait("vector", "pe", tr_v)
        s.wait("vector", f"dxc{p}", dx)
        s.wait("vector", f"dso{p}", last_o1[p])
        s.op("vector", "tensor_tensor", o1st[p][:, 0:4, :], bankf(tb), xcols[p][:, 0:4, :],
             ALU.add, inc="ve")
        v6 = s.op("vector", "tensor_tensor", o1st[p][:, 4:8, :], bankf(tb + 1), xcols[p][:, 4:8, :],
                  ALU.add, inc="ve")
        last_xc[p] = v6
        bank_free[tb] = ("ve", v6)
        bank_free[tb + 1] = ("ve", v6)
        s.wait("scalar", "ve", v6)
        sv = s.op("scalar", "dma_start", inc=f"dso{p}", inck=16,
                  out=o1_r[:, :, ot * 128:(ot + 1) * 128], in_=o1st[p][:, :, :])
        last_o1[p] = sv

    s.barrier()
    if debug == 4:
        return nc, s, dbg
    esC.close()
    esCtx.close()

    esC2 = ExitStack()
    xf2 = [esC2.enter_context(nc.sbuf_tensor(f"xf2{i}", [128, D], F32)) for i in range(2)]
    xsq2 = esC2.enter_context(nc.sbuf_tensor("xsq2", [128, D], F32))
    lnb2 = [esC2.enter_context(nc.sbuf_tensor(f"lnb2{i}", [128, D], BF16)) for i in range(2)]
    lu2 = {}
    for ti in range(NQT):
        emit_ln_tile((xf2, xsq2, lnb2), o1_sp, ti * 128, ti, ln2T,
                     slice(ti * 128, (ti + 1) * 128), lu2)
    s.barrier()
    esC2.close()

    if debug == 45:
        dbg["ln2T"] = nc.dram_tensor("d_ln2T", [128, NDT, TOWN], BF16, kind="ExternalOutput").ap()
        dv = s.op("sync", "dma_start", inc="dsy", inck=16, out=dbg["ln2T"], in_=ln2T[:, :, :])
        s.wait("sync", "dsy", dv)
        return nc, s, dbg

    esD = ExitStack()
    pk1 = [esD.enter_context(nc.sbuf_tensor(f"pk1_{i}", [128, 2048], I32)) for i in range(2)]
    pk3 = [esD.enter_context(nc.sbuf_tensor(f"pk3_{i}", [128, 2048], I32)) for i in range(2)]
    sc1 = [esD.enter_context(nc.sbuf_tensor(f"sc1_{i}", [128, 64], F32)) for i in range(2)]
    sc3 = [esD.enter_context(nc.sbuf_tensor(f"sc3_{i}", [128, 64], F32)) for i in range(2)]
    msb1 = esD.enter_context(nc.sbuf_tensor("msb1", [128, 2048], I32))
    lsb1 = esD.enter_context(nc.sbuf_tensor("lsb1", [128, 2048], I32))
    msb3 = esD.enter_context(nc.sbuf_tensor("msb3", [128, 2048], I32))
    lsb3 = esD.enter_context(nc.sbuf_tensor("lsb3", [128, 2048], I32))
    W1nat = esD.enter_context(nc.sbuf_tensor("W1nat", [128, NDT, 128], BF16))
    W3nat = esD.enter_context(nc.sbuf_tensor("W3nat", [128, NDT, 128], BF16))
    W1T = esD.enter_context(nc.sbuf_tensor("W1T", [128, NDT, 128], BF16))
    W3T = esD.enter_context(nc.sbuf_tensor("W3T", [128, NDT, 128], BF16))
    ast = esD.enter_context(nc.sbuf_tensor("ast", [128, TOWN], BF16))
    bst = esD.enter_context(nc.sbuf_tensor("bst", [128, TOWN], BF16))
    hstage = [esD.enter_context(nc.sbuf_tensor(f"hstage{i}", [128, TOWN], BF16)) for i in range(2)]

    w1v = w1.rearrange("(h s) j -> h (s j)", s=32)
    w3v = w3.rearrange("(h s) j -> h (s j)", s=32)
    s1v = s1.rearrange("(h g) -> h g", g=64)
    s3v = s3.rearrange("(h g) -> h g", g=64)
    ha_r = ha_sp.rearrange("(ht p) t -> p ht t", p=128)

    def emit_dequant(pk, scv, msb, lsb, Wnat, nseg, wait_pk, wait_sc, wait_wnat_free,
                     prev_po=0):
        s.wait("vector", wait_pk, wait_sc)
        s.wait("vector", "ve", prev_po)
        vm = s.op("vector", "tensor_scalar", msb[:, 0:nseg * 64], pk[:, 0:nseg * 64],
                  4, None, ALU.arith_shift_right, inc="ve")
        vl = s.op("vector", "tensor_scalar", lsb[:, 0:nseg * 64], pk[:, 0:nseg * 64],
                  28, 28, ALU.arith_shift_left, ALU.arith_shift_right, inc="ve")
        s.wait("vector", "ve", vl)
        m3 = msb[:, 0:nseg * 64].rearrange("p (s j) -> p s j", j=64)
        l3 = lsb[:, 0:nseg * 64].rearrange("p (s j) -> p s j", j=64)
        sce = scv[:, 0:2 * nseg:2].unsqueeze(2).broadcast_to([128, nseg, 64])
        sco = scv[:, 1:2 * nseg:2].unsqueeze(2).broadcast_to([128, nseg, 64])
        s.wait("vector", "pe", wait_wnat_free)
        po = s.op("vector", "tensor_tensor", Wnat[:, 0:nseg, 0:64], m3, sce, ALU.mult, inc="ve")
        ve = s.op("vector", "tensor_tensor", Wnat[:, 0:nseg, 64:128], l3, sco, ALU.mult, inc="ve")
        return ve, ve

    def emit_wtrans(Wnat, WT, nseg, banks2, ready_ve, ready_po, wt_free):
        s.wait("tensor", "ve", max(ready_ve, ready_po))
        evs = []
        for base in range(0, nseg, 16):
            n = min(16, nseg - base)
            for bi in range(2):
                s.wait("tensor", *bank_free[banks2 + bi])
            for j in range(n):
                bb = banks2 + j // 8
                s.op("tensor", "matmul", bankb(bb)[:, (j % 8) * 128:(j % 8 + 1) * 128],
                     Wnat[:, base + j, :], ident_b[:, :],
                     is_transpose=True, start=(j % 8 == 0),
                     stop=(j % 8 == 7 or j == n - 1),
                     skip_group_check=True,
                     inc=("pe" if (j % 8 == 7 or j == n - 1) else None))
            tr_v = s.cnt["pe"]
            s.wait("scalar", "pe", tr_v)
            s.wait("scalar", "pe", wt_free)
            nb1 = min(8, n)
            e1 = s.op("scalar", "copy", WT[:, base:base + nb1, :], bankb(banks2)[:, 0:nb1 * 128], inc="ac")
            bank_free[banks2] = ("ac", e1)
            if n > 8:
                e2 = s.op("scalar", "copy", WT[:, base + 8:base + n, :],
                          bankb(banks2 + 1)[:, 0:(n - 8) * 128], inc="ac")
                bank_free[banks2 + 1] = ("ac", e2)
                evs.append(e2)
            evs.append(e1)
        return max(evs), s.cnt["pe"]

    bank_free = {b: ("ac", 0) for b in range(8)}
    lastdeq = {"w1": 0, "w3": 0}
    lastwtr = {"w1": 0, "w3": 0}
    lastpo = {}
    lasth = {0: 0, 1: 0}
    lastpk = {}
    lastpo_h = 0
    for ht in range(NHT):
        p = ht % 2
        s.wait("sync", "ve", lastpk.get(("pk1", p), 0))
        s.wait("sync", "ve", lastpo.get("w1", 0))
        d1 = s.op("sync", "dma_start", inc=f"dd1{p}", inck=16,
                  out=pk1[p][:, :], in_=w1v[ht * 128:(ht + 1) * 128, :])
        ds1 = s.op("sync", "dma_start", inc=f"dd1{p}", inck=16,
                   out=sc1[p][:, :], in_=s1v[ht * 128:(ht + 1) * 128, :])
        s.wait("sync", "ve", lastpk.get(("pk3", p), 0))
        s.wait("sync", "ve", lastpo.get("w3", 0))
        d3 = s.op("sync", "dma_start", inc=f"dd3{p}", inck=16,
                  out=pk3[p][:, :], in_=w3v[ht * 128:(ht + 1) * 128, :])
        ds3 = s.op("sync", "dma_start", inc=f"dd3{p}", inck=16,
                   out=sc3[p][:, :], in_=s3v[ht * 128:(ht + 1) * 128, :])
        ve1, po1 = emit_dequant(pk1[p], sc1[p], msb1, lsb1, W1nat, NDT, f"dd1{p}", ds1,
                                lastwtr["w1"], lastpo.get("w1", 0))
        ve3, po3 = emit_dequant(pk3[p], sc3[p], msb3, lsb3, W3nat, NDT, f"dd3{p}", ds3,
                                lastwtr["w3"], lastpo.get("w3", 0))
        lastpo["w1"] = po1
        lastpo["w3"] = po3
        lastpk[("pk1", p)] = ve1
        lastpk[("pk3", p)] = ve3
        wt1, lastwtr["w1"] = emit_wtrans(W1nat, W1T, NDT, 0, ve1, po1, lastdeq["w1"])
        wt3, lastwtr["w3"] = emit_wtrans(W3nat, W3T, NDT, 2, ve3, po3, lastdeq["w3"])
        s.wait("tensor", "ac", max(wt1, wt3))
        for (WT, bb0) in ((W1T, 4), (W3T, 6)):
            for ch in range(2):
                s.wait("tensor", *bank_free[bb0 + ch])
                for dt in range(NDT):
                    s.op("tensor", "matmul", bankf(bb0 + ch),
                         WT[:, dt, :], ln2T[:, dt, ch * 512:(ch + 1) * 512],
                         start=(dt == 0), stop=(dt == NDT - 1),
                         inc=("pe" if dt == NDT - 1 else None))
        mm_v = s.cnt["pe"]
        lastdeq["w1"] = mm_v
        lastdeq["w3"] = mm_v
        s.wait("scalar", "pe", mm_v)
        s.wait("scalar", "po", lastpo_h)
        a1 = s.op("scalar", "activation", ast[:, :], bankf(4, 2), AF.Silu, inc="ac")
        a2 = s.op("scalar", "copy", bst[:, :], bankf(6, 2), inc="ac")
        for bb in (4, 5):
            bank_free[bb] = ("ac", a1)
        for bb in (6, 7):
            bank_free[bb] = ("ac", a2)
        s.wait("gpsimd", "ac", a2)
        s.wait("gpsimd", f"dsh{p}", lasth[p])
        vh_ = s.op("gpsimd", "tensor_tensor", hstage[p][:, :], ast[:, :], bst[:, :], ALU.mult, inc="po")
        lastpo_h = vh_
        s.wait("scalar", "po", vh_)
        sv = s.op("scalar", "dma_start", inc=f"dsh{p}", inck=16,
                  out=ha_sp[ht * 128:(ht + 1) * 128, :], in_=hstage[p][:, :])
        lasth[p] = sv

    s.barrier()
    if debug == 5:
        return nc, s, dbg
    esD.close()
    esLn2.close()

    if debug in (6, 7):
        yA_sp = nc.dram_tensor("d_yA", [TOWN, D], F32, kind="ExternalOutput").ap()
    else:
        yA_sp = nc.dram_tensor("yA_sp", [TOWN, D], F32).ap()
    esE = ExitStack()
    hactH = esE.enter_context(nc.sbuf_tensor("hactH", [128, 43, TOWN], BF16))
    pk2 = [esE.enter_context(nc.sbuf_tensor(f"pk2_{i}", [128, 2752], I32)) for i in range(2)]
    sc2 = [esE.enter_context(nc.sbuf_tensor(f"sc2_{i}", [128, 86], F32)) for i in range(2)]
    msb2 = esE.enter_context(nc.sbuf_tensor("msb2", [128, 2752], I32))
    lsb2 = esE.enter_context(nc.sbuf_tensor("lsb2", [128, 2752], I32))
    W2nat = [esE.enter_context(nc.sbuf_tensor("W2nat", [128, 43, 128], BF16))] * 2
    W2T = [esE.enter_context(nc.sbuf_tensor("W2T", [128, 43, 128], BF16))] * 2
    ystage = [esE.enter_context(nc.sbuf_tensor(f"ystage{i}", [128, TOWN], F32)) for i in range(2)]
    yAc = [esE.enter_context(nc.sbuf_tensor(f"yAc{i}", [128, NQT, 128], F32)) for i in range(2)]
    sum1 = [esE.enter_context(nc.sbuf_tensor(f"sum1_{i}", [128, NQT, 128], F32)) for i in range(2)]
    o1c = [esE.enter_context(nc.sbuf_tensor(f"o1c{i}", [128, NQT, 128], F32)) for i in range(2)]
    outst = [esE.enter_context(nc.sbuf_tensor(f"outst{i}", [128, NQT, 128], F32)) for i in range(2)]

    w2v = w2.rearrange("(o s) j -> o (s j)", s=86)
    yA_r = yA_sp.rearrange("(j p) c -> p j c", p=128)
    s2v = s2.rearrange("(o g) -> o g", g=172)
    y_own_r = y_own.rearrange("(j p) c -> p j c", p=128)

    for half in range(2):
        h0 = half * 43
        s.barrier()
        hl = 0
        for ck in range(4):
            r0 = 11 * ck
            r1 = min(43, 11 * (ck + 1))
            hl = s.op("sync", "dma_start", inc="dha", inck=16,
                      out=hactH[:, r0:r1, :], in_=ha_r[:, h0 + r0:h0 + r1, :])
        s.wait("tensor", "dha", hl)
        lastpk2 = {0: 0, 1: 0}
        lastpo2 = 0
        lastW2 = 0
        lastW2tr = 0
        lastst = {0: 0, 1: 0}
        lastyA = {0: 0, 1: 0}
        lasto1 = {0: 0, 1: 0}
        lastout = {0: 0, 1: 0}
        lastsum = {0: 0, 1: 0}
        eload = {}
        ystv = {}

        def e_loads(ot):
            p = ot % 2
            s.wait("sync", "ve", lastpk2[p])
            s.wait("sync", "ve", lastpo2)
            s.op("sync", "dma_start", inc=f"de{p}", inck=16,
                 out=pk2[p][:, :], in_=w2v[ot * 128:(ot + 1) * 128, half * 2752:(half + 1) * 2752])
            ds2 = s.op("sync", "dma_start", inc=f"de{p}", inck=16,
                       out=sc2[p][:, :], in_=s2v[ot * 128:(ot + 1) * 128, half * 86:(half + 1) * 86])
            do = 0
            if half == 1:
                s.wait("sync", "ve", lastyA[p])
                s.op("sync", "dma_start", inc=f"dyo{p}", inck=16,
                     out=yAc[p][:, :, :], in_=yA_r[:, :, ot * 128:(ot + 1) * 128])
                s.wait("sync", "ve", lasto1[p])
                do = s.op("sync", "dma_start", inc=f"dyo{p}", inck=16,
                          out=o1c[p][:, :, :], in_=o1_r[:, :, ot * 128:(ot + 1) * 128])
            eload[ot] = (ds2, do)

        vs1_h = {}

        def e_finish(ot):
            p = ot % 2
            vs1 = vs1_h.get(ot, 0)
            s.wait("tensor", "ac", ystv[ot])
            for bi in (6, 7):
                s.wait("tensor", *bank_free[bi])
            for j in range(NQT):
                bb = 6 + j // 4
                s.op("tensor", "matmul", bankf(bb)[:, (j % 4) * 128:(j % 4 + 1) * 128],
                     ystage[p][:, j * 128:(j + 1) * 128], ident_f[:, :],
                     start=(j % 4 == 0), stop=(j % 4 == 3),
                     skip_group_check=True, inc=("pe" if j % 4 == 3 else None))
            tr_v = s.cnt["pe"]
            lastst[p] = tr_v
            if half == 0:
                s.wait("scalar", "pe", tr_v)
                s.wait("scalar", f"dya{p}", lastout[p])
                e6 = s.op("scalar", "copy", outst[p][:, 0:4, :], bankf(6), inc="ac")
                e7 = s.op("scalar", "copy", outst[p][:, 4:8, :], bankf(7), inc="ac")
                bank_free[6] = ("ac", e6)
                bank_free[7] = ("ac", e7)
                s.wait("scalar", "ac", e7)
                sv = s.op("scalar", "dma_start", inc=f"dya{p}", inck=16,
                          out=yA_r[:, :, ot * 128:(ot + 1) * 128], in_=outst[p][:, :, :])
                lastout[p] = sv
            else:
                s.wait("vector", "pe", tr_v)
                s.wait("vector", "ve", vs1)
                s.wait("vector", f"dout{p}", lastout[p])
                s.op("vector", "tensor_tensor", outst[p][:, 0:4, :], bankf(6), sum1[p][:, 0:4, :],
                     ALU.add, inc="ve")
                vo = s.op("vector", "tensor_tensor", outst[p][:, 4:8, :], bankf(7), sum1[p][:, 4:8, :],
                          ALU.add, inc="ve")
                lastsum[p] = vo
                for bb in (6, 7):
                    bank_free[bb] = ("ve", vo)
                s.wait("scalar", "ve", vo)
                so = s.op("scalar", "dma_start", inc=f"dout{p}", inck=16,
                          out=y_own_r[:, :, ot * 128:(ot + 1) * 128], in_=outst[p][:, :, :])
                lastout[p] = so

        e_loads(0)
        for ot in range(NDT):
            p = ot % 2
            if ot + 1 < NDT:
                e_loads(ot + 1)
            ve2, po2 = emit_dequant(pk2[p], sc2[p], msb2, lsb2, W2nat[p], 43, f"de{p}", eload[ot][0],
                                    lastW2tr, lastpo2)
            lastpo2 = po2
            lastpk2[p] = ve2
            if half == 1:
                s.wait("vector", f"dyo{p}", eload[ot][1])
                s.wait("vector", "ve", lastsum[p])
                vs1 = s.op("vector", "tensor_tensor", sum1[p][:, :, :], o1c[p][:, :, :],
                           yAc[p][:, :, :], ALU.add, inc="ve")
                lastyA[p] = vs1
                lasto1[p] = vs1
                vs1_h[ot] = vs1
            wt2, lastW2tr = emit_wtrans(W2nat[p], W2T[p], 43, 0, ve2, po2, lastW2)
            if ot >= 1:
                e_finish(ot - 1)
            s.wait("tensor", "ac", wt2)
            for ch in range(2):
                s.wait("tensor", *bank_free[4 + ch])
                for j in range(43):
                    s.op("tensor", "matmul", bankf(4 + ch),
                         W2T[p][:, j, :], hactH[:, j, ch * 512:(ch + 1) * 512],
                         start=(j == 0), stop=(j == 42),
                         inc=("pe" if j == 42 else None))
            mm_v = s.cnt["pe"]
            lastW2 = mm_v
            s.wait("scalar", "pe", mm_v)
            s.wait("scalar", "pe", lastst[p])
            ev = s.op("scalar", "copy", ystage[p][:, :], bankf(4, 2), inc="ac")
            for bb in (4, 5):
                bank_free[bb] = ("ac", ev)
            ystv[ot] = ev
        e_finish(NDT - 1)

    s.barrier()
    return nc, s, dbg


def finalize(nc, s):
    with nc.Block() as block:
        @block.sync
        def _(e):
            for fn in s.st["sync"]:
                fn(e)

        @block.tensor
        def _(e):
            for fn in s.st["tensor"]:
                fn(e)

        @block.vector
        def _(e):
            for fn in s.st["vector"]:
                fn(e)

        @block.scalar
        def _(e):
            for fn in s.st["scalar"]:
                fn(e)

        @block.gpsimd
        def _(e):
            for fn in s.st["gpsimd"]:
                fn(e)
    return nc


def make_in_maps(inputs):
    x = np.ascontiguousarray(inputs["L0_attn_x"], dtype=np.float32)
    qkvw = np.ascontiguousarray(inputs["L0_attn_attn_qkvw"], dtype=np.float32)
    qkvb = np.ascontiguousarray(inputs["L0_attn_attn_qkvb"], dtype=np.float32)
    ow = np.ascontiguousarray(inputs["L0_attn_attn_ow"], dtype=np.float32)
    ob = np.ascontiguousarray(inputs["L0_attn_attn_ob"], dtype=np.float32)
    ws = {k: np.ascontiguousarray(inputs[f"L0_ffn_{k}"]) for k in ("w1", "s1", "w2", "s2", "w3", "s3")}
    in_maps = []
    for c in range(8):
        b, half = c // 2, c % 2
        own = x[b, half * TOWN:(half + 1) * TOWN]
        other = x[b, (1 - half) * TOWN:(2 - half) * TOWN]
        xb = np.concatenate([own, other], axis=0)
        m = {"x_b": xb, "qkvw": qkvw, "qkvb": qkvb, "ow": ow, "ob": ob}
        m.update(ws)
        in_maps.append(m)
    return in_maps


def kernel(**inputs):
    in_maps = make_in_maps(inputs)
    nc, s, dbg = build(debug=0)
    finalize(nc, s)
    res = run_bass_kernel_spmd(nc, in_maps, core_ids=list(range(8)))
    out = np.empty((4, S, D), dtype=np.float32)
    for c in range(8):
        b, half = c // 2, c % 2
        out[b, half * TOWN:(half + 1) * TOWN] = res.results[c]["y_own"]
    return out


# version 08 (final)
# speedup vs baseline: 1.5912x; 1.0346x over previous
import math
from contextlib import ExitStack
import numpy as np
import concourse.bass as bass
import concourse.mybir as mybir
from concourse.bass_utils import run_bass_kernel_spmd

F32 = mybir.dt.float32
BF16 = mybir.dt.bfloat16
I32 = mybir.dt.int32
AF = mybir.ActivationFunctionType
ALU = mybir.AluOpType
AX = mybir.AxisListType

D = 4096
S = 2048
TOWN = 1024
HEADS = 32
HD = 128
HID = 11008
NDT = D // 128
NHT = HID // 128
NTT = S // 128
NQT = TOWN // 128
SCALE = 1.0 / math.sqrt(HD)
EPS = 1e-5

ENGS = ("sync", "tensor", "vector", "scalar", "gpsimd")


class Sched:
    def __init__(self, nc):
        self.nc = nc
        self.st = {e: [] for e in ENGS}
        self.sems = {}
        self.cnt = {}
        self.seen = {e: {} for e in ENGS}

    def sem(self, name):
        self.sems[name] = self.nc.alloc_semaphore(name)
        self.cnt[name] = 0

    def wait(self, eng, name, v):
        if v <= 0 or v <= self.seen[eng].get(name, 0):
            return
        self.seen[eng][name] = v
        sem = self.sems[name]
        self.st[eng].append(lambda e, sem=sem, v=v: e.wait_ge(sem, v))

    def op(self, eng, meth, *args, inc=None, inck=1, **kw):
        sem = self.sems[inc] if inc else None

        def fn(e, meth=meth, args=args, kw=kw, sem=sem, inck=inck):
            ins = getattr(e, meth)(*args, **kw)
            if sem is not None:
                ins.then_inc(sem, inck)
        self.st[eng].append(fn)
        if inc:
            self.cnt[inc] += inck
            return self.cnt[inc]
        return None

    def raw(self, eng, fn):
        self.st[eng].append(fn)

    def barrier(self, engines=ENGS):
        marks = {n: self.cnt[n] for n in self.cnt}
        own = {"sync": "dsy", "tensor": "pe", "vector": "ve",
               "scalar": "ac", "gpsimd": "po"}
        for e in engines:
            for n, v in marks.items():
                if v > 0 and n != own[e]:
                    self.wait(e, n, v)


def ceil_div(a, b):
    return (a + b - 1) // b


def build(debug=0):
    nc = bass.Bass("TRN2", target_bir_lowering=False, debug=False)
    s = Sched(nc)
    for name in ("pe", "ve", "ac", "po", "dsy", "dsc", "dgp",
                 "dxa0", "dxa1", "dwq0", "dwq1", "dsq0", "dsq1", "dsv0", "dsv1",
                 "dh0", "dh1", "dow0", "dow1", "dxc0", "dxc1", "dso0", "dso1",
                 "dd10", "dd11", "dd30", "dd31", "dsh0", "dsh1", "dha",
                 "de0", "de1", "dyo0", "dyo1", "dya0", "dya1", "dout0", "dout1"):
        s.sem(name)

    x_b = nc.dram_tensor("x_b", [S, D], F32, kind="ExternalInput").ap()
    qkvw = nc.dram_tensor("qkvw", [D, 3 * D], F32, kind="ExternalInput").ap()
    qkvb = nc.dram_tensor("qkvb", [3 * D], F32, kind="ExternalInput").ap()
    ow = nc.dram_tensor("ow", [D, D], F32, kind="ExternalInput").ap()
    ob = nc.dram_tensor("ob", [D], F32, kind="ExternalInput").ap()
    w1 = nc.dram_tensor("w1", [352256, 64], I32, kind="ExternalInput").ap()
    s1 = nc.dram_tensor("s1", [704512], F32, kind="ExternalInput").ap()
    w2 = nc.dram_tensor("w2", [352256, 64], I32, kind="ExternalInput").ap()
    s2 = nc.dram_tensor("s2", [704512], F32, kind="ExternalInput").ap()
    w3 = nc.dram_tensor("w3", [352256, 64], I32, kind="ExternalInput").ap()
    s3 = nc.dram_tensor("s3", [704512], F32, kind="ExternalInput").ap()
    y_own = nc.dram_tensor("y_own", [TOWN, D], F32, kind="ExternalOutput").ap()

    def scratch(name, shape, dt, dbg_lvl):
        kind = "ExternalOutput" if (debug == dbg_lvl or debug == 7) else Ellipsis
        if kind is Ellipsis:
            return nc.dram_tensor(name, shape, dt).ap()
        return nc.dram_tensor(name, shape, dt, kind="ExternalOutput").ap()

    q_sp = scratch("d_q", [D, TOWN], BF16, 2)
    k_sp = scratch("d_k", [D, S], BF16, 2)
    v_sp = scratch("d_v", [S, D], BF16, 2)
    o1_sp = scratch("d_out1", [TOWN, D], F32, 4)
    ha_sp = scratch("d_hact", [HID, TOWN], BF16, 5)

    dbg = {}
    if debug in (1, 2):
        dbg["ln1T"] = nc.dram_tensor("d_ln1T", [128, NDT, S], BF16, kind="ExternalOutput").ap()

    ident_b = nc.alloc_sbuf_tensor("ident_b", [128, 128], BF16)
    ident_f = nc.alloc_sbuf_tensor("ident_f", [128, 128], F32)
    iota_f = nc.alloc_sbuf_tensor("iota_f", [128, 128], F32)
    eps_t = nc.alloc_sbuf_tensor("eps_t", [128, 1], F32)
    id1_f = nc.alloc_sbuf_tensor("id1_f", [1, 1], F32)

    psA = nc.alloc_psum_tensor("psA", [128, 4096], F32)

    def bankf(i, n=1):
        return psA[:, i * 512:(i + n) * 512]

    def bankb(i, n=1):
        return psA[:, i * 512:(i + n) * 512].bitcast(BF16)

    s.op("gpsimd", "iota", iota_f[:, :], [[1, 128]], channel_multiplier=-1,
         allow_small_or_imprecise_dtypes=True, inc="po")
    s.op("gpsimd", "memset", eps_t[:, :], EPS, inc="po")
    s.op("gpsimd", "memset", id1_f[:, :], 1.0, inc="po")
    v_iota = s.cnt["po"]
    s.wait("vector", "po", v_iota)
    s.op("vector", "tensor_scalar", ident_b[:, :], iota_f[:, :], 0.0, None, ALU.is_equal, inc="ve")
    s.op("vector", "tensor_scalar", ident_f[:, :], iota_f[:, :], 0.0, None, ALU.is_equal, inc="ve")
    v_ident = s.cnt["ve"]

    st8 = nc.alloc_sbuf_tensor("st8", [128, 8], F32)
    mr_t = nc.alloc_sbuf_tensor("mr_t", [128, 1], F32)
    esLn = ExitStack()
    ln1T = esLn.enter_context(nc.sbuf_tensor("ln1T", [128, NDT, S], BF16))
    esA = ExitStack()
    xf = [esA.enter_context(nc.sbuf_tensor(f"xf{i}", [128, D], F32)) for i in range(2)]
    xsq = esA.enter_context(nc.sbuf_tensor("xsq", [128, D], F32))
    lnb = [esA.enter_context(nc.sbuf_tensor(f"lnb{i}", [128, D], BF16)) for i in range(2)]

    def emit_ln_tile(bufs, src_hbm, row0, ti, lnT_dst, tcols, last_use):
        xf, xsq, lnb = bufs
        p = ti % 2
        s.wait("sync", "ve", last_use.get(("xf", p), 0))
        vdma = s.op("sync", "dma_start", inc=f"dxa{p}", inck=16,
                    out=xf[p][:, :], in_=src_hbm[row0:row0 + 128, :])
        s.wait("vector", f"dxa{p}", vdma)
        v1 = s.op("vector", "tensor_reduce", st8[:, 0:1], xf[p][:, :], AX.X, ALU.add,
                  negate=True, inc="ve")
        s.wait("scalar", f"dxa{p}", vdma)
        s.wait("scalar", "ve", last_use.get("xsq", 0))
        a1 = s.op("scalar", "activation", xsq[:, :], xf[p][:, :], AF.Square, inc="ac")
        s.wait("vector", "ac", a1)
        v2 = s.op("vector", "tensor_reduce", st8[:, 2:3], xsq[:, :], AX.X, ALU.add, inc="ve")
        last_use["xsq"] = v2
        s.wait("vector", "ve", v1)
        s.op("vector", "tensor_scalar", st8[:, 1:2], st8[:, 0:1], 1.0 / D, None, ALU.mult, inc="ve")
        s.wait("vector", "ve", s.cnt["ve"])
        s.op("vector", "tensor_tensor", st8[:, 4:5], st8[:, 1:2], st8[:, 1:2], ALU.mult, inc="ve")
        s.wait("vector", "ve", s.cnt["ve"])
        s.op("vector", "tensor_scalar", st8[:, 3:4], st8[:, 2:3], 1.0 / D, None, ALU.mult, inc="ve")
        s.wait("vector", "ve", s.cnt["ve"])
        v3 = s.op("vector", "tensor_tensor", st8[:, 5:6], st8[:, 3:4], st8[:, 4:5], ALU.subtract, inc="ve")
        s.wait("scalar", "ve", v3)
        a2 = s.op("scalar", "activation", st8[:, 6:7], st8[:, 5:6], AF.Ln, bias=eps_t[:, :], scale=1.0, inc="ac")
        s.wait("vector", "ac", a2)
        v4 = s.op("vector", "tensor_scalar", st8[:, 6:7], st8[:, 6:7], -0.5, None, ALU.mult, inc="ve")
        s.wait("scalar", "ve", v4)
        a3 = s.op("scalar", "activation", st8[:, 7:8], st8[:, 6:7], AF.Exp, inc="ac")
        s.wait("vector", "ac", a3)
        v5 = s.op("vector", "tensor_tensor", mr_t[:, :], st8[:, 1:2], st8[:, 7:8], ALU.mult, inc="ve")
        s.wait("vector", "ve", v5)
        s.wait("vector", "pe", last_use.get(("lnb", p), 0))
        v6 = s.op("vector", "tensor_scalar", lnb[p][:, :], xf[p][:, :],
                  st8[:, 7:8], mr_t[:, :], ALU.mult, ALU.add, inc="ve")
        last_use[("xf", p)] = v6
        s.wait("tensor", "ve", max(v6, v_ident))
        for g in range(4):
            s.wait("tensor", "ac", last_use.get(("Abank", g), 0))
            for j in range(8):
                dt = 8 * g + j
                s.op("tensor", "matmul", bankb(g)[:, j * 128:(j + 1) * 128],
                     lnb[p][:, dt * 128:(dt + 1) * 128], ident_b[:, :],
                     is_transpose=True, start=(j == 0), stop=(j == 7),
                     skip_group_check=True,
                     inc=("pe" if j == 7 else None))
            pev = s.cnt["pe"]
            if g == 3:
                last_use[("lnb", p)] = pev
            s.wait("scalar", "pe", pev)
            av = s.op("scalar", "copy", lnT_dst[:, 8 * g:8 * g + 8, tcols], bankb(g)[:, :], inc="ac")
            last_use[("Abank", g)] = av
        return last_use

    lu = {}
    for ti in range(NTT):
        emit_ln_tile((xf, xsq, lnb), x_b, ti * 128, ti, ln1T,
                     slice(ti * 128, (ti + 1) * 128), lu)

    if debug == 1:
        s.barrier()
        dv = s.op("sync", "dma_start", inc="dsy", inck=16, out=dbg["ln1T"], in_=ln1T[:, :, :])
        s.wait("sync", "dsy", dv)
        return nc, s, dbg

    s.barrier()
    esA.close()

    esQB = ExitStack()
    qb_all = esQB.enter_context(nc.sbuf_tensor("qb_all", [128, 96], F32))
    qb_s = esQB.enter_context(nc.sbuf_tensor("qb_s", [128, 32], F32))
    with nc.sbuf_tensor("brow", [1, 3 * D], F32) as brow:
        bv = s.op("sync", "dma_start", inc="dsy", inck=16, out=brow[:, :], in_=qkvb[None, :])
        s.wait("tensor", "dsy", bv)
        s.wait("tensor", "ve", v_ident)
        for ctb in range(96):
            s.op("tensor", "matmul", bankf(0)[:, ctb:ctb + 1], brow[0:1, ctb * 128:(ctb + 1) * 128],
                 id1_f[:, :], is_transpose=True, start=(ctb == 0), stop=(ctb == 95),
                 skip_group_check=True, inc=("pe" if ctb == 95 else None))
        pev = s.cnt["pe"]
        s.wait("scalar", "pe", pev)
        a_qb = s.op("scalar", "copy", qb_all[:, :], bankf(0)[:, 0:96], inc="ac")
        s.wait("vector", "ac", a_qb)
        v_qb = s.op("vector", "tensor_scalar", qb_s[:, :], qb_all[:, 0:32], SCALE, None, ALU.mult, inc="ve")
        s.barrier()

    esB = ExitStack()
    wstage = [esB.enter_context(nc.sbuf_tensor(f"wstage{i}", [128, NDT, 128], F32)) for i in range(2)]
    wbf = [esB.enter_context(nc.sbuf_tensor(f"wbf{i}", [128, NDT, 128], BF16)) for i in range(2)]
    qkst = [esB.enter_context(nc.sbuf_tensor(f"qkst{i}", [128, S], BF16)) for i in range(2)]
    vtst = [esB.enter_context(nc.sbuf_tensor(f"vtst{i}", [128, NTT, 128], BF16)) for i in range(2)]

    qkvw_r = qkvw.rearrange("(dt p) c -> p dt c", p=128)
    v_sp_r = v_sp.rearrange("(j p) c -> p j c", p=128)
    pair_free = {}
    lastvc = {0: 0, 1: 0}
    lastmm = {0: 0, 1: 0}
    laststage = {0: 0, 1: 0}
    last_vt = {0: 0, 1: 0}
    import os as _os
    _secs = _os.environ.get("B1_SECS", "012")
    for ct in range(96):
        p = ct % 2
        g = ct % 2
        b0 = 4 * g
        sec = ct // 32
        if str(sec) not in _secs:
            continue
        s.wait("sync", "ve", lastvc[p])
        dv = s.op("sync", "dma_start", inc=f"dwq{p}", inck=16,
                  out=wstage[p][:, :, :], in_=qkvw_r[:, :, ct * 128:(ct + 1) * 128])
        s.wait("vector", f"dwq{p}", dv)
        s.wait("vector", "pe", lastmm[p])
        vc = s.op("vector", "tensor_copy", wbf[p][:, :, :], wstage[p][:, :, :], inc="ve")
        lastvc[p] = vc
        s.wait("tensor", "ve", vc)
        ncols = TOWN if sec == 0 else S
        nch = ncols // 512
        nrounds = nch // 2
        ev = 0
        for rnd in range(nrounds):
            pr = b0 + 2 * (rnd % 2)
            key = (g, rnd % 2)
            s.wait("tensor", *pair_free.get(key, ("ac", 0)))
            for ch2 in range(2):
                ch = rnd * 2 + ch2
                for dt in range(NDT):
                    s.op("tensor", "matmul", bankf(pr + ch2),
                         wbf[p][:, dt, :], ln1T[:, dt, ch * 512:(ch + 1) * 512],
                         start=(dt == 0), stop=(dt == NDT - 1),
                         inc=("pe" if (dt == NDT - 1 and ch2 == 1) else None))
            mm_v = s.cnt["pe"]
            s.wait("scalar", "pe", mm_v)
            if rnd == 0:
                s.wait("scalar", f"dsq{p}", laststage[p])
            if sec == 0:
                ev = s.op("scalar", "activation", qkst[p][:, 0:TOWN], bankf(pr, 2), AF.Identity,
                          bias=qb_s[:, ct:ct + 1], scale=SCALE, inc="ac")
            else:
                ev = s.op("scalar", "activation", qkst[p][:, rnd * TOWN:(rnd + 1) * TOWN],
                          bankf(pr, 2), AF.Identity,
                          bias=qb_all[:, ct:ct + 1], scale=1.0, inc="ac")
            pair_free[key] = ("ac", ev)
        lastmm[p] = s.cnt["pe"]
        if sec == 0:
            s.wait("scalar", "ac", ev)
            sv = s.op("scalar", "dma_start", inc=f"dsq{p}", inck=16,
                      out=q_sp[ct * 128:(ct + 1) * 128, :], in_=qkst[p][:, 0:TOWN])
            laststage[p] = sv
        elif sec == 1:
            s.wait("scalar", "ac", ev)
            sv = s.op("scalar", "dma_start", inc=f"dsq{p}", inck=16,
                      out=k_sp[(ct - 32) * 128:(ct - 31) * 128, :], in_=qkst[p][:, :])
            laststage[p] = sv
        else:
            cc = ct - 64
            s.wait("tensor", "ac", ev)
            s.wait("tensor", *pair_free.get((g, 0), ("ac", 0)))
            for j in range(NTT):
                bb = b0 + j // 8
                s.op("tensor", "matmul", bankb(bb)[:, (j % 8) * 128:(j % 8 + 1) * 128],
                     qkst[p][:, j * 128:(j + 1) * 128], ident_b[:, :],
                     is_transpose=True, start=(j % 8 == 0), stop=(j % 8 == 7),
                     skip_group_check=True,
                     inc=("pe" if j % 8 == 7 else None))
        if sec == 2:
            tr_v = s.cnt["pe"]
            s.wait("scalar", "pe", tr_v)
            s.wait("scalar", f"dsv{p}", last_vt[p])
            ev2 = s.op("scalar", "copy", vtst[p][:, :, :], bankb(b0, 2)[:, :], inc="ac")
            pair_free[(g, 0)] = ("ac", ev2)
            s.wait("scalar", "ac", ev2)
            sv = s.op("scalar", "dma_start", inc=f"dsv{p}", inck=16,
                      out=v_sp_r[:, :, cc * 128:(cc + 1) * 128], in_=vtst[p][:, :, :])
            last_vt[p] = sv
        if _os.environ.get("B1_SER"):
            s.barrier()

    if debug == 2:
        s.barrier()
        dv = s.op("sync", "dma_start", inc="dsy", inck=16, out=dbg["ln1T"], in_=ln1T[:, :, :])
        s.wait("sync", "dsy", dv)
        return nc, s, dbg

    s.barrier()
    esB.close()
    esQB.close()
    esLn.close()

    esLn2 = ExitStack()
    ln2T = esLn2.enter_context(nc.sbuf_tensor("ln2T", [128, NDT, TOWN], BF16))
    esCtx = ExitStack()
    ctxT = esCtx.enter_context(nc.sbuf_tensor("ctxT", [128, HEADS, TOWN], BF16))
    esB2 = ExitStack()
    qh = [esB2.enter_context(nc.sbuf_tensor(f"qh{i}", [128, TOWN], BF16)) for i in range(2)]
    kh = [esB2.enter_context(nc.sbuf_tensor(f"kh{i}", [128, S], BF16)) for i in range(2)]
    vh = [esB2.enter_context(nc.sbuf_tensor(f"vh{i}", [128, NTT, 128], BF16)) for i in range(2)]
    probs = [esB2.enter_context(nc.sbuf_tensor(f"probs{i}", [128, S], BF16)) for i in range(2)]
    diag = [esB2.enter_context(nc.sbuf_tensor(f"diag{i}", [128, 128], BF16)) for i in range(2)]
    probsT = esB2.enter_context(nc.sbuf_tensor("probsT", [128, NTT, TOWN], BF16))
    negmax = esB2.enter_context(nc.sbuf_tensor("negmax", [128, 8], F32))
    nm4 = esB2.enter_context(nc.sbuf_tensor("nm4", [128, 8], F32))
    den = esB2.enter_context(nc.sbuf_tensor("den", [128, 8], F32))
    rec = esB2.enter_context(nc.sbuf_tensor("rec", [128, 8], F32))

    bank_free = {b: ("ac", 0) for b in range(8)}
    hcons = {}
    last_exp = {}
    last_rec = {}
    diagmm_p = {0: 0, 1: 0}
    lqk_h = {}
    lv_h = {}
    smm = {}
    a1_h = {}
    vdg_h = {}
    ev_h = {}

    def emit_qk_loads(h):
        hp = h % 2
        s.wait("sync", "pe", hcons.get(("q", hp), 0))
        s.op("sync", "dma_start", inc=f"dh{hp}", inck=16,
             out=qh[hp][:, :], in_=q_sp[h * 128:(h + 1) * 128, :])
        s.wait("sync", "pe", hcons.get(("k", hp), 0))
        lqk_h[h] = s.op("sync", "dma_start", inc=f"dh{hp}", inck=16,
                        out=kh[hp][:, :], in_=k_sp[h * 128:(h + 1) * 128, :])

    def emit_v_load(h):
        hp = h % 2
        s.wait("sync", "pe", hcons.get(("v", hp), 0))
        lv_h[h] = s.op("sync", "dma_start", inc=f"dh{hp}", inck=16,
                       out=vh[hp][:, :, :], in_=v_sp_r[:, :, h * 128:(h + 1) * 128])

    def emit_scores(h, qi):
        hp = h % 2
        b0 = 4 * (qi % 2)
        s.wait("tensor", f"dh{hp}", lqk_h[h])
        ms = {}
        for c in (2, 3, 0, 1):
            s.wait("tensor", *bank_free[b0 + c])
            ms[c] = s.op("tensor", "matmul", bankf(b0 + c),
                         qh[hp][:, qi * 128:(qi + 1) * 128], kh[hp][:, c * 512:(c + 1) * 512],
                         start=True, stop=True, inc="pe")
        smm[(h, qi)] = ms
        if qi == 7:
            hcons[("q", hp)] = ms[1]
            hcons[("k", hp)] = ms[1]

    vcomb_h = {}
    tr_marks = {}

    def emit_chainA(h, qi):
        pp = qi % 2
        b0 = 4 * (qi % 2)
        ms = smm[(h, qi)]
        s.wait("vector", "ac", last_exp.get(qi, 0))
        vm = 0
        for c in (2, 3, 0, 1):
            s.wait("vector", "pe", ms[c])
            vm = s.op("vector", "tensor_reduce", nm4[:, 4 * pp + c:4 * pp + c + 1],
                      bankf(b0 + c), AX.X, ALU.max, negate=True, inc="ve")
        s.wait("vector", "ve", vm)
        vcomb_h[(h, qi)] = s.op("vector", "tensor_reduce", negmax[:, qi:qi + 1],
                                nm4[:, 4 * pp:4 * pp + 4], AX.X, ALU.min, inc="ve")

    def emit_chainB(h, qi):
        pp = qi % 2
        b0 = 4 * (qi % 2)
        s.wait("scalar", "ve", vcomb_h[(h, qi)])
        s.wait("scalar", "ve", last_rec.get(qi, 0))
        s.wait("scalar", "pe", diagmm_p[pp])
        a1 = s.op("scalar", "activation", probs[pp][:, :], bankf(b0, 4), AF.Exp,
                  bias=negmax[:, qi:qi + 1], scale=1.0,
                  accum_out=den[:, qi:qi + 1], inc="ac")
        last_exp[qi] = a1
        a1_h[(h, qi)] = a1
        s.wait("vector", "ac", a1)
        vrec = s.op("vector", "reciprocal", rec[:, qi:qi + 1], den[:, qi:qi + 1], inc="ve")
        last_rec[qi] = vrec
        s.wait("vector", "ve", vrec)
        s.wait("vector", "pe", diagmm_p[pp])
        vdg = s.op("vector", "tensor_scalar", diag[pp][:, :], ident_b[:, :],
                   rec[:, qi:qi + 1], None, ALU.mult, inc="ve")
        vdg_h[(h, qi)] = vdg

    def emit_T(h, qi):
        pp = qi % 2
        b0 = 4 * (qi % 2)
        s.wait("tensor", "ve", vdg_h[(h, qi)])
        s.wait("tensor", "ac", a1_h[(h, qi)])
        marks = []
        for j in range(NTT):
            bb = b0 + j // 4
            s.op("tensor", "matmul", bankf(bb)[:, (j % 4) * 128:(j % 4 + 1) * 128],
                 probs[pp][:, j * 128:(j + 1) * 128], diag[pp][:, :],
                 start=True, stop=True, skip_group_check=True,
                 inc=("pe" if j % 4 == 3 else None))
            if j % 4 == 3:
                marks.append(s.cnt["pe"])
        tr_marks[(h, qi)] = marks
        diagmm_p[pp] = marks[-1]

    def emit_evacs(h, qi):
        b0 = 4 * (qi % 2)
        marks = tr_marks[(h, qi)]
        qcols = slice(qi * 128, (qi + 1) * 128)
        s.wait("scalar", "pe", marks[0])
        e0 = s.op("scalar", "copy", probsT[:, 0:4, qcols], bankf(b0), inc="ac")
        bank_free[b0] = ("ac", e0)
        s.wait("scalar", "pe", marks[1])
        e0 = s.op("scalar", "copy", probsT[:, 4:8, qcols], bankf(b0 + 1), inc="ac")
        bank_free[b0 + 1] = ("ac", e0)
        s.wait("vector", "pe", marks[2])
        e1 = s.op("vector", "tensor_copy", probsT[:, 8:12, qcols], bankf(b0 + 2), inc="ve")
        bank_free[b0 + 2] = ("ve", e1)
        s.wait("vector", "pe", marks[3])
        e1 = s.op("vector", "tensor_copy", probsT[:, 12:16, qcols], bankf(b0 + 3), inc="ve")
        bank_free[b0 + 3] = ("ve", e1)
        ev_h[(h, qi)] = (e0, e1)

    def emit_ctx(h):
        hp = h % 2
        e0, e1 = ev_h[(h, 7)]
        s.wait("tensor", f"dh{hp}", lv_h[h])
        s.wait("tensor", "ac", e0)
        s.wait("tensor", "ve", e1)
        for r in range(2):
            cb = 6 + r
            s.wait("tensor", *bank_free[cb])
            for j in range(NTT):
                s.op("tensor", "matmul", bankf(cb),
                     vh[hp][:, j, :], probsT[:, j, r * 512:(r + 1) * 512],
                     start=(j == 0), stop=(j == NTT - 1),
                     inc=("pe" if j == NTT - 1 else None))
            mmc = s.cnt["pe"]
            if r == 1:
                hcons[("v", hp)] = mmc
            s.wait("scalar", "pe", mmc)
            ec = s.op("scalar", "copy", ctxT[:, h, r * 512:(r + 1) * 512], bankf(cb), inc="ac")
            bank_free[cb] = ("ac", ec)

    stages = [(h, qi) for h in range(HEADS) for qi in range(NQT)]
    emit_qk_loads(0)
    emit_v_load(0)
    emit_qk_loads(1)
    emit_v_load(1)
    emit_scores(0, 0)
    emit_chainA(0, 0)
    for k, (h, qi) in enumerate(stages):
        if k >= 1:
            ph, pq = stages[k - 1]
            emit_evacs(ph, pq)
            if pq == 7:
                emit_ctx(ph)
                if ph + 2 < HEADS:
                    emit_v_load(ph + 2)
        if k + 1 < len(stages):
            nh, nq = stages[k + 1]
            emit_scores(nh, nq)
            if nq == 0 and nh + 1 < HEADS:
                emit_qk_loads(nh + 1)
            emit_chainA(nh, nq)
        emit_chainB(h, qi)
        emit_T(h, qi)
    emit_evacs(HEADS - 1, 7)
    emit_ctx(HEADS - 1)

    if debug == 3:
        dbg["ctxT"] = nc.dram_tensor("d_ctxT", [128, HEADS, TOWN], BF16, kind="ExternalOutput").ap()
        s.barrier()
        dv = s.op("sync", "dma_start", inc="dsy", inck=16, out=dbg["ctxT"], in_=ctxT[:, :, :])
        s.wait("sync", "dsy", dv)
        return nc, s, dbg

    s.barrier()
    esB2.close()

    esC = ExitStack()
    ob_cols = esC.enter_context(nc.sbuf_tensor("ob_cols", [128, 32], F32))
    with nc.sbuf_tensor("brow2", [1, D], F32) as brow2:
        bv2 = s.op("sync", "dma_start", inc="dsy", inck=16, out=brow2[:, :], in_=ob[None, :])
        s.wait("tensor", "dsy", bv2)
        for ctb in range(32):
            s.op("tensor", "matmul", bankf(0)[:, ctb:ctb + 1], brow2[0:1, ctb * 128:(ctb + 1) * 128],
                 id1_f[:, :], is_transpose=True, start=(ctb == 0), stop=(ctb == 31),
                 skip_group_check=True, inc=("pe" if ctb == 31 else None))
        pev = s.cnt["pe"]
        s.wait("scalar", "pe", pev)
        a_ob = s.op("scalar", "copy", ob_cols[:, :], bankf(0)[:, 0:32], inc="ac")
        s.barrier()

    owst = [esC.enter_context(nc.sbuf_tensor(f"owst{i}", [128, NDT, 128], F32)) for i in range(2)]
    owbf = [esC.enter_context(nc.sbuf_tensor(f"owbf{i}", [128, NDT, 128], BF16)) for i in range(2)]
    atst = [esC.enter_context(nc.sbuf_tensor(f"atst{i}", [128, TOWN], F32)) for i in range(2)]
    xcols = [esC.enter_context(nc.sbuf_tensor(f"xcols{i}", [128, NQT, 128], F32)) for i in range(2)]
    o1st = [esC.enter_context(nc.sbuf_tensor(f"o1st{i}", [128, NQT, 128], F32)) for i in range(2)]

    ow_r = ow.rearrange("(ct p) c -> p ct c", p=128)
    x_own_r = x_b[0:TOWN, :].rearrange("(j p) c -> p j c", p=128)
    o1_r = o1_sp.rearrange("(j p) c -> p j c", p=128)

    bank_free = {b: ("ac", 0) for b in range(8)}
    lastconv = {0: 0, 1: 0}
    lastmm = {0: 0, 1: 0}
    last_at = {0: 0, 1: 0}
    last_o1 = {0: 0, 1: 0}
    last_xc = {0: 0, 1: 0}
    for ot in range(NDT):
        p = ot % 2
        s.wait("sync", "ve", lastconv[p])
        dv = s.op("sync", "dma_start", inc=f"dow{p}", inck=16,
                  out=owst[p][:, :, :], in_=ow_r[:, :, ot * 128:(ot + 1) * 128])
        s.wait("sync", "ve", last_xc[p])
        dx = s.op("sync", "dma_start", inc=f"dxc{p}", inck=16,
                  out=xcols[p][:, :, :], in_=x_own_r[:, :, ot * 128:(ot + 1) * 128])
        s.wait("vector", f"dow{p}", dv)
        s.wait("vector", "pe", lastmm[p])
        vc = s.op("vector", "tensor_copy", owbf[p][:, :, :], owst[p][:, :, :], inc="ve")
        lastconv[p] = vc
        s.wait("tensor", "ve", vc)
        evs = []
        for r in range(2):
            bk = (2 * ot + r) % 4
            s.wait("tensor", *bank_free[bk])
            for ct in range(NDT):
                s.op("tensor", "matmul", bankf(bk),
                     owbf[p][:, ct, :], ctxT[:, ct, r * 512:(r + 1) * 512],
                     start=(ct == 0), stop=(ct == NDT - 1),
                     inc=("pe" if ct == NDT - 1 else None))
            mm_v = s.cnt["pe"]
            s.wait("scalar", "pe", mm_v)
            if r == 0:
                s.wait("scalar", "pe", last_at[p])
            ev = s.op("scalar", "activation", atst[p][:, r * 512:(r + 1) * 512], bankf(bk),
                      AF.Identity, bias=ob_cols[:, ot:ot + 1], scale=1.0, inc="ac")
            bank_free[bk] = ("ac", ev)
            evs.append(ev)
        lastmm[p] = s.cnt["pe"]
        tb = 4 + 2 * (ot % 2)
        s.wait("tensor", "ac", evs[-1])
        for j in range(NQT):
            bb = tb + j // 4
            s.wait("tensor", *bank_free[bb]) if j % 4 == 0 else None
            s.op("tensor", "matmul", bankf(bb)[:, (j % 4) * 128:(j % 4 + 1) * 128],
                 atst[p][:, j * 128:(j + 1) * 128], ident_f[:, :],
                 start=(j % 4 == 0), stop=(j % 4 == 3),
                 skip_group_check=True, inc=("pe" if j % 4 == 3 else None))
        tr_v = s.cnt["pe"]
        last_at[p] = tr_v
        s.wait("vector", "pe", tr_v)
        s.wait("vector", f"dxc{p}", dx)
        s.wait("vector", f"dso{p}", last_o1[p])
        s.op("vector", "tensor_tensor", o1st[p][:, 0:4, :], bankf(tb), xcols[p][:, 0:4, :],
             ALU.add, inc="ve")
        v6 = s.op("vector", "tensor_tensor", o1st[p][:, 4:8, :], bankf(tb + 1), xcols[p][:, 4:8, :],
                  ALU.add, inc="ve")
        last_xc[p] = v6
        bank_free[tb] = ("ve", v6)
        bank_free[tb + 1] = ("ve", v6)
        s.wait("scalar", "ve", v6)
        sv = s.op("scalar", "dma_start", inc=f"dso{p}", inck=16,
                  out=o1_r[:, :, ot * 128:(ot + 1) * 128], in_=o1st[p][:, :, :])
        last_o1[p] = sv

    s.barrier()
    if debug == 4:
        return nc, s, dbg
    esC.close()
    esCtx.close()

    esC2 = ExitStack()
    xf2 = [esC2.enter_context(nc.sbuf_tensor(f"xf2{i}", [128, D], F32)) for i in range(2)]
    xsq2 = esC2.enter_context(nc.sbuf_tensor("xsq2", [128, D], F32))
    lnb2 = [esC2.enter_context(nc.sbuf_tensor(f"lnb2{i}", [128, D], BF16)) for i in range(2)]
    lu2 = {}
    for ti in range(NQT):
        emit_ln_tile((xf2, xsq2, lnb2), o1_sp, ti * 128, ti, ln2T,
                     slice(ti * 128, (ti + 1) * 128), lu2)
    s.barrier()
    esC2.close()

    if debug == 45:
        dbg["ln2T"] = nc.dram_tensor("d_ln2T", [128, NDT, TOWN], BF16, kind="ExternalOutput").ap()
        dv = s.op("sync", "dma_start", inc="dsy", inck=16, out=dbg["ln2T"], in_=ln2T[:, :, :])
        s.wait("sync", "dsy", dv)
        return nc, s, dbg

    esD = ExitStack()
    pk1 = [esD.enter_context(nc.sbuf_tensor(f"pk1_{i}", [128, 2048], I32)) for i in range(2)]
    pk3 = [esD.enter_context(nc.sbuf_tensor(f"pk3_{i}", [128, 2048], I32)) for i in range(2)]
    sc1 = [esD.enter_context(nc.sbuf_tensor(f"sc1_{i}", [128, 64], F32)) for i in range(2)]
    sc3 = [esD.enter_context(nc.sbuf_tensor(f"sc3_{i}", [128, 64], F32)) for i in range(2)]
    msb1 = esD.enter_context(nc.sbuf_tensor("msb1", [128, 2048], I32))
    lsb1 = esD.enter_context(nc.sbuf_tensor("lsb1", [128, 2048], I32))
    msb3 = esD.enter_context(nc.sbuf_tensor("msb3", [128, 2048], I32))
    lsb3 = esD.enter_context(nc.sbuf_tensor("lsb3", [128, 2048], I32))
    W1nat = esD.enter_context(nc.sbuf_tensor("W1nat", [128, NDT, 128], BF16))
    W3nat = esD.enter_context(nc.sbuf_tensor("W3nat", [128, NDT, 128], BF16))
    W1T = esD.enter_context(nc.sbuf_tensor("W1T", [128, NDT, 128], BF16))
    W3T = esD.enter_context(nc.sbuf_tensor("W3T", [128, NDT, 128], BF16))
    ast = esD.enter_context(nc.sbuf_tensor("ast", [128, TOWN], BF16))
    bst = esD.enter_context(nc.sbuf_tensor("bst", [128, TOWN], BF16))
    hstage = [esD.enter_context(nc.sbuf_tensor(f"hstage{i}", [128, TOWN], BF16)) for i in range(2)]

    w1v = w1.rearrange("(h s) j -> h (s j)", s=32)
    w3v = w3.rearrange("(h s) j -> h (s j)", s=32)
    s1v = s1.rearrange("(h g) -> h g", g=64)
    s3v = s3.rearrange("(h g) -> h g", g=64)
    ha_r = ha_sp.rearrange("(ht p) t -> p ht t", p=128)

    def emit_dequant(pk, scv, msb, lsb, Wnat, nseg, wait_pk, wait_sc, wait_wnat_free,
                     prev_po=0):
        s.wait("vector", wait_pk, wait_sc)
        s.wait("vector", "ve", prev_po)
        vm = s.op("vector", "tensor_scalar", msb[:, 0:nseg * 64], pk[:, 0:nseg * 64],
                  4, None, ALU.arith_shift_right, inc="ve")
        vl = s.op("vector", "tensor_scalar", lsb[:, 0:nseg * 64], pk[:, 0:nseg * 64],
                  28, 28, ALU.arith_shift_left, ALU.arith_shift_right, inc="ve")
        s.wait("vector", "ve", vl)
        m3 = msb[:, 0:nseg * 64].rearrange("p (s j) -> p s j", j=64)
        l3 = lsb[:, 0:nseg * 64].rearrange("p (s j) -> p s j", j=64)
        sce = scv[:, 0:2 * nseg:2].unsqueeze(2).broadcast_to([128, nseg, 64])
        sco = scv[:, 1:2 * nseg:2].unsqueeze(2).broadcast_to([128, nseg, 64])
        s.wait("vector", "pe", wait_wnat_free)
        po = s.op("vector", "tensor_tensor", Wnat[:, 0:nseg, 0:64], m3, sce, ALU.mult, inc="ve")
        ve = s.op("vector", "tensor_tensor", Wnat[:, 0:nseg, 64:128], l3, sco, ALU.mult, inc="ve")
        return ve, ve

    def emit_wtrans(Wnat, WT, nseg, banks2, ready_ve, ready_po, wt_free):
        s.wait("tensor", "ve", max(ready_ve, ready_po))
        evs = []
        for base in range(0, nseg, 16):
            n = min(16, nseg - base)
            for bi in range(2):
                s.wait("tensor", *bank_free[banks2 + bi])
            for j in range(n):
                bb = banks2 + j // 8
                s.op("tensor", "matmul", bankb(bb)[:, (j % 8) * 128:(j % 8 + 1) * 128],
                     Wnat[:, base + j, :], ident_b[:, :],
                     is_transpose=True, start=(j % 8 == 0),
                     stop=(j % 8 == 7 or j == n - 1),
                     skip_group_check=True,
                     inc=("pe" if (j % 8 == 7 or j == n - 1) else None))
            tr_v = s.cnt["pe"]
            s.wait("scalar", "pe", tr_v)
            s.wait("scalar", "pe", wt_free)
            nb1 = min(8, n)
            e1 = s.op("scalar", "copy", WT[:, base:base + nb1, :], bankb(banks2)[:, 0:nb1 * 128], inc="ac")
            bank_free[banks2] = ("ac", e1)
            if n > 8:
                e2 = s.op("scalar", "copy", WT[:, base + 8:base + n, :],
                          bankb(banks2 + 1)[:, 0:(n - 8) * 128], inc="ac")
                bank_free[banks2 + 1] = ("ac", e2)
                evs.append(e2)
            evs.append(e1)
        return max(evs), s.cnt["pe"]

    bank_free = {b: ("ac", 0) for b in range(8)}
    lastdeq = {"w1": 0, "w3": 0}
    lastwtr = {"w1": 0, "w3": 0}
    lastpo = {}
    lasth = {0: 0, 1: 0}
    lastpk = {}
    lastpo_h = 0
    for ht in range(NHT):
        p = ht % 2
        s.wait("sync", "ve", lastpk.get(("pk1", p), 0))
        s.wait("sync", "ve", lastpo.get("w1", 0))
        d1 = s.op("sync", "dma_start", inc=f"dd1{p}", inck=16,
                  out=pk1[p][:, :], in_=w1v[ht * 128:(ht + 1) * 128, :])
        ds1 = s.op("sync", "dma_start", inc=f"dd1{p}", inck=16,
                   out=sc1[p][:, :], in_=s1v[ht * 128:(ht + 1) * 128, :])
        s.wait("sync", "ve", lastpk.get(("pk3", p), 0))
        s.wait("sync", "ve", lastpo.get("w3", 0))
        d3 = s.op("sync", "dma_start", inc=f"dd3{p}", inck=16,
                  out=pk3[p][:, :], in_=w3v[ht * 128:(ht + 1) * 128, :])
        ds3 = s.op("sync", "dma_start", inc=f"dd3{p}", inck=16,
                   out=sc3[p][:, :], in_=s3v[ht * 128:(ht + 1) * 128, :])
        ve1, po1 = emit_dequant(pk1[p], sc1[p], msb1, lsb1, W1nat, NDT, f"dd1{p}", ds1,
                                lastwtr["w1"], lastpo.get("w1", 0))
        ve3, po3 = emit_dequant(pk3[p], sc3[p], msb3, lsb3, W3nat, NDT, f"dd3{p}", ds3,
                                lastwtr["w3"], lastpo.get("w3", 0))
        lastpo["w1"] = po1
        lastpo["w3"] = po3
        lastpk[("pk1", p)] = ve1
        lastpk[("pk3", p)] = ve3
        wt1, lastwtr["w1"] = emit_wtrans(W1nat, W1T, NDT, 0, ve1, po1, lastdeq["w1"])
        wt3, lastwtr["w3"] = emit_wtrans(W3nat, W3T, NDT, 2, ve3, po3, lastdeq["w3"])
        s.wait("tensor", "ac", max(wt1, wt3))
        for (WT, bb0) in ((W1T, 4), (W3T, 6)):
            for ch in range(2):
                s.wait("tensor", *bank_free[bb0 + ch])
                for dt in range(NDT):
                    s.op("tensor", "matmul", bankf(bb0 + ch),
                         WT[:, dt, :], ln2T[:, dt, ch * 512:(ch + 1) * 512],
                         start=(dt == 0), stop=(dt == NDT - 1),
                         inc=("pe" if dt == NDT - 1 else None))
        mm_v = s.cnt["pe"]
        lastdeq["w1"] = mm_v
        lastdeq["w3"] = mm_v
        s.wait("scalar", "pe", mm_v)
        s.wait("scalar", "po", lastpo_h)
        a1 = s.op("scalar", "activation", ast[:, :], bankf(4, 2), AF.Silu, inc="ac")
        a2 = s.op("scalar", "copy", bst[:, :], bankf(6, 2), inc="ac")
        for bb in (4, 5):
            bank_free[bb] = ("ac", a1)
        for bb in (6, 7):
            bank_free[bb] = ("ac", a2)
        s.wait("gpsimd", "ac", a2)
        s.wait("gpsimd", f"dsh{p}", lasth[p])
        vh_ = s.op("gpsimd", "tensor_tensor", hstage[p][:, :], ast[:, :], bst[:, :], ALU.mult, inc="po")
        lastpo_h = vh_
        s.wait("scalar", "po", vh_)
        sv = s.op("scalar", "dma_start", inc=f"dsh{p}", inck=16,
                  out=ha_sp[ht * 128:(ht + 1) * 128, :], in_=hstage[p][:, :])
        lasth[p] = sv

    s.barrier()
    if debug == 5:
        return nc, s, dbg
    esD.close()
    esLn2.close()

    if debug in (6, 7):
        yA_sp = nc.dram_tensor("d_yA", [TOWN, D], F32, kind="ExternalOutput").ap()
    else:
        yA_sp = nc.dram_tensor("yA_sp", [TOWN, D], F32).ap()
    esE = ExitStack()
    hactH = esE.enter_context(nc.sbuf_tensor("hactH", [128, 43, TOWN], BF16))
    pk2 = [esE.enter_context(nc.sbuf_tensor(f"pk2_{i}", [128, 2752], I32)) for i in range(2)]
    sc2 = [esE.enter_context(nc.sbuf_tensor(f"sc2_{i}", [128, 86], F32)) for i in range(2)]
    msb2 = esE.enter_context(nc.sbuf_tensor("msb2", [128, 2752], I32))
    lsb2 = esE.enter_context(nc.sbuf_tensor("lsb2", [128, 2752], I32))
    W2nat = [esE.enter_context(nc.sbuf_tensor("W2nat", [128, 43, 128], BF16))] * 2
    W2T = [esE.enter_context(nc.sbuf_tensor("W2T", [128, 43, 128], BF16))] * 2
    ystage = [esE.enter_context(nc.sbuf_tensor(f"ystage{i}", [128, TOWN], F32)) for i in range(2)]
    yAc = [esE.enter_context(nc.sbuf_tensor(f"yAc{i}", [128, NQT, 128], F32)) for i in range(2)]
    sum1 = [esE.enter_context(nc.sbuf_tensor(f"sum1_{i}", [128, NQT, 128], F32)) for i in range(2)]
    o1c = [esE.enter_context(nc.sbuf_tensor(f"o1c{i}", [128, NQT, 128], F32)) for i in range(2)]
    outst = [esE.enter_context(nc.sbuf_tensor(f"outst{i}", [128, NQT, 128], F32)) for i in range(2)]

    w2v = w2.rearrange("(o s) j -> o (s j)", s=86)
    yA_r = yA_sp.rearrange("(j p) c -> p j c", p=128)
    s2v = s2.rearrange("(o g) -> o g", g=172)
    y_own_r = y_own.rearrange("(j p) c -> p j c", p=128)

    for half in range(2):
        h0 = half * 43
        s.barrier()
        hl = 0
        for ck in range(4):
            r0 = 11 * ck
            r1 = min(43, 11 * (ck + 1))
            hl = s.op("sync", "dma_start", inc="dha", inck=16,
                      out=hactH[:, r0:r1, :], in_=ha_r[:, h0 + r0:h0 + r1, :])
        s.wait("tensor", "dha", hl)
        lastpk2 = {0: 0, 1: 0}
        lastpo2 = 0
        lastW2 = 0
        lastW2tr = 0
        lastst = {0: 0, 1: 0}
        lastyA = {0: 0, 1: 0}
        lasto1 = {0: 0, 1: 0}
        lastout = {0: 0, 1: 0}
        lastsum = {0: 0, 1: 0}
        eload = {}
        ystv = {}

        def e_loads(ot):
            p = ot % 2
            s.wait("sync", "ve", lastpk2[p])
            s.wait("sync", "ve", lastpo2)
            s.op("sync", "dma_start", inc=f"de{p}", inck=16,
                 out=pk2[p][:, :], in_=w2v[ot * 128:(ot + 1) * 128, half * 2752:(half + 1) * 2752])
            ds2 = s.op("sync", "dma_start", inc=f"de{p}", inck=16,
                       out=sc2[p][:, :], in_=s2v[ot * 128:(ot + 1) * 128, half * 86:(half + 1) * 86])
            do = 0
            if half == 1:
                s.wait("sync", "ve", lastyA[p])
                s.op("sync", "dma_start", inc=f"dyo{p}", inck=16,
                     out=yAc[p][:, :, :], in_=yA_r[:, :, ot * 128:(ot + 1) * 128])
                s.wait("sync", "ve", lasto1[p])
                do = s.op("sync", "dma_start", inc=f"dyo{p}", inck=16,
                          out=o1c[p][:, :, :], in_=o1_r[:, :, ot * 128:(ot + 1) * 128])
            eload[ot] = (ds2, do)

        vs1_h = {}

        def e_finish(ot):
            p = ot % 2
            vs1 = vs1_h.get(ot, 0)
            s.wait("tensor", "ac", ystv[ot])
            for bi in (6, 7):
                s.wait("tensor", *bank_free[bi])
            for j in range(NQT):
                bb = 6 + j // 4
                s.op("tensor", "matmul", bankf(bb)[:, (j % 4) * 128:(j % 4 + 1) * 128],
                     ystage[p][:, j * 128:(j + 1) * 128], ident_f[:, :],
                     start=(j % 4 == 0), stop=(j % 4 == 3),
                     skip_group_check=True, inc=("pe" if j % 4 == 3 else None))
            tr_v = s.cnt["pe"]
            lastst[p] = tr_v
            if half == 0:
                s.wait("scalar", "pe", tr_v)
                s.wait("scalar", f"dya{p}", lastout[p])
                e6 = s.op("scalar", "copy", outst[p][:, 0:4, :], bankf(6), inc="ac")
                e7 = s.op("scalar", "copy", outst[p][:, 4:8, :], bankf(7), inc="ac")
                bank_free[6] = ("ac", e6)
                bank_free[7] = ("ac", e7)
                s.wait("scalar", "ac", e7)
                sv = s.op("scalar", "dma_start", inc=f"dya{p}", inck=16,
                          out=yA_r[:, :, ot * 128:(ot + 1) * 128], in_=outst[p][:, :, :])
                lastout[p] = sv
            else:
                s.wait("vector", "pe", tr_v)
                s.wait("vector", "ve", vs1)
                s.wait("vector", f"dout{p}", lastout[p])
                s.op("vector", "tensor_tensor", outst[p][:, 0:4, :], bankf(6), sum1[p][:, 0:4, :],
                     ALU.add, inc="ve")
                vo = s.op("vector", "tensor_tensor", outst[p][:, 4:8, :], bankf(7), sum1[p][:, 4:8, :],
                          ALU.add, inc="ve")
                lastsum[p] = vo
                for bb in (6, 7):
                    bank_free[bb] = ("ve", vo)
                s.wait("scalar", "ve", vo)
                so = s.op("scalar", "dma_start", inc=f"dout{p}", inck=16,
                          out=y_own_r[:, :, ot * 128:(ot + 1) * 128], in_=outst[p][:, :, :])
                lastout[p] = so

        e_loads(0)
        for ot in range(NDT):
            p = ot % 2
            if ot + 1 < NDT:
                e_loads(ot + 1)
            ve2, po2 = emit_dequant(pk2[p], sc2[p], msb2, lsb2, W2nat[p], 43, f"de{p}", eload[ot][0],
                                    lastW2tr, lastpo2)
            lastpo2 = po2
            lastpk2[p] = ve2
            if half == 1:
                s.wait("vector", f"dyo{p}", eload[ot][1])
                s.wait("vector", "ve", lastsum[p])
                vs1 = s.op("vector", "tensor_tensor", sum1[p][:, :, :], o1c[p][:, :, :],
                           yAc[p][:, :, :], ALU.add, inc="ve")
                lastyA[p] = vs1
                lasto1[p] = vs1
                vs1_h[ot] = vs1
            wt2, lastW2tr = emit_wtrans(W2nat[p], W2T[p], 43, 0, ve2, po2, lastW2)
            if ot >= 1:
                e_finish(ot - 1)
            s.wait("tensor", "ac", wt2)
            for ch in range(2):
                s.wait("tensor", *bank_free[4 + ch])
                for j in range(43):
                    s.op("tensor", "matmul", bankf(4 + ch),
                         W2T[p][:, j, :], hactH[:, j, ch * 512:(ch + 1) * 512],
                         start=(j == 0), stop=(j == 42),
                         inc=("pe" if j == 42 else None))
            mm_v = s.cnt["pe"]
            lastW2 = mm_v
            s.wait("scalar", "pe", mm_v)
            s.wait("scalar", "pe", lastst[p])
            ev = s.op("scalar", "copy", ystage[p][:, :], bankf(4, 2), inc="ac")
            for bb in (4, 5):
                bank_free[bb] = ("ac", ev)
            ystv[ot] = ev
        e_finish(NDT - 1)

    s.barrier()
    return nc, s, dbg


def finalize(nc, s):
    with nc.Block() as block:
        @block.sync
        def _(e):
            for fn in s.st["sync"]:
                fn(e)

        @block.tensor
        def _(e):
            for fn in s.st["tensor"]:
                fn(e)

        @block.vector
        def _(e):
            for fn in s.st["vector"]:
                fn(e)

        @block.scalar
        def _(e):
            for fn in s.st["scalar"]:
                fn(e)

        @block.gpsimd
        def _(e):
            for fn in s.st["gpsimd"]:
                fn(e)
    return nc


def make_in_maps(inputs):
    x = np.ascontiguousarray(inputs["L0_attn_x"], dtype=np.float32)
    qkvw = np.ascontiguousarray(inputs["L0_attn_attn_qkvw"], dtype=np.float32)
    qkvb = np.ascontiguousarray(inputs["L0_attn_attn_qkvb"], dtype=np.float32)
    ow = np.ascontiguousarray(inputs["L0_attn_attn_ow"], dtype=np.float32)
    ob = np.ascontiguousarray(inputs["L0_attn_attn_ob"], dtype=np.float32)
    ws = {k: np.ascontiguousarray(inputs[f"L0_ffn_{k}"]) for k in ("w1", "s1", "w2", "s2", "w3", "s3")}
    in_maps = []
    for c in range(8):
        b, half = c // 2, c % 2
        own = x[b, half * TOWN:(half + 1) * TOWN]
        other = x[b, (1 - half) * TOWN:(2 - half) * TOWN]
        xb = np.concatenate([own, other], axis=0)
        m = {"x_b": xb, "qkvw": qkvw, "qkvb": qkvb, "ow": ow, "ob": ob}
        m.update(ws)
        in_maps.append(m)
    return in_maps


def kernel(**inputs):
    in_maps = make_in_maps(inputs)
    nc, s, dbg = build(debug=0)
    finalize(nc, s)
    res = run_bass_kernel_spmd(nc, in_maps, core_ids=list(range(8)))
    out = np.empty((4, S, D), dtype=np.float32)
    for c in range(8):
        b, half = c // 2, c % 2
        out[b, half * TOWN:(half + 1) * TOWN] = res.results[c]["y_own"]
    return out


# version 09
# speedup vs baseline: 1.7252x; 1.0842x over previous
import math
from contextlib import ExitStack
import numpy as np
import concourse.bass as bass
import concourse.mybir as mybir
from concourse.bass_utils import run_bass_kernel_spmd

F32 = mybir.dt.float32
BF16 = mybir.dt.bfloat16
I32 = mybir.dt.int32
AF = mybir.ActivationFunctionType
ALU = mybir.AluOpType
AX = mybir.AxisListType

D = 4096
S = 2048
TOWN = 1024
HEADS = 32
HD = 128
HID = 11008
NDT = D // 128
NHT = HID // 128
NTT = S // 128
NQT = TOWN // 128
SCALE = 1.0 / math.sqrt(HD)
EPS = 1e-5

ENGS = ("sync", "tensor", "vector", "scalar", "gpsimd")


class Sched:
    def __init__(self, nc):
        self.nc = nc
        self.st = {e: [] for e in ENGS}
        self.sems = {}
        self.cnt = {}
        self.seen = {e: {} for e in ENGS}

    def sem(self, name):
        self.sems[name] = self.nc.alloc_semaphore(name)
        self.cnt[name] = 0

    def wait(self, eng, name, v):
        if v <= 0 or v <= self.seen[eng].get(name, 0):
            return
        self.seen[eng][name] = v
        sem = self.sems[name]
        self.st[eng].append(lambda e, sem=sem, v=v: e.wait_ge(sem, v))

    def op(self, eng, meth, *args, inc=None, inck=1, **kw):
        sem = self.sems[inc] if inc else None

        def fn(e, meth=meth, args=args, kw=kw, sem=sem, inck=inck):
            ins = getattr(e, meth)(*args, **kw)
            if sem is not None:
                ins.then_inc(sem, inck)
        self.st[eng].append(fn)
        if inc:
            self.cnt[inc] += inck
            return self.cnt[inc]
        return None

    def raw(self, eng, fn):
        self.st[eng].append(fn)

    def barrier(self, engines=ENGS):
        marks = {n: self.cnt[n] for n in self.cnt}
        own = {"sync": "dsy", "tensor": "pe", "vector": "ve",
               "scalar": "ac", "gpsimd": "po"}
        for e in engines:
            for n, v in marks.items():
                if v > 0 and n != own[e]:
                    self.wait(e, n, v)


def ceil_div(a, b):
    return (a + b - 1) // b


def build(debug=0):
    nc = bass.Bass("TRN2", target_bir_lowering=False, debug=False)
    s = Sched(nc)
    for name in ("pe", "ve", "ac", "po", "dsy", "dsc", "dgp",
                 "dxa0", "dxa1", "dwq0", "dwq1", "dsq0", "dsq1", "dsv0", "dsv1",
                 "dh0", "dh1", "dow0", "dow1", "dxc0", "dxc1", "dso0", "dso1",
                 "dd10", "dd11", "dd30", "dd31", "dsh0", "dsh1", "dha",
                 "de0", "de1", "dyo0", "dyo1", "dya0", "dya1", "dout0", "dout1"):
        s.sem(name)

    x_b = nc.dram_tensor("x_b", [S, D], F32, kind="ExternalInput").ap()
    qkvw = nc.dram_tensor("qkvw", [D, 3 * D], F32, kind="ExternalInput").ap()
    qkvb = nc.dram_tensor("qkvb", [3 * D], F32, kind="ExternalInput").ap()
    ow = nc.dram_tensor("ow", [D, D], F32, kind="ExternalInput").ap()
    ob = nc.dram_tensor("ob", [D], F32, kind="ExternalInput").ap()
    w1 = nc.dram_tensor("w1", [352256, 64], I32, kind="ExternalInput").ap()
    s1 = nc.dram_tensor("s1", [704512], F32, kind="ExternalInput").ap()
    w2 = nc.dram_tensor("w2", [352256, 64], I32, kind="ExternalInput").ap()
    s2 = nc.dram_tensor("s2", [704512], F32, kind="ExternalInput").ap()
    w3 = nc.dram_tensor("w3", [352256, 64], I32, kind="ExternalInput").ap()
    s3 = nc.dram_tensor("s3", [704512], F32, kind="ExternalInput").ap()
    y_own = nc.dram_tensor("y_own", [TOWN, D], F32, kind="ExternalOutput").ap()

    def scratch(name, shape, dt, dbg_lvl):
        kind = "ExternalOutput" if (debug == dbg_lvl or debug == 7) else Ellipsis
        if kind is Ellipsis:
            return nc.dram_tensor(name, shape, dt).ap()
        return nc.dram_tensor(name, shape, dt, kind="ExternalOutput").ap()

    q_sp = scratch("d_q", [D, TOWN], BF16, 2)
    k_sp = scratch("d_k", [D, S], BF16, 2)
    v_sp = scratch("d_v", [S, D], BF16, 2)
    o1_sp = scratch("d_out1", [TOWN, D], F32, 4)
    ha_sp = scratch("d_hact", [HID, TOWN], BF16, 5)

    dbg = {}
    if debug in (1, 2):
        dbg["ln1T"] = nc.dram_tensor("d_ln1T", [128, NDT, S], BF16, kind="ExternalOutput").ap()

    ident_b = nc.alloc_sbuf_tensor("ident_b", [128, 128], BF16)
    ident_f = nc.alloc_sbuf_tensor("ident_f", [128, 128], F32)
    iota_f = nc.alloc_sbuf_tensor("iota_f", [128, 128], F32)
    eps_t = nc.alloc_sbuf_tensor("eps_t", [128, 1], F32)
    id1_f = nc.alloc_sbuf_tensor("id1_f", [1, 1], F32)

    psA = nc.alloc_psum_tensor("psA", [128, 4096], F32)

    def bankf(i, n=1):
        return psA[:, i * 512:(i + n) * 512]

    def bankb(i, n=1):
        return psA[:, i * 512:(i + n) * 512].bitcast(BF16)

    s.op("gpsimd", "iota", iota_f[:, :], [[1, 128]], channel_multiplier=-1,
         allow_small_or_imprecise_dtypes=True, inc="po")
    s.op("gpsimd", "memset", eps_t[:, :], EPS, inc="po")
    s.op("gpsimd", "memset", id1_f[:, :], 1.0, inc="po")
    v_iota = s.cnt["po"]
    s.wait("vector", "po", v_iota)
    s.op("vector", "tensor_scalar", ident_b[:, :], iota_f[:, :], 0.0, None, ALU.is_equal, inc="ve")
    s.op("vector", "tensor_scalar", ident_f[:, :], iota_f[:, :], 0.0, None, ALU.is_equal, inc="ve")
    v_ident = s.cnt["ve"]

    st8 = nc.alloc_sbuf_tensor("st8", [128, 8], F32)
    mr_t = nc.alloc_sbuf_tensor("mr_t", [128, 1], F32)
    esLn = ExitStack()
    ln1T = esLn.enter_context(nc.sbuf_tensor("ln1T", [128, NDT, S], BF16))
    esA = ExitStack()
    xf = [esA.enter_context(nc.sbuf_tensor(f"xf{i}", [128, D], F32)) for i in range(2)]
    xsq = esA.enter_context(nc.sbuf_tensor("xsq", [128, D], F32))
    lnb = [esA.enter_context(nc.sbuf_tensor(f"lnb{i}", [128, D], BF16)) for i in range(2)]

    def emit_ln_tile(bufs, src_hbm, row0, ti, lnT_dst, tcols, last_use):
        xf, xsq, lnb = bufs
        p = ti % 2
        s.wait("sync", "ve", last_use.get(("xf", p), 0))
        vdma = s.op("sync", "dma_start", inc=f"dxa{p}", inck=16,
                    out=xf[p][:, :], in_=src_hbm[row0:row0 + 128, :])
        s.wait("vector", f"dxa{p}", vdma)
        v1 = s.op("vector", "tensor_reduce", st8[:, 0:1], xf[p][:, :], AX.X, ALU.add,
                  negate=True, inc="ve")
        s.wait("scalar", f"dxa{p}", vdma)
        s.wait("scalar", "ve", last_use.get("xsq", 0))
        a1 = s.op("scalar", "activation", xsq[:, :], xf[p][:, :], AF.Square, inc="ac")
        s.wait("vector", "ac", a1)
        v2 = s.op("vector", "tensor_reduce", st8[:, 2:3], xsq[:, :], AX.X, ALU.add, inc="ve")
        last_use["xsq"] = v2
        s.wait("vector", "ve", v1)
        s.op("vector", "tensor_scalar", st8[:, 1:2], st8[:, 0:1], 1.0 / D, None, ALU.mult, inc="ve")
        s.wait("vector", "ve", s.cnt["ve"])
        s.op("vector", "tensor_tensor", st8[:, 4:5], st8[:, 1:2], st8[:, 1:2], ALU.mult, inc="ve")
        s.wait("vector", "ve", s.cnt["ve"])
        s.op("vector", "tensor_scalar", st8[:, 3:4], st8[:, 2:3], 1.0 / D, None, ALU.mult, inc="ve")
        s.wait("vector", "ve", s.cnt["ve"])
        v3 = s.op("vector", "tensor_tensor", st8[:, 5:6], st8[:, 3:4], st8[:, 4:5], ALU.subtract, inc="ve")
        s.wait("scalar", "ve", v3)
        a2 = s.op("scalar", "activation", st8[:, 6:7], st8[:, 5:6], AF.Ln, bias=eps_t[:, :], scale=1.0, inc="ac")
        s.wait("vector", "ac", a2)
        v4 = s.op("vector", "tensor_scalar", st8[:, 6:7], st8[:, 6:7], -0.5, None, ALU.mult, inc="ve")
        s.wait("scalar", "ve", v4)
        a3 = s.op("scalar", "activation", st8[:, 7:8], st8[:, 6:7], AF.Exp, inc="ac")
        s.wait("vector", "ac", a3)
        v5 = s.op("vector", "tensor_tensor", mr_t[:, :], st8[:, 1:2], st8[:, 7:8], ALU.mult, inc="ve")
        s.wait("vector", "ve", v5)
        s.wait("vector", "pe", last_use.get(("lnb", p), 0))
        v6 = s.op("vector", "tensor_scalar", lnb[p][:, :], xf[p][:, :],
                  st8[:, 7:8], mr_t[:, :], ALU.mult, ALU.add, inc="ve")
        last_use[("xf", p)] = v6
        s.wait("tensor", "ve", max(v6, v_ident))
        for g in range(4):
            s.wait("tensor", "ac", last_use.get(("Abank", g), 0))
            for j in range(8):
                dt = 8 * g + j
                s.op("tensor", "matmul", bankb(g)[:, j * 128:(j + 1) * 128],
                     lnb[p][:, dt * 128:(dt + 1) * 128], ident_b[:, :],
                     is_transpose=True, start=(j == 0), stop=(j == 7),
                     skip_group_check=True,
                     inc=("pe" if j == 7 else None))
            pev = s.cnt["pe"]
            if g == 3:
                last_use[("lnb", p)] = pev
            s.wait("scalar", "pe", pev)
            av = s.op("scalar", "copy", lnT_dst[:, 8 * g:8 * g + 8, tcols], bankb(g)[:, :], inc="ac")
            last_use[("Abank", g)] = av
        return last_use

    lu = {}
    for ti in range(NTT):
        emit_ln_tile((xf, xsq, lnb), x_b, ti * 128, ti, ln1T,
                     slice(ti * 128, (ti + 1) * 128), lu)

    if debug == 1:
        s.barrier()
        dv = s.op("sync", "dma_start", inc="dsy", inck=16, out=dbg["ln1T"], in_=ln1T[:, :, :])
        s.wait("sync", "dsy", dv)
        return nc, s, dbg

    s.barrier()
    esA.close()

    esQB = ExitStack()
    qb_all = esQB.enter_context(nc.sbuf_tensor("qb_all", [128, 96], F32))
    qb_s = esQB.enter_context(nc.sbuf_tensor("qb_s", [128, 32], F32))
    with nc.sbuf_tensor("brow", [1, 3 * D], F32) as brow:
        bv = s.op("sync", "dma_start", inc="dsy", inck=16, out=brow[:, :], in_=qkvb[None, :])
        s.wait("tensor", "dsy", bv)
        s.wait("tensor", "ve", v_ident)
        for ctb in range(96):
            s.op("tensor", "matmul", bankf(0)[:, ctb:ctb + 1], brow[0:1, ctb * 128:(ctb + 1) * 128],
                 id1_f[:, :], is_transpose=True, start=(ctb == 0), stop=(ctb == 95),
                 skip_group_check=True, inc=("pe" if ctb == 95 else None))
        pev = s.cnt["pe"]
        s.wait("scalar", "pe", pev)
        a_qb = s.op("scalar", "copy", qb_all[:, :], bankf(0)[:, 0:96], inc="ac")
        s.wait("vector", "ac", a_qb)
        v_qb = s.op("vector", "tensor_scalar", qb_s[:, :], qb_all[:, 0:32], SCALE, None, ALU.mult, inc="ve")
        s.barrier()

    esB = ExitStack()
    wstage = [esB.enter_context(nc.sbuf_tensor(f"wstage{i}", [128, NDT, 128], F32)) for i in range(2)]
    wbf = [esB.enter_context(nc.sbuf_tensor(f"wbf{i}", [128, NDT, 128], BF16)) for i in range(2)]
    qkst = [esB.enter_context(nc.sbuf_tensor(f"qkst{i}", [128, S], BF16)) for i in range(2)]
    vtst = [esB.enter_context(nc.sbuf_tensor(f"vtst{i}", [128, NTT, 128], BF16)) for i in range(2)]

    qkvw_r = qkvw.rearrange("(dt p) c -> p dt c", p=128)
    v_sp_r = v_sp.rearrange("(j p) c -> p j c", p=128)
    pair_free = {}
    lastvc = {0: 0, 1: 0}
    lastmm = {0: 0, 1: 0}
    laststage = {0: 0, 1: 0}
    last_vt = {0: 0, 1: 0}
    import os as _os
    _secs = _os.environ.get("B1_SECS", "012")
    for ct in range(96):
        p = ct % 2
        g = ct % 2
        b0 = 4 * g
        sec = ct // 32
        if str(sec) not in _secs:
            continue
        s.wait("sync", "ve", lastvc[p])
        dv = s.op("sync", "dma_start", inc=f"dwq{p}", inck=16,
                  out=wstage[p][:, :, :], in_=qkvw_r[:, :, ct * 128:(ct + 1) * 128])
        s.wait("vector", f"dwq{p}", dv)
        s.wait("vector", "pe", lastmm[p])
        vc = s.op("vector", "tensor_copy", wbf[p][:, :, :], wstage[p][:, :, :], inc="ve")
        lastvc[p] = vc
        s.wait("tensor", "ve", vc)
        ncols = TOWN if sec == 0 else S
        nch = ncols // 512
        nrounds = nch // 2
        ev = 0
        for rnd in range(nrounds):
            pr = b0 + 2 * (rnd % 2)
            key = (g, rnd % 2)
            s.wait("tensor", *pair_free.get(key, ("ac", 0)))
            for ch2 in range(2):
                ch = rnd * 2 + ch2
                for dt in range(NDT):
                    s.op("tensor", "matmul", bankf(pr + ch2),
                         wbf[p][:, dt, :], ln1T[:, dt, ch * 512:(ch + 1) * 512],
                         start=(dt == 0), stop=(dt == NDT - 1),
                         inc=("pe" if (dt == NDT - 1 and ch2 == 1) else None))
            mm_v = s.cnt["pe"]
            s.wait("scalar", "pe", mm_v)
            if rnd == 0:
                s.wait("scalar", f"dsq{p}", laststage[p])
            if sec == 0:
                ev = s.op("scalar", "activation", qkst[p][:, 0:TOWN], bankf(pr, 2), AF.Identity,
                          bias=qb_s[:, ct:ct + 1], scale=SCALE, inc="ac")
            else:
                ev = s.op("scalar", "activation", qkst[p][:, rnd * TOWN:(rnd + 1) * TOWN],
                          bankf(pr, 2), AF.Identity,
                          bias=qb_all[:, ct:ct + 1], scale=1.0, inc="ac")
            pair_free[key] = ("ac", ev)
        lastmm[p] = s.cnt["pe"]
        if sec == 0:
            s.wait("scalar", "ac", ev)
            sv = s.op("scalar", "dma_start", inc=f"dsq{p}", inck=16,
                      out=q_sp[ct * 128:(ct + 1) * 128, :], in_=qkst[p][:, 0:TOWN])
            laststage[p] = sv
        elif sec == 1:
            s.wait("scalar", "ac", ev)
            sv = s.op("scalar", "dma_start", inc=f"dsq{p}", inck=16,
                      out=k_sp[(ct - 32) * 128:(ct - 31) * 128, :], in_=qkst[p][:, :])
            laststage[p] = sv
        else:
            cc = ct - 64
            s.wait("tensor", "ac", ev)
            s.wait("tensor", *pair_free.get((g, 0), ("ac", 0)))
            for j in range(NTT):
                bb = b0 + j // 8
                s.op("tensor", "matmul", bankb(bb)[:, (j % 8) * 128:(j % 8 + 1) * 128],
                     qkst[p][:, j * 128:(j + 1) * 128], ident_b[:, :],
                     is_transpose=True, start=(j % 8 == 0), stop=(j % 8 == 7),
                     skip_group_check=True,
                     inc=("pe" if j % 8 == 7 else None))
        if sec == 2:
            tr_v = s.cnt["pe"]
            s.wait("scalar", "pe", tr_v)
            s.wait("scalar", f"dsv{p}", last_vt[p])
            ev2 = s.op("scalar", "copy", vtst[p][:, :, :], bankb(b0, 2)[:, :], inc="ac")
            pair_free[(g, 0)] = ("ac", ev2)
            s.wait("scalar", "ac", ev2)
            sv = s.op("scalar", "dma_start", inc=f"dsv{p}", inck=16,
                      out=v_sp_r[:, :, cc * 128:(cc + 1) * 128], in_=vtst[p][:, :, :])
            last_vt[p] = sv
        if _os.environ.get("B1_SER"):
            s.barrier()

    if debug == 2:
        s.barrier()
        dv = s.op("sync", "dma_start", inc="dsy", inck=16, out=dbg["ln1T"], in_=ln1T[:, :, :])
        s.wait("sync", "dsy", dv)
        return nc, s, dbg

    s.barrier()
    esB.close()
    esQB.close()
    esLn.close()

    esLn2 = ExitStack()
    ln2T = esLn2.enter_context(nc.sbuf_tensor("ln2T", [128, NDT, TOWN], BF16))
    esCtx = ExitStack()
    ctxT = esCtx.enter_context(nc.sbuf_tensor("ctxT", [128, HEADS, TOWN], BF16))
    esB2 = ExitStack()
    qh = [esB2.enter_context(nc.sbuf_tensor(f"qh{i}", [128, TOWN], BF16)) for i in range(2)]
    kh = [esB2.enter_context(nc.sbuf_tensor(f"kh{i}", [128, S], BF16)) for i in range(2)]
    vh = [esB2.enter_context(nc.sbuf_tensor(f"vh{i}", [128, NTT, 128], BF16)) for i in range(2)]
    probs = [esB2.enter_context(nc.sbuf_tensor(f"probs{i}", [128, S], BF16)) for i in range(2)]
    diag = [esB2.enter_context(nc.sbuf_tensor(f"diag{i}", [128, 128], BF16)) for i in range(2)]
    probsT = esB2.enter_context(nc.sbuf_tensor("probsT", [128, NTT, TOWN], BF16))
    negmax = esB2.enter_context(nc.sbuf_tensor("negmax", [128, 8], F32))
    nm4 = esB2.enter_context(nc.sbuf_tensor("nm4", [128, 8], F32))
    den = esB2.enter_context(nc.sbuf_tensor("den", [128, 8], F32))
    rec = esB2.enter_context(nc.sbuf_tensor("rec", [128, 8], F32))

    bank_free = {b: ("ac", 0) for b in range(8)}
    hcons = {}
    last_exp = {}
    last_rec = {}
    diagmm_p = {0: 0, 1: 0}
    lqk_h = {}
    lv_h = {}
    smm = {}
    a1_h = {}
    vdg_h = {}
    ev_h = {}

    def emit_qk_loads(h):
        hp = h % 2
        s.wait("sync", "pe", hcons.get(("q", hp), 0))
        s.op("sync", "dma_start", inc=f"dh{hp}", inck=16,
             out=qh[hp][:, :], in_=q_sp[h * 128:(h + 1) * 128, :])
        s.wait("sync", "pe", hcons.get(("k", hp), 0))
        lqk_h[h] = s.op("sync", "dma_start", inc=f"dh{hp}", inck=16,
                        out=kh[hp][:, :], in_=k_sp[h * 128:(h + 1) * 128, :])

    def emit_v_load(h):
        hp = h % 2
        s.wait("sync", "pe", hcons.get(("v", hp), 0))
        lv_h[h] = s.op("sync", "dma_start", inc=f"dh{hp}", inck=16,
                       out=vh[hp][:, :, :], in_=v_sp_r[:, :, h * 128:(h + 1) * 128])

    def emit_scores(h, qi):
        hp = h % 2
        b0 = 4 * (qi % 2)
        s.wait("tensor", f"dh{hp}", lqk_h[h])
        ms = {}
        for c in (2, 3, 0, 1):
            s.wait("tensor", *bank_free[b0 + c])
            ms[c] = s.op("tensor", "matmul", bankf(b0 + c),
                         qh[hp][:, qi * 128:(qi + 1) * 128], kh[hp][:, c * 512:(c + 1) * 512],
                         start=True, stop=True, inc="pe")
        smm[(h, qi)] = ms
        if qi == 7:
            hcons[("q", hp)] = ms[1]
            hcons[("k", hp)] = ms[1]

    vcomb_h = {}
    tr_marks = {}

    def emit_chainA(h, qi):
        pp = qi % 2
        b0 = 4 * (qi % 2)
        ms = smm[(h, qi)]
        s.wait("vector", "ac", last_exp.get(qi, 0))
        vm = 0
        for c in (2, 3, 0, 1):
            s.wait("vector", "pe", ms[c])
            vm = s.op("vector", "tensor_reduce", nm4[:, 4 * pp + c:4 * pp + c + 1],
                      bankf(b0 + c), AX.X, ALU.max, negate=True, inc="ve")
        s.wait("vector", "ve", vm)
        vcomb_h[(h, qi)] = s.op("vector", "tensor_reduce", negmax[:, qi:qi + 1],
                                nm4[:, 4 * pp:4 * pp + 4], AX.X, ALU.min, inc="ve")

    def emit_chainB(h, qi):
        pp = qi % 2
        b0 = 4 * (qi % 2)
        s.wait("scalar", "ve", vcomb_h[(h, qi)])
        s.wait("scalar", "ve", last_rec.get(qi, 0))
        s.wait("scalar", "pe", diagmm_p[pp])
        a1 = s.op("scalar", "activation", probs[pp][:, :], bankf(b0, 4), AF.Exp,
                  bias=negmax[:, qi:qi + 1], scale=1.0,
                  accum_out=den[:, qi:qi + 1], inc="ac")
        last_exp[qi] = a1
        a1_h[(h, qi)] = a1
        s.wait("vector", "ac", a1)
        vrec = s.op("vector", "reciprocal", rec[:, qi:qi + 1], den[:, qi:qi + 1], inc="ve")
        last_rec[qi] = vrec
        s.wait("vector", "ve", vrec)
        s.wait("vector", "pe", diagmm_p[pp])
        vdg = s.op("vector", "tensor_scalar", diag[pp][:, :], ident_b[:, :],
                   rec[:, qi:qi + 1], None, ALU.mult, inc="ve")
        vdg_h[(h, qi)] = vdg

    def emit_T(h, qi):
        pp = qi % 2
        b0 = 4 * (qi % 2)
        s.wait("tensor", "ve", vdg_h[(h, qi)])
        s.wait("tensor", "ac", a1_h[(h, qi)])
        marks = []
        for j in range(NTT):
            bb = b0 + j // 4
            s.op("tensor", "matmul", bankf(bb)[:, (j % 4) * 128:(j % 4 + 1) * 128],
                 probs[pp][:, j * 128:(j + 1) * 128], diag[pp][:, :],
                 start=True, stop=True, skip_group_check=True,
                 inc=("pe" if j % 4 == 3 else None))
            if j % 4 == 3:
                marks.append(s.cnt["pe"])
        tr_marks[(h, qi)] = marks
        diagmm_p[pp] = marks[-1]

    def emit_evacs(h, qi):
        b0 = 4 * (qi % 2)
        marks = tr_marks[(h, qi)]
        qcols = slice(qi * 128, (qi + 1) * 128)
        s.wait("scalar", "pe", marks[0])
        e0 = s.op("scalar", "copy", probsT[:, 0:4, qcols], bankf(b0), inc="ac")
        bank_free[b0] = ("ac", e0)
        s.wait("scalar", "pe", marks[1])
        e0 = s.op("scalar", "copy", probsT[:, 4:8, qcols], bankf(b0 + 1), inc="ac")
        bank_free[b0 + 1] = ("ac", e0)
        s.wait("vector", "pe", marks[2])
        e1 = s.op("vector", "tensor_copy", probsT[:, 8:12, qcols], bankf(b0 + 2), inc="ve")
        bank_free[b0 + 2] = ("ve", e1)
        s.wait("vector", "pe", marks[3])
        e1 = s.op("vector", "tensor_copy", probsT[:, 12:16, qcols], bankf(b0 + 3), inc="ve")
        bank_free[b0 + 3] = ("ve", e1)
        ev_h[(h, qi)] = (e0, e1)

    def emit_ctx(h):
        hp = h % 2
        e0, e1 = ev_h[(h, 7)]
        s.wait("tensor", f"dh{hp}", lv_h[h])
        s.wait("tensor", "ac", e0)
        s.wait("tensor", "ve", e1)
        for r in range(2):
            cb = 6 + r
            s.wait("tensor", *bank_free[cb])
            for j in range(NTT):
                s.op("tensor", "matmul", bankf(cb),
                     vh[hp][:, j, :], probsT[:, j, r * 512:(r + 1) * 512],
                     start=(j == 0), stop=(j == NTT - 1),
                     inc=("pe" if j == NTT - 1 else None))
            mmc = s.cnt["pe"]
            if r == 1:
                hcons[("v", hp)] = mmc
            s.wait("scalar", "pe", mmc)
            ec = s.op("scalar", "copy", ctxT[:, h, r * 512:(r + 1) * 512], bankf(cb), inc="ac")
            bank_free[cb] = ("ac", ec)

    stages = [(h, qi) for h in range(HEADS) for qi in range(NQT)]
    emit_qk_loads(0)
    emit_v_load(0)
    emit_qk_loads(1)
    emit_v_load(1)
    emit_scores(0, 0)
    emit_chainA(0, 0)
    for k, (h, qi) in enumerate(stages):
        if k >= 1:
            ph, pq = stages[k - 1]
            emit_evacs(ph, pq)
            if pq == 7:
                emit_ctx(ph)
                if ph + 2 < HEADS:
                    emit_v_load(ph + 2)
        if k + 1 < len(stages):
            nh, nq = stages[k + 1]
            emit_scores(nh, nq)
            if nq == 0 and nh + 1 < HEADS:
                emit_qk_loads(nh + 1)
            emit_chainA(nh, nq)
        emit_chainB(h, qi)
        emit_T(h, qi)
    emit_evacs(HEADS - 1, 7)
    emit_ctx(HEADS - 1)

    if debug == 3:
        dbg["ctxT"] = nc.dram_tensor("d_ctxT", [128, HEADS, TOWN], BF16, kind="ExternalOutput").ap()
        s.barrier()
        dv = s.op("sync", "dma_start", inc="dsy", inck=16, out=dbg["ctxT"], in_=ctxT[:, :, :])
        s.wait("sync", "dsy", dv)
        return nc, s, dbg

    s.barrier()
    esB2.close()

    esC = ExitStack()
    ob_cols = esC.enter_context(nc.sbuf_tensor("ob_cols", [128, 32], F32))
    with nc.sbuf_tensor("brow2", [1, D], F32) as brow2:
        bv2 = s.op("sync", "dma_start", inc="dsy", inck=16, out=brow2[:, :], in_=ob[None, :])
        s.wait("tensor", "dsy", bv2)
        for ctb in range(32):
            s.op("tensor", "matmul", bankf(0)[:, ctb:ctb + 1], brow2[0:1, ctb * 128:(ctb + 1) * 128],
                 id1_f[:, :], is_transpose=True, start=(ctb == 0), stop=(ctb == 31),
                 skip_group_check=True, inc=("pe" if ctb == 31 else None))
        pev = s.cnt["pe"]
        s.wait("scalar", "pe", pev)
        a_ob = s.op("scalar", "copy", ob_cols[:, :], bankf(0)[:, 0:32], inc="ac")
        s.barrier()

    owst = [esC.enter_context(nc.sbuf_tensor(f"owst{i}", [128, NDT, 128], F32)) for i in range(2)]
    owbf = [esC.enter_context(nc.sbuf_tensor(f"owbf{i}", [128, NDT, 128], BF16)) for i in range(2)]
    atst = [esC.enter_context(nc.sbuf_tensor(f"atst{i}", [128, TOWN], F32)) for i in range(2)]
    xcols = [esC.enter_context(nc.sbuf_tensor(f"xcols{i}", [128, NQT, 128], F32)) for i in range(2)]
    o1st = [esC.enter_context(nc.sbuf_tensor(f"o1st{i}", [128, NQT, 128], F32)) for i in range(2)]

    ow_r = ow.rearrange("(ct p) c -> p ct c", p=128)
    x_own_r = x_b[0:TOWN, :].rearrange("(j p) c -> p j c", p=128)
    o1_r = o1_sp.rearrange("(j p) c -> p j c", p=128)

    bank_free = {b: ("ac", 0) for b in range(8)}
    lastconv = {0: 0, 1: 0}
    lastmm = {0: 0, 1: 0}
    last_at = {0: 0, 1: 0}
    last_o1 = {0: 0, 1: 0}
    last_xc = {0: 0, 1: 0}
    for ot in range(NDT):
        p = ot % 2
        s.wait("sync", "ve", lastconv[p])
        dv = s.op("sync", "dma_start", inc=f"dow{p}", inck=16,
                  out=owst[p][:, :, :], in_=ow_r[:, :, ot * 128:(ot + 1) * 128])
        s.wait("sync", "ve", last_xc[p])
        dx = s.op("sync", "dma_start", inc=f"dxc{p}", inck=16,
                  out=xcols[p][:, :, :], in_=x_own_r[:, :, ot * 128:(ot + 1) * 128])
        s.wait("vector", f"dow{p}", dv)
        s.wait("vector", "pe", lastmm[p])
        vc = s.op("vector", "tensor_copy", owbf[p][:, :, :], owst[p][:, :, :], inc="ve")
        lastconv[p] = vc
        s.wait("tensor", "ve", vc)
        evs = []
        for r in range(2):
            bk = (2 * ot + r) % 4
            s.wait("tensor", *bank_free[bk])
            for ct in range(NDT):
                s.op("tensor", "matmul", bankf(bk),
                     owbf[p][:, ct, :], ctxT[:, ct, r * 512:(r + 1) * 512],
                     start=(ct == 0), stop=(ct == NDT - 1),
                     inc=("pe" if ct == NDT - 1 else None))
            mm_v = s.cnt["pe"]
            s.wait("scalar", "pe", mm_v)
            if r == 0:
                s.wait("scalar", "pe", last_at[p])
            ev = s.op("scalar", "activation", atst[p][:, r * 512:(r + 1) * 512], bankf(bk),
                      AF.Identity, bias=ob_cols[:, ot:ot + 1], scale=1.0, inc="ac")
            bank_free[bk] = ("ac", ev)
            evs.append(ev)
        lastmm[p] = s.cnt["pe"]
        tb = 4 + 2 * (ot % 2)
        s.wait("tensor", "ac", evs[-1])
        for j in range(NQT):
            bb = tb + j // 4
            s.wait("tensor", *bank_free[bb]) if j % 4 == 0 else None
            s.op("tensor", "matmul", bankf(bb)[:, (j % 4) * 128:(j % 4 + 1) * 128],
                 atst[p][:, j * 128:(j + 1) * 128], ident_f[:, :],
                 start=(j % 4 == 0), stop=(j % 4 == 3),
                 skip_group_check=True, inc=("pe" if j % 4 == 3 else None))
        tr_v = s.cnt["pe"]
        last_at[p] = tr_v
        s.wait("vector", "pe", tr_v)
        s.wait("vector", f"dxc{p}", dx)
        s.wait("vector", f"dso{p}", last_o1[p])
        s.op("vector", "tensor_tensor", o1st[p][:, 0:4, :], bankf(tb), xcols[p][:, 0:4, :],
             ALU.add, inc="ve")
        v6 = s.op("vector", "tensor_tensor", o1st[p][:, 4:8, :], bankf(tb + 1), xcols[p][:, 4:8, :],
                  ALU.add, inc="ve")
        last_xc[p] = v6
        bank_free[tb] = ("ve", v6)
        bank_free[tb + 1] = ("ve", v6)
        s.wait("scalar", "ve", v6)
        sv = s.op("scalar", "dma_start", inc=f"dso{p}", inck=16,
                  out=o1_r[:, :, ot * 128:(ot + 1) * 128], in_=o1st[p][:, :, :])
        last_o1[p] = sv

    s.barrier()
    if debug == 4:
        return nc, s, dbg
    esC.close()
    esCtx.close()

    esC2 = ExitStack()
    xf2 = [esC2.enter_context(nc.sbuf_tensor(f"xf2{i}", [128, D], F32)) for i in range(2)]
    xsq2 = esC2.enter_context(nc.sbuf_tensor("xsq2", [128, D], F32))
    lnb2 = [esC2.enter_context(nc.sbuf_tensor(f"lnb2{i}", [128, D], BF16)) for i in range(2)]
    lu2 = {}
    for ti in range(NQT):
        emit_ln_tile((xf2, xsq2, lnb2), o1_sp, ti * 128, ti, ln2T,
                     slice(ti * 128, (ti + 1) * 128), lu2)
    s.barrier()
    esC2.close()

    if debug == 45:
        dbg["ln2T"] = nc.dram_tensor("d_ln2T", [128, NDT, TOWN], BF16, kind="ExternalOutput").ap()
        dv = s.op("sync", "dma_start", inc="dsy", inck=16, out=dbg["ln2T"], in_=ln2T[:, :, :])
        s.wait("sync", "dsy", dv)
        return nc, s, dbg

    esD = ExitStack()
    pk1 = esD.enter_context(nc.sbuf_tensor("pk1", [128, 2048], I32))
    pk3 = esD.enter_context(nc.sbuf_tensor("pk3", [128, 2048], I32))
    sc1 = [esD.enter_context(nc.sbuf_tensor(f"sc1_{i}", [128, 64], F32)) for i in range(2)]
    sc3 = [esD.enter_context(nc.sbuf_tensor(f"sc3_{i}", [128, 64], F32)) for i in range(2)]
    msb1 = esD.enter_context(nc.sbuf_tensor("msb1", [128, 2048], I32))
    lsb1 = esD.enter_context(nc.sbuf_tensor("lsb1", [128, 2048], I32))
    msb3 = esD.enter_context(nc.sbuf_tensor("msb3", [128, 2048], I32))
    lsb3 = esD.enter_context(nc.sbuf_tensor("lsb3", [128, 2048], I32))
    W1nat = [esD.enter_context(nc.sbuf_tensor(f"W1nat{i}", [128, NDT, 128], BF16)) for i in range(2)]
    W3nat = [esD.enter_context(nc.sbuf_tensor(f"W3nat{i}", [128, NDT, 128], BF16)) for i in range(2)]
    W1T = [esD.enter_context(nc.sbuf_tensor(f"W1T{i}", [128, NDT, 128], BF16)) for i in range(2)]
    W3T = [esD.enter_context(nc.sbuf_tensor(f"W3T{i}", [128, NDT, 128], BF16)) for i in range(2)]
    ast = esD.enter_context(nc.sbuf_tensor("ast", [128, TOWN], BF16))
    bst = esD.enter_context(nc.sbuf_tensor("bst", [128, TOWN], BF16))
    hstage = [esD.enter_context(nc.sbuf_tensor(f"hstage{i}", [128, TOWN], BF16)) for i in range(2)]

    w1v = w1.rearrange("(h s) j -> h (s j)", s=32)
    w3v = w3.rearrange("(h s) j -> h (s j)", s=32)
    s1v = s1.rearrange("(h g) -> h g", g=64)
    s3v = s3.rearrange("(h g) -> h g", g=64)
    ha_r = ha_sp.rearrange("(ht p) t -> p ht t", p=128)

    def emit_dequant(pk, scv, msb, lsb, Wnat, nseg, wait_pk, wait_sc, wait_wnat_free,
                     prev_po=0):
        s.wait("vector", wait_pk, wait_sc)
        s.wait("vector", "ve", prev_po)
        vm = s.op("vector", "tensor_scalar", msb[:, 0:nseg * 64], pk[:, 0:nseg * 64],
                  4, None, ALU.arith_shift_right, inc="ve")
        vl = s.op("vector", "tensor_scalar", lsb[:, 0:nseg * 64], pk[:, 0:nseg * 64],
                  28, 28, ALU.arith_shift_left, ALU.arith_shift_right, inc="ve")
        s.wait("vector", "ve", vl)
        m3 = msb[:, 0:nseg * 64].rearrange("p (s j) -> p s j", j=64)
        l3 = lsb[:, 0:nseg * 64].rearrange("p (s j) -> p s j", j=64)
        sce = scv[:, 0:2 * nseg:2].unsqueeze(2).broadcast_to([128, nseg, 64])
        sco = scv[:, 1:2 * nseg:2].unsqueeze(2).broadcast_to([128, nseg, 64])
        s.wait("vector", "pe", wait_wnat_free)
        po = s.op("vector", "tensor_tensor", Wnat[:, 0:nseg, 0:64], m3, sce, ALU.mult, inc="ve")
        ve = s.op("vector", "tensor_tensor", Wnat[:, 0:nseg, 64:128], l3, sco, ALU.mult, inc="ve")
        return ve, ve, vl

    def emit_wtrans(Wnat, WT, nseg, pairs, ready_ve, ready_po, wt_free):
        s.wait("tensor", "ve", max(ready_ve, ready_po))
        evs = []
        for bi, base in enumerate(range(0, nseg, 16)):
            n = min(16, nseg - base)
            banks2 = pairs[bi % len(pairs)]
            for k in range(2):
                s.wait("tensor", *bank_free[banks2 + k])
            for j in range(n):
                bb = banks2 + j // 8
                s.op("tensor", "matmul", bankb(bb)[:, (j % 8) * 128:(j % 8 + 1) * 128],
                     Wnat[:, base + j, :], ident_b[:, :],
                     is_transpose=True, start=(j % 8 == 0),
                     stop=(j % 8 == 7 or j == n - 1),
                     skip_group_check=True,
                     inc=("pe" if (j % 8 == 7 or j == n - 1) else None))
            tr_v = s.cnt["pe"]
            s.wait("scalar", "pe", tr_v)
            s.wait("scalar", "pe", wt_free)
            nb1 = min(8, n)
            e1 = s.op("scalar", "copy", WT[:, base:base + nb1, :], bankb(banks2)[:, 0:nb1 * 128], inc="ac")
            bank_free[banks2] = ("ac", e1)
            if n > 8:
                e2 = s.op("scalar", "copy", WT[:, base + 8:base + n, :],
                          bankb(banks2 + 1)[:, 0:(n - 8) * 128], inc="ac")
                bank_free[banks2 + 1] = ("ac", e2)
                evs.append(e2)
            evs.append(e1)
        return max(evs), s.cnt["pe"]

    bank_free = {b: ("ac", 0) for b in range(8)}
    lastMM = {0: 0, 1: 0}
    lastwtr = {}
    lastpk = {}
    lasth = {0: 0, 1: 0}
    lastpo_h = 0
    dload = {}
    deq_h = {}
    evac_h = {}

    lastsc = {}

    def d_loads(ht):
        p = ht % 2
        s.wait("sync", "ve", lastpk.get("pk1", 0))
        s.wait("sync", "ve", lastsc.get(("w1", p), 0))
        s.op("sync", "dma_start", inc=f"dd1{p}", inck=16,
             out=pk1[:, :], in_=w1v[ht * 128:(ht + 1) * 128, :])
        ds1 = s.op("sync", "dma_start", inc=f"dd1{p}", inck=16,
                   out=sc1[p][:, :], in_=s1v[ht * 128:(ht + 1) * 128, :])
        s.wait("sync", "ve", lastpk.get("pk3", 0))
        s.wait("sync", "ve", lastsc.get(("w3", p), 0))
        s.op("sync", "dma_start", inc=f"dd3{p}", inck=16,
             out=pk3[:, :], in_=w3v[ht * 128:(ht + 1) * 128, :])
        ds3 = s.op("sync", "dma_start", inc=f"dd3{p}", inck=16,
                   out=sc3[p][:, :], in_=s3v[ht * 128:(ht + 1) * 128, :])
        dload[ht] = (ds1, ds3)

    def d_dequant(ht):
        p = ht % 2
        ve1, po1, vl1 = emit_dequant(pk1, sc1[p], msb1, lsb1, W1nat[p], NDT,
                                     f"dd1{p}", dload[ht][0],
                                     lastwtr.get(("w1", p), 0), 0)
        ve3, po3, vl3 = emit_dequant(pk3, sc3[p], msb3, lsb3, W3nat[p], NDT,
                                     f"dd3{p}", dload[ht][1],
                                     lastwtr.get(("w3", p), 0), 0)
        lastpk["pk1"] = vl1
        lastpk["pk3"] = vl3
        lastsc[("w1", p)] = ve1
        lastsc[("w3", p)] = ve3
        deq_h[ht] = (ve1, po1, ve3, po3)

    def d_stageT(ht):
        p = ht % 2
        ve1, po1, ve3, po3 = deq_h[ht]
        wt1, lastwtr[("w1", p)] = emit_wtrans(W1nat[p], W1T[p], NDT, (0, 2), ve1, po1, lastMM[p])
        wt3, lastwtr[("w3", p)] = emit_wtrans(W3nat[p], W3T[p], NDT, (0, 2), ve3, po3, lastMM[p])
        evac_h[ht] = max(wt1, wt3)

    d_loads(0)
    d_dequant(0)
    d_loads(1)
    d_dequant(1)
    d_stageT(0)
    for ht in range(NHT):
        p = ht % 2
        if ht + 2 < NHT:
            d_loads(ht + 2)
            d_dequant(ht + 2)
        if ht + 1 < NHT:
            d_stageT(ht + 1)
        s.wait("tensor", "ac", evac_h[ht])
        for (WT, bb0) in ((W1T[p], 4), (W3T[p], 6)):
            for ch in range(2):
                s.wait("tensor", *bank_free[bb0 + ch])
                for dt in range(NDT):
                    s.op("tensor", "matmul", bankf(bb0 + ch),
                         WT[:, dt, :], ln2T[:, dt, ch * 512:(ch + 1) * 512],
                         start=(dt == 0), stop=(dt == NDT - 1),
                         inc=("pe" if dt == NDT - 1 else None))
        mm_v = s.cnt["pe"]
        lastMM[p] = mm_v
        s.wait("scalar", "pe", mm_v)
        s.wait("scalar", "po", lastpo_h)
        a1 = s.op("scalar", "activation", ast[:, :], bankf(4, 2), AF.Silu, inc="ac")
        a2 = s.op("scalar", "copy", bst[:, :], bankf(6, 2), inc="ac")
        for bb in (4, 5):
            bank_free[bb] = ("ac", a1)
        for bb in (6, 7):
            bank_free[bb] = ("ac", a2)
        s.wait("gpsimd", "ac", a2)
        s.wait("gpsimd", f"dsh{p}", lasth[p])
        vh_ = s.op("gpsimd", "tensor_tensor", hstage[p][:, :], ast[:, :], bst[:, :], ALU.mult, inc="po")
        lastpo_h = vh_
        s.wait("scalar", "po", vh_)
        sv = s.op("scalar", "dma_start", inc=f"dsh{p}", inck=16,
                  out=ha_sp[ht * 128:(ht + 1) * 128, :], in_=hstage[p][:, :])
        lasth[p] = sv

    s.barrier()
    if debug == 5:
        return nc, s, dbg
    esD.close()
    esLn2.close()

    if debug in (6, 7):
        yA_sp = nc.dram_tensor("d_yA", [TOWN, D], F32, kind="ExternalOutput").ap()
    else:
        yA_sp = nc.dram_tensor("yA_sp", [TOWN, D], F32).ap()
    esE = ExitStack()
    hactH = esE.enter_context(nc.sbuf_tensor("hactH", [128, 43, TOWN], BF16))
    pk2 = esE.enter_context(nc.sbuf_tensor("pk2", [128, 2752], I32))
    sc2 = [esE.enter_context(nc.sbuf_tensor(f"sc2_{i}", [128, 86], F32)) for i in range(2)]
    msb2 = esE.enter_context(nc.sbuf_tensor("msb2", [128, 2752], I32))
    lsb2 = esE.enter_context(nc.sbuf_tensor("lsb2", [128, 2752], I32))
    W2nat = [esE.enter_context(nc.sbuf_tensor(f"W2nat{i}", [128, 43, 128], BF16)) for i in range(2)]
    W2T = [esE.enter_context(nc.sbuf_tensor(f"W2T{i}", [128, 43, 128], BF16)) for i in range(2)]
    ystage = [esE.enter_context(nc.sbuf_tensor(f"ystage{i}", [128, TOWN], F32)) for i in range(2)]
    yAc = [esE.enter_context(nc.sbuf_tensor(f"yAc{i}", [128, NQT, 128], F32)) for i in range(2)]
    o1c = [esE.enter_context(nc.sbuf_tensor(f"o1c{i}", [128, NQT, 128], F32)) for i in range(2)]
    outst = [esE.enter_context(nc.sbuf_tensor(f"outst{i}", [128, NQT, 128], F32)) for i in range(2)]

    w2v = w2.rearrange("(o s) j -> o (s j)", s=86)
    yA_r = yA_sp.rearrange("(j p) c -> p j c", p=128)
    s2v = s2.rearrange("(o g) -> o g", g=172)
    y_own_r = y_own.rearrange("(j p) c -> p j c", p=128)

    for half in range(2):
        h0 = half * 43
        s.barrier()
        hl = 0
        for ck in range(4):
            r0 = 11 * ck
            r1 = min(43, 11 * (ck + 1))
            hl = s.op("sync", "dma_start", inc="dha", inck=16,
                      out=hactH[:, r0:r1, :], in_=ha_r[:, h0 + r0:h0 + r1, :])
        s.wait("tensor", "dha", hl)
        lastpk2 = 0
        lastsc2 = {0: 0, 1: 0}
        lastMM2 = {0: 0, 1: 0}
        lastW2tr = {0: 0, 1: 0}
        lastst = {0: 0, 1: 0}
        lastyA = {0: 0, 1: 0}
        lastout = {0: 0, 1: 0}
        lastsum = {0: 0, 1: 0}
        eload = {}
        edeq = {}
        eevac = {}
        ystv = {}

        def e_loads(ot):
            p = ot % 2
            s.wait("sync", "ve", lastpk2)
            s.wait("sync", "ve", lastsc2[p])
            s.op("sync", "dma_start", inc=f"de{p}", inck=16,
                 out=pk2[:, :], in_=w2v[ot * 128:(ot + 1) * 128, half * 2752:(half + 1) * 2752])
            ds2 = s.op("sync", "dma_start", inc=f"de{p}", inck=16,
                       out=sc2[p][:, :], in_=s2v[ot * 128:(ot + 1) * 128, half * 86:(half + 1) * 86])
            eload[ot] = (ds2, 0)

        def e_dequant(ot):
            p = ot % 2
            ve2, po2, vl2 = emit_dequant(pk2, sc2[p], msb2, lsb2, W2nat[p], 43,
                                         f"de{p}", eload[ot][0], lastW2tr[p], 0)
            nonlocal lastpk2
            lastpk2 = vl2
            lastsc2[p] = ve2
            edeq[ot] = (ve2, po2)

        def e_stageT(ot):
            p = ot % 2
            ve2, po2 = edeq[ot]
            wt2, lastW2tr[p] = emit_wtrans(W2nat[p], W2T[p], 43, (0, 2), ve2, po2, lastMM2[p])
            eevac[ot] = wt2

        def e_loads_act(ot):
            p = ot % 2
            s.wait("sync", "ve", lastyA[p])
            s.op("sync", "dma_start", inc=f"dyo{p}", inck=16,
                 out=yAc[p][:, :, :], in_=yA_r[:, :, ot * 128:(ot + 1) * 128])
            s.wait("sync", "ve", lastsum[p])
            do = s.op("sync", "dma_start", inc=f"dyo{p}", inck=16,
                      out=o1c[p][:, :, :], in_=o1_r[:, :, ot * 128:(ot + 1) * 128])
            eload[ot] = (eload[ot][0], do)

        vs1_h = {}

        def e_finish(ot):
            p = ot % 2
            vs1 = vs1_h.get(ot, 0)
            s.wait("tensor", "ac", ystv[ot])
            for bi in (6, 7):
                s.wait("tensor", *bank_free[bi])
            for j in range(NQT):
                bb = 6 + j // 4
                s.op("tensor", "matmul", bankf(bb)[:, (j % 4) * 128:(j % 4 + 1) * 128],
                     ystage[p][:, j * 128:(j + 1) * 128], ident_f[:, :],
                     start=(j % 4 == 0), stop=(j % 4 == 3),
                     skip_group_check=True, inc=("pe" if j % 4 == 3 else None))
            tr_v = s.cnt["pe"]
            lastst[p] = tr_v
            if half == 0:
                s.wait("scalar", "pe", tr_v)
                s.wait("scalar", f"dya{p}", lastout[p])
                e6 = s.op("scalar", "copy", outst[p][:, 0:4, :], bankf(6), inc="ac")
                e7 = s.op("scalar", "copy", outst[p][:, 4:8, :], bankf(7), inc="ac")
                bank_free[6] = ("ac", e6)
                bank_free[7] = ("ac", e7)
                s.wait("scalar", "ac", e7)
                sv = s.op("scalar", "dma_start", inc=f"dya{p}", inck=16,
                          out=yA_r[:, :, ot * 128:(ot + 1) * 128], in_=outst[p][:, :, :])
                lastout[p] = sv
            else:
                s.wait("vector", "pe", tr_v)
                s.wait("vector", "ve", vs1)
                s.wait("vector", f"dout{p}", lastout[p])
                s.op("vector", "tensor_tensor", outst[p][:, 0:4, :], bankf(6), o1c[p][:, 0:4, :],
                     ALU.add, inc="ve")
                vo = s.op("vector", "tensor_tensor", outst[p][:, 4:8, :], bankf(7), o1c[p][:, 4:8, :],
                          ALU.add, inc="ve")
                lastsum[p] = vo
                for bb in (6, 7):
                    bank_free[bb] = ("ve", vo)
                s.wait("scalar", "ve", vo)
                so = s.op("scalar", "dma_start", inc=f"dout{p}", inck=16,
                          out=y_own_r[:, :, ot * 128:(ot + 1) * 128], in_=outst[p][:, :, :])
                lastout[p] = so

        e_loads(0)
        e_dequant(0)
        e_loads(1)
        e_dequant(1)
        e_stageT(0)
        if half == 1:
            e_loads_act(0)
        for ot in range(NDT):
            p = ot % 2
            if ot + 2 < NDT:
                e_loads(ot + 2)
                e_dequant(ot + 2)
            if ot + 1 < NDT:
                e_stageT(ot + 1)
            if ot >= 1:
                e_finish(ot - 1)
            if half == 1:
                if ot + 1 < NDT:
                    e_loads_act(ot + 1)
                s.wait("vector", f"dyo{p}", eload[ot][1])
                vs1 = s.op("vector", "tensor_tensor", o1c[p][:, :, :], o1c[p][:, :, :],
                           yAc[p][:, :, :], ALU.add, inc="ve")
                lastyA[p] = vs1
                vs1_h[ot] = vs1
            s.wait("tensor", "ac", eevac[ot])
            for ch in range(2):
                s.wait("tensor", *bank_free[4 + ch])
                for j in range(43):
                    s.op("tensor", "matmul", bankf(4 + ch),
                         W2T[p][:, j, :], hactH[:, j, ch * 512:(ch + 1) * 512],
                         start=(j == 0), stop=(j == 42),
                         inc=("pe" if j == 42 else None))
            mm_v = s.cnt["pe"]
            lastMM2[p] = mm_v
            s.wait("scalar", "pe", mm_v)
            s.wait("scalar", "pe", lastst[p])
            ev = s.op("scalar", "copy", ystage[p][:, :], bankf(4, 2), inc="ac")
            for bb in (4, 5):
                bank_free[bb] = ("ac", ev)
            ystv[ot] = ev
        e_finish(NDT - 1)

    s.barrier()
    return nc, s, dbg


def finalize(nc, s):
    with nc.Block() as block:
        @block.sync
        def _(e):
            for fn in s.st["sync"]:
                fn(e)

        @block.tensor
        def _(e):
            for fn in s.st["tensor"]:
                fn(e)

        @block.vector
        def _(e):
            for fn in s.st["vector"]:
                fn(e)

        @block.scalar
        def _(e):
            for fn in s.st["scalar"]:
                fn(e)

        @block.gpsimd
        def _(e):
            for fn in s.st["gpsimd"]:
                fn(e)
    return nc


def make_in_maps(inputs):
    x = np.ascontiguousarray(inputs["L0_attn_x"], dtype=np.float32)
    qkvw = np.ascontiguousarray(inputs["L0_attn_attn_qkvw"], dtype=np.float32)
    qkvb = np.ascontiguousarray(inputs["L0_attn_attn_qkvb"], dtype=np.float32)
    ow = np.ascontiguousarray(inputs["L0_attn_attn_ow"], dtype=np.float32)
    ob = np.ascontiguousarray(inputs["L0_attn_attn_ob"], dtype=np.float32)
    ws = {k: np.ascontiguousarray(inputs[f"L0_ffn_{k}"]) for k in ("w1", "s1", "w2", "s2", "w3", "s3")}
    in_maps = []
    for c in range(8):
        b, half = c // 2, c % 2
        own = x[b, half * TOWN:(half + 1) * TOWN]
        other = x[b, (1 - half) * TOWN:(2 - half) * TOWN]
        xb = np.concatenate([own, other], axis=0)
        m = {"x_b": xb, "qkvw": qkvw, "qkvb": qkvb, "ow": ow, "ob": ob}
        m.update(ws)
        in_maps.append(m)
    return in_maps


def kernel(**inputs):
    in_maps = make_in_maps(inputs)
    nc, s, dbg = build(debug=0)
    finalize(nc, s)
    res = run_bass_kernel_spmd(nc, in_maps, core_ids=list(range(8)))
    out = np.empty((4, S, D), dtype=np.float32)
    for c in range(8):
        b, half = c // 2, c % 2
        out[b, half * TOWN:(half + 1) * TOWN] = res.results[c]["y_own"]
    return out
